# version 09
# baseline (speedup 1.0000x reference)
import contextlib
import numpy as np
import ml_dtypes

import concourse.bass as bass
import concourse.mybir as mybir
from concourse.bass_utils import run_bass_kernel_spmd

F32 = mybir.dt.float32
BF16 = mybir.dt.bfloat16
F16 = mybir.dt.float16
I32 = mybir.dt.int32
U16 = mybir.dt.uint16
AF = mybir.ActivationFunctionType
ALU = mybir.AluOpType

D = 2048
PROJ = 4640
DSSM = 2048
NH = 32
PH = 64
NG = 8
SG = 32
T = 2048
NTILE = 4
TT = 512
NCHUNK = 16
CH = 128
NM = 37
NK = 16
LN_EPS = 1e-5
GN_EPS = 1e-5
RMS_EPS = 1e-6
II = 5632
NKI = 44
DMA = 16

STAGE = "full"


def build_program(stage=STAGE):
    nc = bass.Bass("TRN2", target_bir_lowering=False)

    d_x = nc.dram_tensor("x", [T, D], F32, kind="ExternalInput")
    d_xh = nc.dram_tensor("xh", [3, D], F32, kind="ExternalInput")
    d_mask = nc.dram_tensor("maskp", [1, 1], F32, kind="ExternalInput")
    d_Win = nc.dram_tensor("W_in", [D, PROJ], F32, kind="ExternalInput")
    d_Wout = nc.dram_tensor("W_out", [DSSM, D], F32, kind="ExternalInput")
    d_binr = nc.dram_tensor("b_in_t", [128, NM], F32, kind="ExternalInput")
    d_mup = nc.dram_tensor("mup_t", [128, NM], F32, kind="ExternalInput")
    d_convw = nc.dram_tensor("conv_w_t", [128, 80], F32, kind="ExternalInput")
    d_convb = nc.dram_tensor("conv_b_t", [128, 20], F32, kind="ExternalInput")
    d_Alog = nc.dram_tensor("A_log_c", [NH, 1], F32, kind="ExternalInput")
    d_dtb = nc.dram_tensor("dt_bias_c", [NH, 1], F32, kind="ExternalInput")
    d_Drow = nc.dram_tensor("D_row", [1, NH], F32, kind="ExternalInput")
    d_nwrow = nc.dram_tensor("normw_row", [1, DSSM], F32, kind="ExternalInput")
    d_borow = nc.dram_tensor("bout_row", [1, D], F32, kind="ExternalInput")
    d_guu = nc.dram_tensor("gu_u16", [2 * II, 512], U16, kind="ExternalInput")
    d_gus = nc.dram_tensor("gu_scales", [16, 2 * II], F16, kind="ExternalInput")
    d_dnu = nc.dram_tensor("dn_u16", [D, 1408], U16, kind="ExternalInput")
    d_dns = nc.dram_tensor("dn_scales", [NKI, D], F16, kind="ExternalInput")
    d_idb = nc.dram_tensor("ident_b", [128, 128], BF16, kind="ExternalInput")
    d_idf = nc.dram_tensor("ident_f", [128, 128], F32, kind="ExternalInput")
    d_onehot = nc.dram_tensor("onehot", [NH, NH * 128], F32, kind="ExternalInput")
    d_tri = nc.dram_tensor("tri", [128, 128], F32, kind="ExternalInput")
    d_sc = nc.dram_tensor("sconst", [1, 8], F32, kind="ExternalInput")

    d_out = nc.dram_tensor("out", [T, D], F32, kind="ExternalOutput")

    z_scr = nc.dram_tensor("z_scr", [16, 128, T], F32)
    Y_scr = nc.dram_tensor("Y_scr", [NCHUNK, 128, DSSM], F32)
    xmid_scr = nc.dram_tensor("xmid_scr", [T, D], F32)
    x2n_scr = nc.dram_tensor("x2n_scr", [T, D], BF16)
    C0_scr = nc.dram_tensor("C0_scr", [NCHUNK, SG, NG * 128], BF16)
    LAM_scr = nc.dram_tensor("LAM_scr", [NCHUNK, 128, NH], F32)
    h_scr = nc.dram_tensor("h_scr", [NKI, 128, T], BF16)
    gusq_scr = nc.dram_tensor("gusq_scr", [16, 2 * II], F16)
    dnsq_scr = nc.dram_tensor("dnsq_scr", [NKI, D], F16)
    cc_in = nc.dram_tensor("cc_in", [SG, NH * PH], F32)
    cc_out = nc.dram_tensor("cc_out", [2 * SG, NH * PH], F32)

    plan = {e: [] for e in ["sync", "gpsimd", "vector", "scalar", "tensor"]}
    cnt = {}

    def inc(s, n=1):
        cnt[s] = cnt.get(s, 0) + n
        return cnt[s]

    def cur(s):
        return cnt.get(s, 0)

    esg = contextlib.ExitStack()
    with esg:
        def sbt(name, shape, dt_, scope=None):
            return (scope or esg).enter_context(nc.sbuf_tensor(name, shape, dt_))

        def pst(name, shape, dt_):
            return esg.enter_context(nc.psum_tensor(name, shape, dt_))

        SEMN = ["sAUX", "sXIN", "sWIN", "sZRW", "sYRW", "sMSC", "sCC", "sGUI",
                "sDNI", "sHIO", "sOUT", "sPE", "sV", "sS", "sG", "sSY",
                "sAUXG", "sGUS", "sDNS"]
        SEM = {n: esg.enter_context(nc.semaphore(n)) for n in SEMN}
        (sAUX, sXIN, sWIN, sZRW, sYRW, sMSC, sCC, sGUI, sDNI, sHIO, sOUT,
         sPE, sV, sS, sG, sSY, sAUXG, sGUS, sDNS) = (SEM[n] for n in SEMN)
        DMASEMS = [sAUX, sXIN, sWIN, sZRW, sYRW, sMSC, sCC, sGUI, sDNI,
                   sHIO, sOUT, sAUXG, sGUS, sDNS]

        def Y_(fn):
            plan["sync"].append(fn)

        def G_(fn):
            plan["gpsimd"].append(fn)

        def V_(fn):
            plan["vector"].append(fn)

        def S_(fn):
            plan["scalar"].append(fn)

        def T_(fn):
            plan["tensor"].append(fn)

        def Vd():
            V_(lambda e: e.drain())

        def Sd():
            S_(lambda e: e.drain())

        def Gd():
            G_(lambda e: e.drain())

        def Vw(sm, val):
            if val > 0:
                V_(lambda e, sm=sm, val=val: e.wait_ge(sm, val))

        def Sw(sm, val):
            if val > 0:
                S_(lambda e, sm=sm, val=val: e.wait_ge(sm, val))

        def Tw(sm, val):
            if val > 0:
                T_(lambda e, sm=sm, val=val: e.wait_ge(sm, val))

        def Gw(sm, val):
            if val > 0:
                G_(lambda e, sm=sm, val=val: e.wait_ge(sm, val))

        def Yw(sm, val):
            if val > 0:
                Y_(lambda e, sm=sm, val=val: e.wait_ge(sm, val))

        def Vi(fn):
            n = inc("sV")
            V_(lambda e, fn=fn: fn(e).then_inc(sV, 1))
            return n

        def Si(fn):
            n = inc("sS")
            S_(lambda e, fn=fn: fn(e).then_inc(sS, 1))
            return n

        def Ti(fn):
            n = inc("sPE")
            T_(lambda e, fn=fn: fn(e).then_inc(sPE, 1))
            return n

        def Gi(fn):
            n = inc("sG")
            G_(lambda e, fn=fn: fn(e).then_inc(sG, 1))
            return n

        def Yi(fn):
            n = inc("sSY")
            Y_(lambda e, fn=fn: fn(e).then_inc(sSY, 1))
            return n

        def YD(sem, fn):
            Yw(sem, curs(sem))
            Y_(lambda e, fn=fn, sem=sem: fn(e).then_inc(sem, DMA))
            return incs(sem, DMA)

        def GD(sem, fn):
            Gw(sem, curs(sem))
            G_(lambda e, fn=fn, sem=sem: fn(e).then_inc(sem, DMA))
            return incs(sem, DMA)

        _semname = {id(SEM[n]): n for n in SEMN}

        def incs(sm, n=1):
            return inc(_semname[id(sm)], n)

        def curs(sm):
            return cur(_semname[id(sm)])

        def barrier():
            for sm in DMASEMS:
                Yw(sm, curs(sm))
            y_m = Yi(lambda e: e.nop())
            g_m = Gi(lambda e: e.drain())
            v_m = Vi(lambda e: e.drain())
            s_m = Si(lambda e: e.drain())
            t_m = Ti(lambda e: e.drain())
            for w, marks in [(Yw, [(sG, g_m), (sV, v_m), (sS, s_m), (sPE, t_m)]),
                             (Gw, [(sSY, y_m), (sV, v_m), (sS, s_m), (sPE, t_m)]),
                             (Vw, [(sSY, y_m), (sG, g_m), (sS, s_m), (sPE, t_m)]),
                             (Sw, [(sSY, y_m), (sG, g_m), (sV, v_m), (sPE, t_m)]),
                             (Tw, [(sSY, y_m), (sG, g_m), (sV, v_m), (sS, s_m)])]:
                for sm, val in marks:
                    w(sm, val)

        c_idb = sbt("c_idb", [128, 128], BF16)
        c_idf = sbt("c_idf", [128, 128], F32)
        c_tri = sbt("c_tri", [128, 128], F32)
        c_mask128 = sbt("c_mask128", [128, 1], F32)
        c_mask32 = sbt("c_mask32", [SG, 1], F32)
        c_Alog = sbt("c_Alog", [NH, 1], F32)
        c_nA = sbt("c_nA", [NH, 1], F32)
        c_dtb = sbt("c_dtb", [NH, 1], F32)
        c_dparam = sbt("c_dparam", [128, NH], F32)
        c_sc = sbt("c_sc", [128, 8], F32)
        bn6 = sbt("bn6", [128, 24], F32)
        bn2 = sbt("bn2", [128, 2], F32)
        rstd = sbt("rstd", [128, 1], F32)
        nmr = sbt("nmr", [128, 1], F32)
        sqacc = sbt("sqacc", [128, 4], F32)
        sqscr = sbt("sqscr", [128, 512], F32)
        Sf = sbt("Sf", [SG, NH * PH], F32)
        Sinit = sbt("Sinit", [SG, NH * PH], F32)
        Sinitb = sbt("Sinitb", [SG, NH * PH], BF16)
        Ybuf = sbt("Ybuf", [128, DSSM], F32)
        xmt = sbt("xmt", [128, D], F32)

        pA = pst("pA", [128, 512], F32)
        pB = pst("pB", [128, 512], F32)
        pTb = pst("pTb", [128, 1024], BF16)
        pTf = pst("pTf", [128, 512], F32)
        pCB = pst("pCB", [128, 512], F32)
        pE = pst("pE", [128, 512], F32)
        pY = pst("pY", [128, 512], F32)
        pYi = pst("pYi", [128, 512], F32)

        for dst, src in [(c_idb, d_idb), (c_idf, d_idf), (c_tri, d_tri),
                         (c_Alog, d_Alog), (c_dtb, d_dtb)]:
            Yw(sAUX, curs(sAUX))
            Y_(lambda e, dst=dst, src=src: e.dma_start(
                dst[:, :], src[:, :]).then_inc(sAUX, DMA))
            incs(sAUX, DMA)
        for dst, src, np_ in [(c_dparam, d_Drow, 128), (c_mask128, d_mask, 128),
                              (c_mask32, d_mask, SG), (c_sc, d_sc, 128)]:
            Gw(sAUXG, curs(sAUXG))
            G_(lambda e, dst=dst, src=src, np_=np_: e.dma_start(
                dst[:, :], src.ap().partition_broadcast(np_)).then_inc(sAUXG, DMA))
            incs(sAUXG, DMA)
        AUX0 = curs(sAUX)
        AUXG0 = curs(sAUXG)
        Vw(sAUX, AUX0)
        Vw(sAUXG, AUXG0)
        V_(lambda e: e.memset(Sf[:, :], 0.0))
        Sw(sAUX, AUX0)
        Sw(sAUXG, AUXG0)
        S_(lambda e: e.activation(c_nA[:, :], c_Alog[:, :], AF.Exp))
        s_na = Si(lambda e: e.drain())
        Vw(sS, s_na)
        V_(lambda e: e.tensor_scalar(c_nA[:, :], c_nA[:, :], -1.0, None,
                                     ALU.mult))

        with contextlib.ExitStack() as es0:
            scsb = sbt("scsb", [NKI, D], F16, es0)
            scsb2 = sbt("scsb2", [16, 2 * II], F16, es0)
            Gw(sAUXG, curs(sAUXG))
            G_(lambda e: e.dma_start(scsb2[:, :], d_gus[:, :])
               .then_inc(sAUXG, DMA))
            incs(sAUXG, DMA)
            AUX1 = curs(sAUXG)
            Vw(sAUXG, AUX1)
            V_(lambda e: e.tensor_scalar(scsb2[:, :], scsb2[:, :], 0.25, None,
                                         ALU.mult))
            v_gu = Vi(lambda e: e.drain())
            Yw(sV, v_gu)
            Yw(sMSC, curs(sMSC))
            Y_(lambda e: e.dma_start(gusq_scr[:, :], scsb2[:, :])
               .then_inc(sMSC, DMA))
            incs(sMSC, DMA)
            Gw(sAUXG, curs(sAUXG))
            G_(lambda e: e.dma_start(scsb[:, :], d_dns[:, :]).then_inc(sAUXG, DMA))
            incs(sAUXG, DMA)
            AUX2 = curs(sAUXG)
            Vw(sAUXG, AUX2)
            V_(lambda e: e.tensor_scalar(scsb[:, :], scsb[:, :], 0.25, None,
                                         ALU.mult))
            v_dn = Vi(lambda e: e.drain())
            Yw(sV, v_dn)
            Yw(sMSC, curs(sMSC))
            Y_(lambda e: e.dma_start(dnsq_scr[:, :], scsb[:, :])
               .then_inc(sMSC, DMA))
            incs(sMSC, DMA)
            barrier()

        def emit_ln(src, dst, epscol, rows=128):
            for q in range(4):
                V_(lambda e, q=q, src=src, rows=rows: e.bn_stats(
                    bn6[0:rows, 6 * q:6 * (q + 1)],
                    src[0:rows, 512 * q:512 * (q + 1)]))
            V_(lambda e: e.drain())
            vm = Vi(lambda e, rows=rows: e.bn_aggr(bn2[0:rows, :],
                                                   bn6[0:rows, :]))
            Sw(sV, vm)
            S_(lambda e, epscol=epscol, rows=rows: e.activation(
                rstd[0:rows, :], bn2[0:rows, 1:2], AF.Ln,
                bias=c_sc[0:rows, epscol:epscol + 1]))
            S_(lambda e: e.drain())
            sm = Si(lambda e, rows=rows: e.activation(
                rstd[0:rows, :], rstd[0:rows, :], AF.Exp,
                scale=c_sc[0:rows, 4:5]))
            Vw(sS, sm)
            vm2 = Vi(lambda e, rows=rows: e.scalar_tensor_tensor(
                nmr[0:rows, :], bn2[0:rows, 0:1], -1.0, rstd[0:rows, :],
                ALU.mult, ALU.mult))
            V_(lambda e: e.drain())
            Sw(sV, vm2)
            return Si(lambda e, src=src, dst=dst, rows=rows: e.activation(
                dst[0:rows, :], src[0:rows, :], AF.Identity,
                bias=nmr[0:rows, :], scale=rstd[0:rows, :]))

        with contextlib.ExitStack() as es2:
            c_oneh = sbt("c_oneh", [NH, NH * 128], F32, es2)
            c_binr = sbt("c_binr", [128, NM], F32, es2)
            c_mup = sbt("c_mup", [128, NM], F32, es2)
            c_bmup = sbt("c_bmup", [128, NM], F32, es2)
            c_convw = sbt("c_convw", [128, 80], F32, es2)
            c_convb = sbt("c_convb", [128, 20], F32, es2)
            xsub = sbt("xsub", [128, D], F32, es2)
            xn = sbt("xn", [128, D], BF16, es2)
            xnT = sbt("xnT", [128, NK * TT], BF16, es2)
            wbuf = [sbt(f"wbuf{i}", [128, NK * 128], BF16, es2) for i in range(2)]
            zev = [sbt(f"zev{i}", [128, TT], F32, es2) for i in range(2)]
            xin = sbt("xin", [128, 20 * (TT + 3)], F32, es2)
            xbcb = sbt("xbcb", [128, 20 * TT], BF16, es2)
            convacc = [sbt(f"convacc{i}", [128, TT], F32, es2) for i in range(2)]
            dtf = sbt("dtf", [NH, TT], F32, es2)
            dtp = sbt("dtp", [NH, TT], F32, es2)
            sp1 = sbt("sp1", [NH, TT], F32, es2)
            sp2 = sbt("sp2", [NH, TT], F32, es2)
            acum = sbt("acum", [NH, TT], F32, es2)
            zeros32 = sbt("zeros32", [NH, CH], F32, es2)
            ufm = sbt("ufm", [NH, CH], F32, es2)
            lamfm = sbt("lamfm", [NH, CH], F32, es2)
            LAMfm = sbt("LAMfm", [NH, CH], F32, es2)
            gam = sbt("gam", [NH, 1], F32, es2)
            logR = sbt("logR", [NH, 1], F32, es2)
            grow = sbt("grow", [1, NH], F32, es2)
            gbc = sbt("gbc", [SG, NH], F32, es2)
            acumT = sbt("acumT", [128, NH], F32, es2)
            nacumT = sbt("nacumT", [128, NH], F32, es2)
            dtpT = sbt("dtpT", [128, NH], F32, es2)
            uT = sbt("uT", [128, NH], F32, es2)
            lamT = sbt("lamT", [128, NH], F32, es2)
            LAMT = sbt("LAMT", [128, NH], F32, es2)
            dpT = sbt("dpT", [128, NH], F32, es2)
            xtt = sbt("xtt", [128, DSSM], BF16, es2)
            BT = sbt("BT", [128, 512], BF16, es2)
            B0 = sbt("B0", [SG, NG * 128], BF16, es2)
            C0 = sbt("C0", [SG, NG * 128], BF16, es2)
            Bhat = sbt("Bhat", [128, SG], BF16, es2)
            cbm = sbt("cbm", [128, NG * 128], F32, es2)

            Sb = sbt("Sb", [SG, NH * PH], BF16, es2)
            es2h = contextlib.ExitStack()
            xhs = sbt("xhs", [3, D], F32, es2h)
            xhn = sbt("xhn", [3, D], BF16, es2h)
            xhT = sbt("xhT", [128, NK * 3], BF16, es2h)

            for dst, src in [(c_oneh, d_onehot), (c_binr, d_binr),
                             (c_mup, d_mup), (c_convw, d_convw),
                             (c_convb, d_convb)]:
                Yw(sAUX, curs(sAUX))
                Y_(lambda e, dst=dst, src=src: e.dma_start(
                    dst[:, :], src[:, :]).then_inc(sAUX, DMA))
                incs(sAUX, DMA)
            AUXC = curs(sAUX)
            Vw(sAUX, AUXC)
            V_(lambda e: e.tensor_tensor(c_bmup[:, :], c_binr[:, :],
                                         c_mup[:, :], ALU.mult))
            V_(lambda e: e.memset(zeros32[:, :], 0.0))
            V_(lambda e: e.memset(logR[:, :], 0.0))
            v_cst2 = Vi(lambda e: e.drain())
            Sw(sV, v_cst2)
            Tw(sV, v_cst2)
            Gw(sV, v_cst2)

            wbuf_gate = [0, 0]

            def stream_win(m, bi):
                if wbuf_gate[bi]:
                    Gw(sPE, wbuf_gate[bi])
                hi = 128 * (m + 1) if m < 36 else PROJ
                ocnt = 128 if m < 36 else 32
                src = d_Win.ap().rearrange("(c p) o -> p c o", p=128)[
                    :, :, 128 * m:hi]
                Gw(sWIN, curs(sWIN))
                G_(lambda e, src=src, bi=bi, ocnt=ocnt: e.dma_start(
                    wbuf[bi][:, :].rearrange("p (c o) -> p c o", c=NK)
                    [:, :, 0:ocnt], src).then_inc(sWIN, DMA))
                return incs(sWIN, DMA)

            Gw(sXIN, curs(sXIN))
            G_(lambda e: e.dma_start(xhs[:, :], d_xh[:, :]).then_inc(sXIN, DMA))
            incs(sXIN, DMA)
            XH0 = curs(sXIN)
            Vw(sXIN, XH0)
            Sw(sXIN, XH0)
            s_hn = emit_ln(xhs, xhn, 0, rows=3)
            Tw(sS, s_hn)
            for k in range(NK):
                pe = Ti(lambda e, k=k: e.transpose(
                    pTb[0:128, 0:3], xhn[:, 128 * k:128 * (k + 1)],
                    c_idb[0:3, 0:3]))
                Vw(sPE, pe)
                vv = Vi(lambda e, k=k: e.tensor_copy(
                    xhT[:, 3 * k:3 * (k + 1)], pTb[0:128, 0:3]))
                Tw(sV, vv)
            for mi, m in enumerate(range(16, 36)):
                bi = mi % 2
                wv = stream_win(m, bi)
                Tw(sWIN, wv)
                for k in range(NK):
                    T_(lambda e, k=k, bi=bi: e.matmul(
                        pA[0:128, 0:3], wbuf[bi][:, 128 * k:128 * (k + 1)],
                        xhT[:, 3 * k:3 * (k + 1)],
                        start=(k == 0), stop=(k == NK - 1)))
                pe = Ti(lambda e: e.drain())
                wbuf_gate[bi] = pe
                Sw(sPE, pe)
                S_(lambda e, m=m, mi=mi: e.activation(
                    xin[:, 515 * mi:515 * mi + 3], pA[0:128, 0:3], AF.Identity,
                    bias=c_bmup[:, m:m + 1], scale=c_mup[:, m:m + 1]))
                sm = Si(lambda e: e.drain())
                Tw(sS, sm)
                Vw(sS, sm)
                V_(lambda e, mi=mi: e.tensor_scalar(
                    xin[:, 515 * mi:515 * mi + 3],
                    xin[:, 515 * mi:515 * mi + 3],
                    c_mask128[:, :], None, ALU.mult))
            v_halo = Vi(lambda e: e.drain())
            Sw(sV, v_halo)
            es2h.close()
            esb = [sbt(f"esb{i}", [128, 128], F32, es2) for i in range(2)]
            mpb = [sbt(f"mpb{i}", [128, 128], BF16, es2) for i in range(2)]
            Ybuf2 = sbt("Ybuf2", [128, DSSM], F32, es2)
            Bhat2 = [sbt(f"Bhat2{i}", [128, SG], BF16, es2) for i in range(2)]

            xsub_gate = [0]
            zev_dma = [0, 0]

            def emit_inproj_tile(t):
                pe0 = cur("sPE")
                Vw(sPE, pe0)
                Sw(sPE, pe0)
                for gi in range(4):
                    row0 = TT * t + 128 * gi
                    if xsub_gate[0]:
                        Gw(sS, xsub_gate[0])
                    Gw(sXIN, curs(sXIN))
                    G_(lambda e, row0=row0: e.dma_start(
                        xsub[:, :], d_x[row0:row0 + 128, :]).then_inc(sXIN, DMA))
                    xv = incs(sXIN, DMA)
                    Vw(sXIN, xv)
                    Sw(sXIN, xv)
                    s_n = emit_ln(xsub, xn, 0)
                    xsub_gate[0] = s_n
                    Tw(sS, s_n)
                    for k in range(NK):
                        pe = Ti(lambda e, k=k: e.transpose(
                            pTb[0:128, 0:128], xn[:, 128 * k:128 * (k + 1)],
                            c_idb[:, :]))
                        Vw(sPE, pe)
                        vv = Vi(lambda e, k=k, gi=gi: e.tensor_copy(
                            xnT[:, TT * k + 128 * gi:TT * k + 128 * (gi + 1)],
                            pTb[0:128, 0:128]))
                        Tw(sV, vv)
                pa_gate = [0, 0]
                for m in range(NM):
                    bi = m % 2
                    wv = stream_win(m, bi)
                    bank = pA if bi == 0 else pB
                    Tw(sWIN, wv)
                    if pa_gate[bi]:
                        Tw(sS, pa_gate[bi])
                    rows = 128 if m < 36 else 32
                    for k in range(NK):
                        T_(lambda e, k=k, bi=bi, bank=bank, rows=rows: e.matmul(
                            bank[0:rows, 0:TT],
                            wbuf[bi][:, 128 * k:128 * k + rows],
                            xnT[:, TT * k:TT * (k + 1)],
                            start=(k == 0), stop=(k == NK - 1)))
                    pe = Ti(lambda e: e.drain())
                    wbuf_gate[bi] = pe
                    Sw(sPE, pe)
                    if m < 16:
                        zi = m % 2
                        if zev_dma[zi]:
                            Sw(sZRW, zev_dma[zi])
                        S_(lambda e, m=m, bank=bank, zi=zi: e.activation(
                            zev[zi][:, :], bank[0:128, 0:TT], AF.Identity,
                            bias=c_bmup[:, m:m + 1], scale=c_mup[:, m:m + 1]))
                        sm = Si(lambda e: e.drain())
                        pa_gate[bi] = sm
                        Yw(sS, sm)
                        Yw(sZRW, curs(sZRW))
                        Y_(lambda e, m=m, t=t, zi=zi: e.dma_start(
                            z_scr[m, :, TT * t:TT * (t + 1)], zev[zi][:, :])
                            .then_inc(sZRW, DMA))
                        zev_dma[zi] = incs(sZRW, DMA)
                    elif m < 36:
                        mi = m - 16
                        S_(lambda e, mi=mi, m=m, bank=bank: e.activation(
                            xin[:, 515 * mi + 3:515 * mi + 515],
                            bank[0:128, 0:TT], AF.Identity,
                            bias=c_bmup[:, m:m + 1], scale=c_mup[:, m:m + 1]))
                        pa_gate[bi] = Si(lambda e: e.drain())
                    else:
                        S_(lambda e, bank=bank: e.activation(
                            dtf[:, :], bank[0:SG, 0:TT], AF.Identity,
                            bias=c_bmup[0:SG, 36:37], scale=c_mup[0:SG, 36:37]))
                        pa_gate[bi] = Si(lambda e: e.drain())
                return Si(lambda e: e.drain())

            def emit_conv_tile(t, s_ip):
                Vw(sS, s_ip)
                Sw(sS, s_ip)
                silu_done = [0, 0]
                for mi in range(20):
                    base = 515 * mi
                    ci = mi % 2
                    acc = convacc[ci]
                    if silu_done[ci]:
                        Vw(sS, silu_done[ci])
                    V_(lambda e, base=base, mi=mi, acc=acc: e.tensor_scalar(
                        acc[:, :], xin[:, base:base + TT],
                        c_convw[:, 4 * mi:4 * mi + 1], None, ALU.mult))
                    for kk in range(1, 4):
                        V_(lambda e, base=base, mi=mi, kk=kk, acc=acc:
                           e.scalar_tensor_tensor(
                               acc[:, :], xin[:, base + kk:base + kk + TT],
                               c_convw[:, 4 * mi + kk:4 * mi + kk + 1],
                               acc[:, :], ALU.mult, ALU.add))
                    vv = Vi(lambda e: e.drain())
                    Sw(sV, vv)
                    silu_done[ci] = Si(lambda e, mi=mi, acc=acc: e.activation(
                        xbcb[:, TT * mi:TT * (mi + 1)], acc[:, :], AF.Silu,
                        bias=c_convb[:, mi:mi + 1]))
                s_conv = Si(lambda e: e.drain())
                v_tail = 0
                if t < NTILE - 1:
                    Vw(sS, s_conv)
                    for mi in range(20):
                        V_(lambda e, mi=mi: e.tensor_copy(
                            xin[:, 515 * mi:515 * mi + 3],
                            xin[:, 515 * mi + 512:515 * mi + 515]))
                    v_tail = Vi(lambda e: e.drain())
                return s_conv, v_tail

            def emit_dt_tile(t, s_ip):
                Vw(sS, s_ip)
                V_(lambda e: e.tensor_scalar(sp1[:, :], dtf[:, :],
                                             c_dtb[0:NH, :], None, ALU.add))
                vm = Vi(lambda e: e.drain())
                Sw(sV, vm)
                S_(lambda e: e.activation(sp2[:, :], sp1[:, :], AF.Abs))
                S_(lambda e: e.drain())
                S_(lambda e: e.activation(sp2[:, :], sp2[:, :], AF.Exp,
                                          scale=c_sc[0:NH, 5:6]))
                S_(lambda e: e.drain())
                sm = Si(lambda e: e.activation(sp2[:, :], sp2[:, :], AF.Ln,
                                               bias=c_sc[0:NH, 3:4]))
                Vw(sS, sm)
                V_(lambda e: e.tensor_scalar(sp1[:, :], sp1[:, :], 0.0, None,
                                             ALU.max))
                V_(lambda e: e.tensor_tensor(dtp[:, :], sp1[:, :], sp2[:, :],
                                             ALU.add))
                V_(lambda e: e.tensor_scalar(sp1[:, :], dtp[:, :],
                                             c_nA[0:NH, :], None, ALU.mult))
                for ct in range(4):
                    V_(lambda e, ct=ct: e.tensor_tensor_scan(
                        acum[:, CH * ct:CH * (ct + 1)],
                        sp1[:, CH * ct:CH * (ct + 1)],
                        zeros32[:, :], 0.0, ALU.add, ALU.add))
                return Vi(lambda e: e.drain())

            ssd_prev = {"v_logr": 0, "v_store": 0, "pe_end": 0}

            def emit_ssd_chunk(t, ct, ccg, v_dt):
                colc = CH * ct
                Sw(sV, v_dt)
                if ssd_prev["v_logr"]:
                    Sw(sV, ssd_prev["v_logr"])
                S_(lambda e, colc=colc: e.activation(
                    gam[:, :], acum[:, colc + CH - 1:colc + CH], AF.Exp))
                S_(lambda e, colc=colc: e.activation(
                    ufm[:, :], acum[:, colc:colc + CH], AF.Exp,
                    scale=c_sc[0:NH, 5:6],
                    bias=acum[:, colc + CH - 1:colc + CH]))
                S_(lambda e, colc=colc: e.activation(
                    lamfm[:, :], acum[:, colc:colc + CH], AF.Exp))
                S_(lambda e, colc=colc: e.activation(
                    LAMfm[:, :], acum[:, colc:colc + CH], AF.Exp,
                    bias=logR[0:NH, :]))
                s_vecs = Si(lambda e: e.drain())
                Vw(sS, s_vecs)
                ssd_prev["v_logr"] = Vi(lambda e, colc=colc: e.tensor_tensor(
                    logR[:, :], logR[:, :],
                    acum[:, colc + CH - 1:colc + CH], ALU.add))
                V_(lambda e: e.drain())
                Tw(sS, s_vecs)
                for srcap, dstT in [
                        (acum[:, colc:colc + CH], acumT),
                        (dtp[:, colc:colc + CH], dtpT),
                        (ufm[:, :], uT), (lamfm[:, :], lamT),
                        (LAMfm[:, :], LAMT)]:
                    pe = Ti(lambda e, srcap=srcap: e.transpose(
                        pTf[0:128, 0:NH], srcap, c_idf[0:NH, 0:NH]))
                    Vw(sPE, pe)
                    vv = Vi(lambda e, dstT=dstT: e.tensor_copy(
                        dstT[:, :], pTf[0:128, 0:NH]))
                    Tw(sV, vv)
                pe = Ti(lambda e: e.transpose(
                    pTf[0:1, 0:NH], gam[:, :], c_idf[0:NH, 0:NH]))
                Vw(sPE, pe)
                vv = Vi(lambda e: e.tensor_copy(grow[:, :], pTf[0:1, 0:NH]))
                Tw(sV, vv)
                T_(lambda e: e.matmul(pTf[0:SG, 256:256 + NH],
                                      c_tri[0:1, 0:SG], grow[:, :],
                                      start=True, stop=True))
                pe2 = Ti(lambda e: e.drain())
                Vw(sPE, pe2)
                vv = Vi(lambda e: e.tensor_copy(gbc[:, :],
                                                pTf[0:SG, 256:256 + NH]))
                Tw(sV, vv)
                V_(lambda e: e.tensor_scalar(nacumT[:, :], acumT[:, :], -1.0,
                                             None, ALU.mult))
                V_(lambda e: e.reciprocal(dpT[:, :], dtpT[:, :]))
                V_(lambda e: e.drain())
                V_(lambda e: e.tensor_tensor(dpT[:, :], dpT[:, :],
                                             c_dparam[:, :], ALU.mult))
                v_tm = Vi(lambda e: e.drain())
                Tw(sV, v_tm)
                for k in range(NK):
                    pe = Ti(lambda e, k=k, ct=ct: e.transpose(
                        pTb[0:128, 0:128],
                        xbcb[:, TT * k + CH * ct:TT * k + CH * (ct + 1)],
                        c_idb[:, :]))
                    Vw(sPE, pe)
                    for hh in range(2):
                        h = 2 * k + hh
                        vv = Vi(lambda e, h=h, hh=hh: e.tensor_scalar(
                            xtt[:, PH * h:PH * (h + 1)],
                            pTb[0:128, PH * hh:PH * (hh + 1)],
                            dtpT[:, h:h + 1], None, ALU.mult))
                    Tw(sV, vv)
                for bi_, k in enumerate((16, 17, 18, 19)):
                    pe = Ti(lambda e, k=k, ct=ct: e.transpose(
                        pTb[0:128, 0:128],
                        xbcb[:, TT * k + CH * ct:TT * k + CH * (ct + 1)],
                        c_idb[:, :]))
                    Vw(sPE, pe)
                    vv = Vi(lambda e, bi_=bi_: e.tensor_copy(
                        BT[:, 128 * bi_:128 * (bi_ + 1)], pTb[0:128, 0:128]))
                    Tw(sV, vv)
                for g in range(NG):
                    pe = Ti(lambda e, g=g: e.transpose(
                        pTb[0:SG, 0:128], BT[:, SG * g:SG * (g + 1)],
                        c_idb[:, :]))
                    Vw(sPE, pe)
                    vv = Vi(lambda e, g=g: e.tensor_copy(
                        B0[:, 128 * g:128 * (g + 1)], pTb[0:SG, 0:128]))
                    Tw(sV, vv)
                for g in range(NG):
                    pe = Ti(lambda e, g=g: e.transpose(
                        pTb[0:SG, 0:128], BT[:, 256 + SG * g:256 + SG * (g + 1)],
                        c_idb[:, :]))
                    Vw(sPE, pe)
                    vv = Vi(lambda e, g=g: e.tensor_copy(
                        C0[:, 128 * g:128 * (g + 1)], pTb[0:SG, 0:128]))
                    Tw(sV, vv)
                V_(lambda e: e.tensor_copy(Sb[:, :], Sf[:, :]))
                v_sb = Vi(lambda e: e.drain())
                Tw(sV, v_sb)
                cb_gate = [0, 0]
                cb_banks = [pCB, pY]
                for g in range(NG):
                    half = g % 2
                    bank = cb_banks[half]
                    if cb_gate[half]:
                        Tw(sV, cb_gate[half])
                    pe = Ti(lambda e, g=g, bank=bank: e.matmul(
                        bank[0:128, 0:128],
                        B0[:, 128 * g:128 * (g + 1)],
                        C0[:, 128 * g:128 * (g + 1)], start=True, stop=True))
                    Vw(sPE, pe)
                    vv = Vi(lambda e, g=g, bank=bank: e.tensor_tensor(
                        cbm[:, 128 * g:128 * (g + 1)],
                        bank[0:128, 0:128],
                        c_tri[:, :], ALU.mult))
                    cb_gate[half] = vv
                Ebank = [pE, pCB]
                DSbank = [pA, pB]
                pe_e = {}
                pe_y = {}
                pe_yi = {}
                pe_ds = {}
                vclip = {}
                sexp = {}
                vm_m = {}
                vb_m = {}
                y1_m = {}
                y2_m = {}
                st_m = {}
                Tw(sV, cur("sV"))

                def emit_pe_head(h):
                    g = h // 4
                    if h - 1 >= 0:
                        Tw(sV, y1_m[h - 1])
                        Tw(sV, y2_m[h - 1])
                    if h - 2 >= 0:
                        Tw(sV, st_m[h - 2])
                    Tw(sV, vm_m[h])
                    Tw(sV, vb_m[h])
                    pe_y[h] = Ti(lambda e, h=h: e.matmul(
                        pY[0:128, 0:PH], mpb[h % 2][:, :],
                        xtt[:, PH * h:PH * (h + 1)], start=True, stop=True))
                    pe_yi[h] = Ti(lambda e, h=h, g=g: e.matmul(
                        pYi[0:128, 0:PH], C0[:, 128 * g:128 * (g + 1)],
                        Sb[:, PH * h:PH * (h + 1)], start=True, stop=True))
                    pe_ds[h] = Ti(lambda e, h=h: e.matmul(
                        DSbank[h % 2][0:SG, 0:PH], Bhat2[h % 2][:, :],
                        xtt[:, PH * h:PH * (h + 1)], start=True, stop=True))

                def emit_v_evac_head(h):
                    Vw(sPE, pe_y[h])
                    y1_m[h] = Vi(lambda e, h=h: e.scalar_tensor_tensor(
                        Ybuf[:, PH * h:PH * (h + 1)],
                        xtt[:, PH * h:PH * (h + 1)], dpT[:, h:h + 1],
                        pY[0:128, 0:PH], ALU.mult, ALU.add))
                    Vw(sPE, pe_yi[h])
                    y2_m[h] = Vi(lambda e, h=h: e.tensor_scalar(
                        Ybuf2[:, PH * h:PH * (h + 1)], pYi[0:128, 0:PH],
                        lamT[:, h:h + 1], None, ALU.mult))
                    Vw(sPE, pe_ds[h])
                    st_m[h] = Vi(lambda e, h=h: e.scalar_tensor_tensor(
                        Sf[:, PH * h:PH * (h + 1)],
                        Sf[:, PH * h:PH * (h + 1)],
                        gbc[:, h:h + 1], DSbank[h % 2][0:SG, 0:PH],
                        ALU.mult, ALU.add))

                for i in range(NH + 3):
                    if i < NH:
                        h = i
                        if h - 2 >= 0:
                            Tw(sV, vclip[h - 2])
                        pe_e[h] = Ti(lambda e, h=h, colc=colc: e.matmul(
                            Ebank[h % 2][0:128, 0:128],
                            c_oneh[:, 128 * h:128 * (h + 1)],
                            acum[:, colc:colc + CH], start=True, stop=True))
                    if 0 <= i - 1 < NH:
                        h = i - 1
                        Vw(sPE, pe_e[h])
                        vclip[h] = Vi(lambda e, h=h: e.tensor_scalar(
                            esb[h % 2][:, :], Ebank[h % 2][0:128, 0:128],
                            nacumT[:, h:h + 1], 0.0, ALU.add, ALU.min))
                        Sw(sV, vclip[h])
                        sexp[h] = Si(lambda e, h=h: e.activation(
                            esb[h % 2][:, :], esb[h % 2][:, :], AF.Exp))
                    if 0 <= i - 3 < NH:
                        emit_v_evac_head(i - 3)
                    if 0 <= i - 2 < NH:
                        h = i - 2
                        g = h // 4
                        Vw(sS, sexp[h])
                        if h - 2 >= 0:
                            Vw(sPE, pe_y[h - 2])
                        vm_m[h] = Vi(lambda e, h=h, g=g: e.tensor_tensor(
                            mpb[h % 2][:, :], esb[h % 2][:, :],
                            cbm[:, 128 * g:128 * (g + 1)], ALU.mult))
                        if h - 2 >= 0:
                            Vw(sPE, pe_ds[h - 2])
                        vb_m[h] = Vi(lambda e, h=h, g=g: e.tensor_scalar(
                            Bhat2[h % 2][:, :], BT[:, SG * g:SG * (g + 1)],
                            uT[:, h:h + 1], None, ALU.mult))
                        emit_pe_head(h)
                V_(lambda e: e.tensor_tensor(Ybuf[:, :], Ybuf[:, :],
                                             Ybuf2[:, :], ALU.add))
                v_done = Vi(lambda e: e.drain())
                Yw(sV, v_done)
                Yw(sYRW, curs(sYRW))
                Y_(lambda e, ccg=ccg: e.dma_start(
                    Y_scr[ccg, :, :], Ybuf[:, :]).then_inc(sYRW, DMA))
                yv = incs(sYRW, DMA)
                Yw(sMSC, curs(sMSC))
                Y_(lambda e, ccg=ccg: e.dma_start(
                    C0_scr[ccg, :, :], C0[:, :]).then_inc(sMSC, DMA))
                incs(sMSC, DMA)
                Yw(sMSC, curs(sMSC))
                Y_(lambda e, ccg=ccg: e.dma_start(
                    LAM_scr[ccg, :, :], LAMT[:, :]).then_inc(sMSC, DMA))
                incs(sMSC, DMA)
                Vw(sYRW, yv)
                Vw(sMSC, curs(sMSC))
                Tw(sMSC, curs(sMSC))
                return v_done

            ccg = 0
            v_dt_prev = 0
            for t in range(NTILE):
                if v_dt_prev:
                    Sw(sV, v_dt_prev)
                s_ip = emit_inproj_tile(t)
                s_conv, v_tail = emit_conv_tile(t, s_ip)
                v_dt = emit_dt_tile(t, s_ip)
                v_dt_prev = v_dt
                Tw(sS, s_conv)
                for ct in range(4):
                    emit_ssd_chunk(t, ct, ccg, v_dt)
                    ccg += 1
                if v_tail:
                    Sw(sV, v_tail)

            v_state = Vi(lambda e: e.drain())
            Yw(sV, v_state)
            Yw(sMSC, curs(sMSC))
            Y_(lambda e: e.dma_start(cc_in[:, :], Sf[:, :]).then_inc(sMSC, DMA))
            incs(sMSC, DMA)
            MSC_ST = curs(sMSC)
            Gw(sMSC, MSC_ST)
            G_(lambda e: e.collective_compute(
                "AllGather", ALU.bypass,
                replica_groups=[[0, 1], [2, 3], [4, 5], [6, 7]],
                ins=[cc_in.ap().opt()], outs=[cc_out.ap().opt()])
                .then_inc(sCC, 1))
            incs(sCC, 1)
            Gw(sCC, 1)
            Gw(sAUXG, curs(sAUXG))
            G_(lambda e: e.dma_start(Sinit[:, :], cc_out[0:SG, :])
               .then_inc(sAUXG, DMA))
            incs(sAUXG, DMA)
            Vw(sCC, 1)
            Vw(sAUXG, curs(sAUXG))
            V_(lambda e: e.tensor_scalar(Sinit[:, :], Sinit[:, :],
                                         c_mask32[:, :], None, ALU.mult))
            V_(lambda e: e.tensor_copy(Sinitb[:, :], Sinit[:, :]))
            barrier()

        with contextlib.ExitStack() as es4:
            c_normw = sbt("c_normw", [128, DSSM], F32, es4)
            c_bout = sbt("c_bout", [128, D], F32, es4)
            zfm = sbt("zfm", [128, NK * 128], F32, es4)
            zT = sbt("zT", [128, D], F32, es4)
            ssq8 = sbt("ssq8", [128, 8], F32, es4)
            rstd8 = sbt("rstd8", [128, 8], F32, es4)
            ynb = sbt("ynb", [128, DSSM], BF16, es4)
            ynT = sbt("ynT", [128, DSSM], BF16, es4)
            woutb = sbt("woutb", [128, NK * D], BF16, es4)
            x2n = sbt("x2n", [128, D], BF16, es4)
            lamld = sbt("lamld", [128, NH], F32, es4)
            c0ld = sbt("c0ld", [SG, NG * 128], BF16, es4)

            for dst, src, np_ in [(c_normw, d_nwrow, 128),
                                  (c_bout, d_borow, 128)]:
                Gw(sAUXG, curs(sAUXG))
                G_(lambda e, dst=dst, src=src, np_=np_: e.dma_start(
                    dst[:, :], src.ap().partition_broadcast(np_))
                    .then_inc(sAUXG, DMA))
                incs(sAUXG, DMA)
            AUX4 = curs(sAUXG)
            for k in range(NK):
                src = d_Wout.ap().rearrange("(c p) o -> p c o", p=128)[:, k, :]
                Gw(sWIN, curs(sWIN))
                G_(lambda e, src=src, k=k: e.dma_start(
                    woutb[:, D * k:D * (k + 1)], src).then_inc(sWIN, DMA))
                incs(sWIN, DMA)
            WOUT_LD = curs(sWIN)
            Vw(sAUXG, AUX4)
            Tw(sWIN, WOUT_LD)

            gates = {"ybuf_v": 0, "zfm_pe": 0, "ld_v": 0, "ld_pe": 0,
                     "xmt_s": 0, "xmt_msc": 0}
            for cg in range(NCHUNK):
                if gates["ybuf_v"]:
                    Yw(sV, gates["ybuf_v"])
                Yw(sYRW, curs(sYRW))
                Y_(lambda e, cg=cg: e.dma_start(
                    Ybuf[:, :], Y_scr[cg, :, :]).then_inc(sYRW, DMA))
                y_ld = incs(sYRW, DMA)
                if gates["zfm_pe"]:
                    Yw(sPE, gates["zfm_pe"])
                Yw(sZRW, curs(sZRW))
                Y_(lambda e, cg=cg: e.dma_start(
                    zfm[:, :].rearrange("p (m t) -> p m t", m=16),
                    z_scr.ap().transpose([1, 0, 2])[:, :,
                                                    CH * cg:CH * (cg + 1)])
                    .then_inc(sZRW, DMA))
                z_ld = incs(sZRW, DMA)
                if gates["ld_v"]:
                    Yw(sV, gates["ld_v"])
                if gates["ld_pe"]:
                    Yw(sPE, gates["ld_pe"])
                Yw(sMSC, curs(sMSC))
                Y_(lambda e, cg=cg: e.dma_start(
                    lamld[:, :], LAM_scr[cg, :, :]).then_inc(sMSC, DMA))
                incs(sMSC, DMA)
                Yw(sMSC, curs(sMSC))
                Y_(lambda e, cg=cg: e.dma_start(
                    c0ld[:, :], C0_scr[cg, :, :]).then_inc(sMSC, DMA))
                ld_msc = incs(sMSC, DMA)
                if gates["xmt_s"]:
                    Gw(sS, gates["xmt_s"])
                if gates["xmt_msc"]:
                    Gw(sMSC, gates["xmt_msc"])
                Gw(sXIN, curs(sXIN))
                G_(lambda e, cg=cg: e.dma_start(
                    xmt[:, :], d_x[CH * cg:CH * (cg + 1), :])
                    .then_inc(sXIN, DMA))
                x_ld = incs(sXIN, DMA)
                Tw(sMSC, ld_msc)
                Vw(sYRW, y_ld)
                Vw(sMSC, ld_msc)
                for h in range(NH):
                    g = h // 4
                    cb = pCB if h % 2 == 0 else pE
                    pe_c = Ti(lambda e, h=h, g=g, cb=cb: e.matmul(
                        cb[0:128, 0:PH], c0ld[:, 128 * g:128 * (g + 1)],
                        Sinitb[:, PH * h:PH * (h + 1)], start=True, stop=True))
                    Vw(sPE, pe_c)
                    vv = Vi(lambda e, h=h, cb=cb: e.scalar_tensor_tensor(
                        Ybuf[:, PH * h:PH * (h + 1)], cb[0:128, 0:PH],
                        lamld[:, h:h + 1], Ybuf[:, PH * h:PH * (h + 1)],
                        ALU.mult, ALU.add))
                    Tw(sV, vv)
                gates["ld_v"] = cur("sV")
                gates["ld_pe"] = cur("sPE")
                Vw(sZRW, z_ld)
                Tw(sZRW, z_ld)
                for k in range(NK):
                    pe = Ti(lambda e, k=k: e.transpose(
                        pTf[0:128, 0:128], zfm[:, 128 * k:128 * (k + 1)],
                        c_idf[:, :]))
                    Vw(sPE, pe)
                    vv = Vi(lambda e, k=k: e.tensor_copy(
                        zT[:, 128 * k:128 * (k + 1)], pTf[0:128, 0:128]))
                    Tw(sV, vv)
                gates["zfm_pe"] = cur("sPE")
                v_zt = Vi(lambda e: e.drain())
                Sw(sV, v_zt)
                S_(lambda e: e.activation(zT[:, :], zT[:, :], AF.Silu))
                s_silu = Si(lambda e: e.drain())
                Vw(sS, s_silu)
                V_(lambda e: e.tensor_tensor(Ybuf[:, :], Ybuf[:, :], zT[:, :],
                                             ALU.mult))
                v_gate = Vi(lambda e: e.drain())
                Sw(sV, v_gate)
                for g in range(8):
                    S_(lambda e, g=g: e.activation(
                        sqscr[:, 0:256], Ybuf[:, 256 * g:256 * (g + 1)],
                        AF.Square, accum_out=ssq8[:, g:g + 1]))
                S_(lambda e: e.drain())
                S_(lambda e: e.activation(ssq8[:, :], ssq8[:, :], AF.Ln,
                                          scale=c_sc[:, 6:7],
                                          bias=c_sc[:, 0:1]))
                S_(lambda e: e.drain())
                s_gn = Si(lambda e: e.activation(rstd8[:, :], ssq8[:, :],
                                                 AF.Exp, scale=c_sc[:, 4:5]))
                Vw(sS, s_gn)
                for g in range(8):
                    V_(lambda e, g=g: e.scalar_tensor_tensor(
                        ynb[:, 256 * g:256 * (g + 1)],
                        Ybuf[:, 256 * g:256 * (g + 1)], rstd8[:, g:g + 1],
                        c_normw[:, 256 * g:256 * (g + 1)], ALU.mult, ALU.mult))
                v_yn = Vi(lambda e: e.drain())
                gates["ybuf_v"] = v_yn
                Tw(sV, v_yn)
                for k in range(NK):
                    pe = Ti(lambda e, k=k: e.transpose(
                        pTb[0:128, 0:128], ynb[:, 128 * k:128 * (k + 1)],
                        c_idb[:, :]))
                    Vw(sPE, pe)
                    vv = Vi(lambda e, k=k: e.tensor_copy(
                        ynT[:, 128 * k:128 * (k + 1)], pTb[0:128, 0:128]))
                    Tw(sV, vv)
                banks4 = [pA, pB, pY, pYi]
                for n in range(4):
                    for k in range(NK):
                        T_(lambda e, n=n, k=k: e.matmul(
                            banks4[n][0:128, 0:512],
                            ynT[:, 128 * k:128 * (k + 1)],
                            woutb[:, D * k + 512 * n:D * k + 512 * (n + 1)],
                            start=(k == 0), stop=(k == NK - 1)))
                pe_o = Ti(lambda e: e.drain())
                Vw(sPE, pe_o)
                Vw(sXIN, x_ld)
                for n in range(4):
                    V_(lambda e, n=n: e.tensor_tensor(
                        xmt[:, 512 * n:512 * (n + 1)],
                        xmt[:, 512 * n:512 * (n + 1)],
                        banks4[n][0:128, 0:512], ALU.add))
                V_(lambda e: e.tensor_tensor(xmt[:, :], xmt[:, :],
                                             c_bout[:, :], ALU.add))
                v_xm = Vi(lambda e: e.drain())
                Tw(sV, v_xm)
                Yw(sV, v_xm)
                Yw(sMSC, curs(sMSC))
                Y_(lambda e, cg=cg: e.dma_start(
                    xmid_scr[CH * cg:CH * (cg + 1), :], xmt[:, :])
                    .then_inc(sMSC, DMA))
                incs(sMSC, DMA)
                if stage == "xmid":
                    Yw(sOUT, curs(sOUT))
                    Y_(lambda e, cg=cg: e.dma_start(
                        d_out[CH * cg:CH * (cg + 1), :], xmt[:, :])
                        .then_inc(sOUT, DMA))
                    incs(sOUT, DMA)
                s_n2 = emit_ln(xmt, x2n, 0)
                gates["xmt_s"] = s_n2
                Yw(sS, s_n2)
                Yw(sMSC, curs(sMSC))
                Y_(lambda e, cg=cg: e.dma_start(
                    x2n_scr[CH * cg:CH * (cg + 1), :], x2n[:, :])
                    .then_inc(sMSC, DMA))
                gates["xmt_msc"] = incs(sMSC, DMA)
            barrier()

        if stage == "xmid":
            Yw(sOUT, curs(sOUT))
            _replay(nc, plan)
            return nc

        with contextlib.ExitStack() as esf:
            u16b = [sbt(f"u16b{i}", [128, 512], U16, esf) for i in range(2)]
            i32b = sbt("i32b", [128, 512], I32, esf)
            tnib = sbt("tnib", [128, 512], I32, esf)
            dx1 = sbt("dx1", [128, 512], I32, esf)
            dx2 = sbt("dx2", [128, 512], I32, esf)
            dsg = sbt("dsg", [128, 512], I32, esf)
            dbits = sbt("dbits", [128, 512], U16, esf)
            sclb = [sbt(f"sclb{i}", [128, 512], F16, esf) for i in range(2)]

            dec_gate = [0, 0]

            def emit_decode_c(pi, dst, cslot):
                V_(lambda e, pi=pi: e.tensor_copy(i32b[:, :], u16b[pi][:, :]))
                for j in range(4):
                    V_(lambda e, j=j: e.tensor_scalar(
                        tnib[:, :], i32b[:, :], 4 * j, 15,
                        ALU.logical_shift_right, ALU.bitwise_and))
                    V_(lambda e: e.tensor_scalar(
                        dx1[:, :], tnib[:, :], 7, 6,
                        ALU.bitwise_and, ALU.logical_shift_left))
                    V_(lambda e: e.tensor_scalar(
                        dx2[:, :], tnib[:, :], 14, 7 << 14,
                        ALU.logical_shift_left, ALU.bitwise_and))
                    V_(lambda e: e.tensor_scalar(
                        dsg[:, :], tnib[:, :], 8, 12,
                        ALU.bitwise_and, ALU.logical_shift_left))
                    V_(lambda e: e.scalar_tensor_tensor(
                        dx1[:, :], dx1[:, :], 16384, dx2[:, :],
                        ALU.add, ALU.min))
                    V_(lambda e: e.tensor_tensor(
                        dbits[:, :], dx1[:, :], dsg[:, :], ALU.add))
                    vv = Vi(lambda e, pi=pi, dst=dst, cslot=cslot, j=j:
                            e.tensor_tensor(
                                dst[:, 512 * (4 * cslot + j):
                                    512 * (4 * cslot + j + 1)],
                                dbits[:, :].bitcast(BF16), sclb[pi][:, :],
                                ALU.mult))
                dec_gate[pi] = cur("sV")
                return cur("sV")

            with contextlib.ExitStack() as es5:
                x2nT = sbt("x2nT", [128, NK * T], BF16, es5)
                x2rl = sbt("x2rl", [128, D], BF16, es5)
                decg = [sbt(f"decg{i}", [128, 16 * 512], BF16, es5)
                        for i in range(2)]
                hb = [sbt(f"hb{i}", [128, TT], BF16, es5) for i in range(2)]
                gsil = [sbt(f"gsil{i}", [128, TT], BF16, es5) for i in range(2)]

                x2_gate = 0
                for cg in range(NCHUNK):
                    if x2_gate:
                        Gw(sPE, x2_gate)
                    Gw(sXIN, curs(sXIN))
                    G_(lambda e, cg=cg: e.dma_start(
                        x2rl[:, :], x2n_scr[CH * cg:CH * (cg + 1), :])
                        .then_inc(sXIN, DMA))
                    xv = incs(sXIN, DMA)
                    Tw(sXIN, xv)
                    for cj in range(NK):
                        c4, j4 = cj // 4, cj % 4
                        pe = Ti(lambda e, c4=c4, j4=j4: e.transpose(
                            pTb[0:128, 0:128],
                            x2rl[:, 512 * c4 + j4:512 * (c4 + 1):4],
                            c_idb[:, :]))
                        Vw(sPE, pe)
                        vv = Vi(lambda e, cj=cj, cg=cg: e.tensor_copy(
                            x2nT[:, T * cj + CH * cg:T * cj + CH * (cg + 1)],
                            pTb[0:128, 0:128]))
                        Tw(sV, vv)
                    x2_gate = cur("sPE")

                hb_gate = [0, 0]
                gu_banks = [(pA, pB), (pCB, pE)]
                gu_sgate = [0, 0]
                gu_vgate = [0, 0]
                gu_it = 0
                for s5 in range(11):
                    pe_mm_prev = cur("sPE")
                    for side in range(2):
                        orow = 512 * s5 + II * side
                        for c in range(4):
                            pi = c % 2
                            if dec_gate[pi]:
                                Yw(sV, dec_gate[pi])
                                Gw(sV, dec_gate[pi])
                            Yw(sGUI, curs(sGUI))
                            Y_(lambda e, pi=pi, c=c, orow=orow:
                               e.dma_start_transpose(
                                   u16b[pi][:, :],
                                   d_guu[orow:orow + 512,
                                         128 * c:128 * (c + 1)])
                               .then_inc(sGUI, DMA))
                            uv = incs(sGUI, DMA)
                            for q in range(4):
                                Gw(sGUS, curs(sGUS))
                                G_(lambda e, pi=pi, c=c, q=q, orow=orow:
                                   e.dma_start(
                                       sclb[pi][32 * q:32 * (q + 1), :],
                                       gusq_scr[4 * c + q:4 * c + q + 1,
                                                orow:orow + 512]
                                       .partition_broadcast(32))
                                   .then_inc(sGUS, DMA))
                                incs(sGUS, DMA)
                            Vw(sGUI, uv)
                            Vw(sGUS, curs(sGUS))
                            if c == 0:
                                Vw(sPE, pe_mm_prev)
                            emit_decode_c(pi, decg[side], c)
                    v_dec = Vi(lambda e: e.drain())
                    Tw(sV, v_dec)
                    for j5 in range(4):
                        ki = 4 * s5 + j5
                        for nt in range(4):
                            par = gu_it % 2
                            bg, bu = gu_banks[par]
                            if gu_sgate[par]:
                                Tw(sS, gu_sgate[par])
                            if gu_vgate[par]:
                                Tw(sV, gu_vgate[par])
                            for cj in range(NK):
                                T_(lambda e, cj=cj, j5=j5, nt=nt, bg=bg:
                                   e.matmul(
                                       bg[0:128, 0:TT],
                                       decg[0][:, 512 * cj + j5:
                                               512 * (cj + 1):4],
                                       x2nT[:, T * cj + TT * nt:
                                            T * cj + TT * (nt + 1)],
                                       start=(cj == 0), stop=(cj == NK - 1)))
                            for cj in range(NK - 1):
                                T_(lambda e, cj=cj, j5=j5, nt=nt, bu=bu:
                                   e.matmul(
                                       bu[0:128, 0:TT],
                                       decg[1][:, 512 * cj + j5:
                                               512 * (cj + 1):4],
                                       x2nT[:, T * cj + TT * nt:
                                            T * cj + TT * (nt + 1)],
                                       start=(cj == 0), stop=False))
                            pe_g = Ti(lambda e, j5=j5, nt=nt, bu=bu: e.matmul(
                                bu[0:128, 0:TT],
                                decg[1][:, 512 * (NK - 1) + j5:512 * NK:4],
                                x2nT[:, T * (NK - 1) + TT * nt:
                                     T * (NK - 1) + TT * (nt + 1)],
                                start=False, stop=True))
                            Sw(sPE, pe_g)
                            s_g5 = Si(lambda e, par=par, bg=bg: e.activation(
                                gsil[par][:, :], bg[0:128, 0:TT], AF.Silu))
                            gu_sgate[par] = s_g5
                            hi = gu_it % 2
                            if hb_gate[hi]:
                                Vw(sHIO, hb_gate[hi])
                            Vw(sS, s_g5)
                            vv = Vi(lambda e, hi=hi, par=par, bu=bu:
                                    e.tensor_tensor(
                                        hb[hi][:, :], gsil[par][:, :],
                                        bu[0:128, 0:TT], ALU.mult))
                            gu_vgate[par] = vv
                            Yw(sV, vv)
                            Yw(sHIO, curs(sHIO))
                            Y_(lambda e, ki=ki, nt=nt, hi=hi: e.dma_start(
                                h_scr[ki, :, TT * nt:TT * (nt + 1)],
                                hb[hi][:, :]).then_inc(sHIO, DMA))
                            hb_gate[hi] = incs(sHIO, DMA)
                            gu_it += 1
                barrier()

            with contextlib.ExitStack() as es6:
                decd = sbt("decd", [128, NKI * 512], BF16, es6)
                hall = sbt("hall", [128, 22 * 512], BF16, es6)
                y2f = sbt("y2f", [128, 512], BF16, es6)
                y2T = sbt("y2T", [128, 16 * D], BF16, es6)
                rrms = sbt("rrms", [128, 1], F32, es6)
                outt = sbt("outt", [128, D], F32, es6)

                dbanks = [pCB, pE, pY, pYi]
                for dp in range(4):
                    dbase = 512 * dp
                    pe_prev = cur("sPE")
                    for c6 in range(11):
                        pi = c6 % 2
                        if dec_gate[pi]:
                            Yw(sV, dec_gate[pi])
                            Gw(sV, dec_gate[pi])
                        Yw(sDNI, curs(sDNI))
                        Y_(lambda e, pi=pi, c6=c6, dbase=dbase:
                           e.dma_start_transpose(
                               u16b[pi][:, :],
                               d_dnu[dbase:dbase + 512,
                                     128 * c6:128 * (c6 + 1)])
                           .then_inc(sDNI, DMA))
                        incs(sDNI, DMA)
                        for q in range(4):
                            Gw(sDNS, curs(sDNS))
                            G_(lambda e, pi=pi, c6=c6, q=q, dbase=dbase:
                               e.dma_start(
                                   sclb[pi][32 * q:32 * (q + 1), :],
                                   dnsq_scr[4 * c6 + q:4 * c6 + q + 1,
                                            dbase:dbase + 512]
                                   .partition_broadcast(32))
                               .then_inc(sDNS, DMA))
                            incs(sDNS, DMA)
                        Vw(sDNI, curs(sDNI))
                        Vw(sDNS, curs(sDNS))
                        if c6 == 0:
                            Vw(sPE, pe_prev)
                        emit_decode_c(pi, decd, c6)
                    v_dd = Vi(lambda e: e.drain())
                    Tw(sV, v_dd)
                    pe_h = 0
                    for nt in range(4):
                        for ht in range(2):
                            Yw(sHIO, curs(sHIO))
                            if pe_h:
                                Yw(sPE, pe_h)
                            Yw(sHIO, curs(sHIO))
                            Y_(lambda e, nt=nt, ht=ht: e.dma_start(
                                hall[:, :].rearrange("p (c t) -> p c t", c=22),
                                h_scr.ap().transpose([1, 0, 2])
                                [:, 22 * ht:22 * (ht + 1),
                                 TT * nt:TT * (nt + 1)]).then_inc(sHIO, DMA))
                            h_ld = incs(sHIO, DMA)
                            Tw(sHIO, h_ld)
                            for dc in range(4):
                                for kk in range(22):
                                    kki = 22 * ht + kk
                                    T_(lambda e, kk=kk, kki=kki, dc=dc, ht=ht:
                                       e.matmul(
                                           dbanks[dc][0:128, 0:TT],
                                           decd[:, 512 * kki + 128 * dc:
                                                512 * kki + 128 * (dc + 1)],
                                           hall[:, TT * kk:TT * (kk + 1)],
                                           start=(ht == 0 and kk == 0),
                                           stop=(ht == 1 and kk == 21)))
                            pe_h = Ti(lambda e: e.drain())
                        for dc in range(4):
                            Sw(sPE, cur("sPE"))
                            S_(lambda e, dc=dc: e.activation(
                                y2f[:, :], dbanks[dc][0:128, 0:TT], AF.Identity))
                            s_y2 = Si(lambda e: e.drain())
                            Tw(sS, s_y2)
                            for tc in range(4):
                                pe = Ti(lambda e, tc=tc: e.transpose(
                                    pTb[0:128, 0:128],
                                    y2f[:, 128 * tc:128 * (tc + 1)],
                                    c_idb[:, :]))
                                Vw(sPE, pe)
                                tkc = 4 * nt + tc
                                dcol = dbase + 128 * dc
                                vv = Vi(lambda e, tkc=tkc, dcol=dcol:
                                        e.tensor_copy(
                                            y2T[:, D * tkc + dcol:
                                                D * tkc + dcol + 128],
                                            pTb[0:128, 0:128]))
                                Tw(sV, vv)
                v_p6 = Vi(lambda e: e.drain())

                xm_gate = 0
                for tk in range(16):
                    if xm_gate:
                        Gw(sV, xm_gate)
                    Gw(sXIN, curs(sXIN))
                    G_(lambda e, tk=tk: e.dma_start(
                        xmt[:, :], xmid_scr[CH * tk:CH * (tk + 1), :])
                        .then_inc(sXIN, DMA))
                    xm_ld = incs(sXIN, DMA)
                    Sw(sV, v_p6)
                    for q in range(4):
                        S_(lambda e, tk=tk, q=q: e.activation(
                            sqscr[:, 0:512],
                            y2T[:, D * tk + 512 * q:D * tk + 512 * (q + 1)],
                            AF.Square, accum_out=sqacc[:, q:q + 1]))
                    s_r = Si(lambda e: e.drain())
                    Vw(sS, s_r)
                    V_(lambda e: e.tensor_tensor(sqacc[:, 0:1], sqacc[:, 0:1],
                                                 sqacc[:, 1:2], ALU.add))
                    V_(lambda e: e.tensor_tensor(sqacc[:, 2:3], sqacc[:, 2:3],
                                                 sqacc[:, 3:4], ALU.add))
                    V_(lambda e: e.drain())
                    v_r = Vi(lambda e: e.tensor_tensor(
                        sqacc[:, 0:1], sqacc[:, 0:1], sqacc[:, 2:3], ALU.add))
                    Sw(sV, v_r)
                    S_(lambda e: e.activation(rrms[:, :], sqacc[:, 0:1],
                                              AF.Ln, scale=c_sc[:, 7:8],
                                              bias=c_sc[:, 2:3]))
                    S_(lambda e: e.drain())
                    s_rr = Si(lambda e: e.activation(rrms[:, :], rrms[:, :],
                                                     AF.Exp,
                                                     scale=c_sc[:, 4:5]))
                    Vw(sS, s_rr)
                    Vw(sXIN, xm_ld)
                    for q in range(4):
                        V_(lambda e, tk=tk, q=q: e.scalar_tensor_tensor(
                            outt[:, 512 * q:512 * (q + 1)],
                            y2T[:, D * tk + 512 * q:D * tk + 512 * (q + 1)],
                            rrms[:, :], xmt[:, 512 * q:512 * (q + 1)],
                            ALU.mult, ALU.add))
                    v_o = Vi(lambda e: e.drain())
                    xm_gate = v_o
                    Yw(sV, v_o)
                    Yw(sOUT, curs(sOUT))
                    Y_(lambda e, tk=tk: e.dma_start(
                        d_out[CH * tk:CH * (tk + 1), :], outt[:, :])
                        .then_inc(sOUT, DMA))
                    incs(sOUT, DMA)
                    Vw(sOUT, curs(sOUT))
                Yw(sOUT, curs(sOUT))
        _replay(nc, plan)
    return nc


def _replay(nc, plan):
    with nc.Block() as block:
        @block.sync
        def _(e):
            for f in plan["sync"]:
                f(e)

        @block.gpsimd
        def _(e):
            for f in plan["gpsimd"]:
                f(e)

        @block.vector
        def _(e):
            for f in plan["vector"]:
                f(e)

        @block.scalar
        def _(e):
            for f in plan["scalar"]:
                f(e)

        @block.tensor
        def _(e):
            for f in plan["tensor"]:
                f(e)


def _host_consts():
    ident_b = np.eye(128, dtype=ml_dtypes.bfloat16)
    ident_f = np.eye(128, dtype=np.float32)
    onehot = np.zeros((NH, NH * 128), dtype=np.float32)
    for h in range(NH):
        onehot[h, 128 * h:128 * (h + 1)] = 1.0
    tri = np.triu(np.ones((128, 128), dtype=np.float32))
    return ident_b, ident_f, onehot, tri


def make_in_maps(L0_attn_hidden_states, L0_attn_W_in, L0_attn_b_in, L0_attn_conv_w,
           L0_attn_conv_b, L0_attn_A_log, L0_attn_D_param, L0_attn_dt_bias,
           L0_attn_norm_w, L0_attn_W_out, L0_attn_b_out, L0_attn_mup_vector,
           L0_ffn_gate_up_packed, L0_ffn_gate_up_scales, L0_ffn_down_packed,
           L0_ffn_down_scales):
    x = np.ascontiguousarray(np.asarray(L0_attn_hidden_states, np.float32))
    B, L, _ = x.shape
    W_in = np.ascontiguousarray(np.asarray(L0_attn_W_in, np.float32))
    W_out = np.ascontiguousarray(np.asarray(L0_attn_W_out, np.float32))
    b_in = np.asarray(L0_attn_b_in, np.float32)
    mup = np.asarray(L0_attn_mup_vector, np.float32)
    conv_w = np.asarray(L0_attn_conv_w, np.float32)
    conv_b = np.asarray(L0_attn_conv_b, np.float32)
    A_log = np.asarray(L0_attn_A_log, np.float32)
    D_param = np.asarray(L0_attn_D_param, np.float32)
    dt_bias = np.asarray(L0_attn_dt_bias, np.float32)
    norm_w = np.asarray(L0_attn_norm_w, np.float32)
    b_out = np.asarray(L0_attn_b_out, np.float32)
    gup = np.ascontiguousarray(np.asarray(L0_ffn_gate_up_packed, np.int32))
    gus = np.ascontiguousarray(np.asarray(L0_ffn_gate_up_scales, np.float16))
    dnp = np.ascontiguousarray(np.asarray(L0_ffn_down_packed, np.int32))
    dns = np.ascontiguousarray(np.asarray(L0_ffn_down_scales, np.float16))

    def col_chunks(vec, nchunk):
        out = np.zeros((128, nchunk), np.float32)
        flat = vec.reshape(-1)
        for m in range(nchunk):
            seg = flat[128 * m:128 * (m + 1)]
            out[:len(seg), m] = seg
        return out

    b_in_t = col_chunks(b_in, NM)
    mup_t = col_chunks(mup, NM)
    conv_w_t = np.zeros((128, 80), np.float32)
    conv_b_t = np.zeros((128, 20), np.float32)
    for mi in range(20):
        conv_w_t[:, 4 * mi:4 * mi + 4] = conv_w[128 * mi:128 * (mi + 1), :]
        conv_b_t[:, mi] = conv_b[128 * mi:128 * (mi + 1)]
    ident_b, ident_f, onehot, tri = _host_consts()

    common = dict(
        W_in=W_in, W_out=W_out, b_in_t=b_in_t, mup_t=mup_t,
        conv_w_t=conv_w_t, conv_b_t=conv_b_t,
        A_log_c=np.ascontiguousarray(A_log.reshape(NH, 1)),
        dt_bias_c=np.ascontiguousarray(dt_bias.reshape(NH, 1)),
        D_row=np.ascontiguousarray(D_param.reshape(1, NH)),
        normw_row=np.ascontiguousarray(norm_w.reshape(1, DSSM)),
        bout_row=np.ascontiguousarray(b_out.reshape(1, D)),
        gu_u16=gup.view(np.uint16), gu_scales=gus,
        dn_u16=dnp.view(np.uint16), dn_scales=dns,
        ident_b=ident_b, ident_f=ident_f, onehot=onehot, tri=tri,
        sconst=np.array([[LN_EPS, 0.0, RMS_EPS, 1.0, -0.5, -1.0,
                          1.0 / 256.0, 1.0 / D]], np.float32),
    )
    in_maps = []
    for c in range(8):
        seq, half = c // 2, c % 2
        rows = x[seq, 2048 * half:2048 * (half + 1), :]
        halo = (x[seq, 2045:2048, :] if half == 1
                else np.zeros((3, D), np.float32))
        m = dict(common)
        m["x"] = np.ascontiguousarray(rows)
        m["xh"] = np.ascontiguousarray(halo)
        m["maskp"] = np.array([[float(half)]], np.float32)
        in_maps.append(m)
    return in_maps


def kernel(**inputs):
    x = np.asarray(inputs["L0_attn_hidden_states"], np.float32)
    B, L, _ = x.shape
    in_maps = make_in_maps(**inputs)
    nc = build_program()
    res = run_bass_kernel_spmd(nc, in_maps, core_ids=list(range(8)))
    out = np.empty((B, L, D), np.float32)
    for c in range(8):
        seq, half = c // 2, c % 2
        out[seq, 2048 * half:2048 * (half + 1), :] = \
            np.asarray(res.results[c]["out"], np.float32)
    return out


# version 10
# speedup vs baseline: 1.1301x; 1.1301x over previous
import contextlib
import numpy as np
import ml_dtypes

import concourse.bass as bass
import concourse.mybir as mybir
from concourse.bass_utils import run_bass_kernel_spmd

F32 = mybir.dt.float32
BF16 = mybir.dt.bfloat16
F16 = mybir.dt.float16
I32 = mybir.dt.int32
U16 = mybir.dt.uint16
AF = mybir.ActivationFunctionType
ALU = mybir.AluOpType

D = 2048
PROJ = 4640
DSSM = 2048
NH = 32
PH = 64
NG = 8
SG = 32
T = 2048
NTILE = 4
TT = 512
NCHUNK = 16
CH = 128
NM = 37
NK = 16
LN_EPS = 1e-5
GN_EPS = 1e-5
RMS_EPS = 1e-6
II = 5632
NKI = 44
DMA = 16

STAGE = "full"


def build_program(stage=STAGE):
    nc = bass.Bass("TRN2", target_bir_lowering=False)

    d_x = nc.dram_tensor("x", [T, D], F32, kind="ExternalInput")
    d_xh = nc.dram_tensor("xh", [3, D], F32, kind="ExternalInput")
    d_mask = nc.dram_tensor("maskp", [1, 1], F32, kind="ExternalInput")
    d_Win = nc.dram_tensor("W_in", [D, PROJ], F32, kind="ExternalInput")
    d_Wout = nc.dram_tensor("W_out", [DSSM, D], F32, kind="ExternalInput")
    d_binr = nc.dram_tensor("b_in_t", [128, NM], F32, kind="ExternalInput")
    d_mup = nc.dram_tensor("mup_t", [128, NM], F32, kind="ExternalInput")
    d_convw = nc.dram_tensor("conv_w_t", [128, 80], F32, kind="ExternalInput")
    d_convb = nc.dram_tensor("conv_b_t", [128, 20], F32, kind="ExternalInput")
    d_Alog = nc.dram_tensor("A_log_c", [NH, 1], F32, kind="ExternalInput")
    d_dtb = nc.dram_tensor("dt_bias_c", [NH, 1], F32, kind="ExternalInput")
    d_Drow = nc.dram_tensor("D_row", [1, NH], F32, kind="ExternalInput")
    d_nwrow = nc.dram_tensor("normw_row", [1, DSSM], F32, kind="ExternalInput")
    d_borow = nc.dram_tensor("bout_row", [1, D], F32, kind="ExternalInput")
    d_guu = nc.dram_tensor("gu_u16", [2 * II, 512], U16, kind="ExternalInput")
    d_gus = nc.dram_tensor("gu_scales", [16, 2 * II], F16, kind="ExternalInput")
    d_dnu = nc.dram_tensor("dn_u16", [D, 1408], U16, kind="ExternalInput")
    d_dns = nc.dram_tensor("dn_scales", [NKI, D], F16, kind="ExternalInput")
    d_idb = nc.dram_tensor("ident_b", [128, 128], BF16, kind="ExternalInput")
    d_idf = nc.dram_tensor("ident_f", [128, 128], F32, kind="ExternalInput")
    d_onehot = nc.dram_tensor("onehot", [NH, NH * 128], F32, kind="ExternalInput")
    d_tri = nc.dram_tensor("tri", [128, 128], F32, kind="ExternalInput")
    d_sc = nc.dram_tensor("sconst", [1, 8], F32, kind="ExternalInput")

    d_out = nc.dram_tensor("out", [T, D], F32, kind="ExternalOutput")

    z_scr = nc.dram_tensor("z_scr", [16, 128, T], F32)
    Y_scr = nc.dram_tensor("Y_scr", [NCHUNK, 128, DSSM], F32)
    xmid_scr = nc.dram_tensor("xmid_scr", [T, D], F32)
    x2n_scr = nc.dram_tensor("x2n_scr", [T, D], BF16)
    C0_scr = nc.dram_tensor("C0_scr", [NCHUNK, SG, NG * 128], BF16)
    LAM_scr = nc.dram_tensor("LAM_scr", [NCHUNK, 128, NH], F32)
    h_scr = nc.dram_tensor("h_scr", [NKI, 128, T], BF16)
    gusq_scr = nc.dram_tensor("gusq_scr", [16, 2 * II], F16)
    dnsq_scr = nc.dram_tensor("dnsq_scr", [NKI, D], F16)
    cc_in = nc.dram_tensor("cc_in", [SG, NH * PH], F32)
    cc_out = nc.dram_tensor("cc_out", [2 * SG, NH * PH], F32)

    plan = {e: [] for e in ["sync", "gpsimd", "vector", "scalar", "tensor"]}
    cnt = {}

    def inc(s, n=1):
        cnt[s] = cnt.get(s, 0) + n
        return cnt[s]

    def cur(s):
        return cnt.get(s, 0)

    esg = contextlib.ExitStack()
    with esg:
        def sbt(name, shape, dt_, scope=None):
            return (scope or esg).enter_context(nc.sbuf_tensor(name, shape, dt_))

        def pst(name, shape, dt_):
            return esg.enter_context(nc.psum_tensor(name, shape, dt_))

        SEMN = ["sAUX", "sXIN", "sWIN", "sZRW", "sYRW", "sMSC", "sCC", "sGUI",
                "sDNI", "sHIO", "sOUT", "sPE", "sV", "sS", "sG", "sSY",
                "sAUXG", "sGUS", "sDNS"]
        SEM = {n: esg.enter_context(nc.semaphore(n)) for n in SEMN}
        (sAUX, sXIN, sWIN, sZRW, sYRW, sMSC, sCC, sGUI, sDNI, sHIO, sOUT,
         sPE, sV, sS, sG, sSY, sAUXG, sGUS, sDNS) = (SEM[n] for n in SEMN)
        DMASEMS = [sAUX, sXIN, sWIN, sZRW, sYRW, sMSC, sCC, sGUI, sDNI,
                   sHIO, sOUT, sAUXG, sGUS, sDNS]

        def Y_(fn):
            plan["sync"].append(fn)

        def G_(fn):
            plan["gpsimd"].append(fn)

        def V_(fn):
            plan["vector"].append(fn)

        def S_(fn):
            plan["scalar"].append(fn)

        def T_(fn):
            plan["tensor"].append(fn)

        def Vd():
            V_(lambda e: e.drain())

        def Sd():
            S_(lambda e: e.drain())

        def Gd():
            G_(lambda e: e.drain())

        def Vw(sm, val):
            if val > 0:
                V_(lambda e, sm=sm, val=val: e.wait_ge(sm, val))

        def Sw(sm, val):
            if val > 0:
                S_(lambda e, sm=sm, val=val: e.wait_ge(sm, val))

        def Tw(sm, val):
            if val > 0:
                T_(lambda e, sm=sm, val=val: e.wait_ge(sm, val))

        def Gw(sm, val):
            if val > 0:
                G_(lambda e, sm=sm, val=val: e.wait_ge(sm, val))

        def Yw(sm, val):
            if val > 0:
                Y_(lambda e, sm=sm, val=val: e.wait_ge(sm, val))

        def Vi(fn):
            n = inc("sV")
            V_(lambda e, fn=fn: fn(e).then_inc(sV, 1))
            return n

        def Si(fn):
            n = inc("sS")
            S_(lambda e, fn=fn: fn(e).then_inc(sS, 1))
            return n

        def Ti(fn):
            n = inc("sPE")
            T_(lambda e, fn=fn: fn(e).then_inc(sPE, 1))
            return n

        def Gi(fn):
            n = inc("sG")
            G_(lambda e, fn=fn: fn(e).then_inc(sG, 1))
            return n

        def Yi(fn):
            n = inc("sSY")
            Y_(lambda e, fn=fn: fn(e).then_inc(sSY, 1))
            return n

        def YD(sem, fn):
            Yw(sem, curs(sem))
            Y_(lambda e, fn=fn, sem=sem: fn(e).then_inc(sem, DMA))
            return incs(sem, DMA)

        def GD(sem, fn):
            Gw(sem, curs(sem))
            G_(lambda e, fn=fn, sem=sem: fn(e).then_inc(sem, DMA))
            return incs(sem, DMA)

        _semname = {id(SEM[n]): n for n in SEMN}

        def incs(sm, n=1):
            return inc(_semname[id(sm)], n)

        def curs(sm):
            return cur(_semname[id(sm)])

        def barrier():
            for sm in DMASEMS:
                Yw(sm, curs(sm))
            y_m = Yi(lambda e: e.nop())
            g_m = Gi(lambda e: e.drain())
            v_m = Vi(lambda e: e.drain())
            s_m = Si(lambda e: e.drain())
            t_m = Ti(lambda e: e.drain())
            for w, marks in [(Yw, [(sG, g_m), (sV, v_m), (sS, s_m), (sPE, t_m)]),
                             (Gw, [(sSY, y_m), (sV, v_m), (sS, s_m), (sPE, t_m)]),
                             (Vw, [(sSY, y_m), (sG, g_m), (sS, s_m), (sPE, t_m)]),
                             (Sw, [(sSY, y_m), (sG, g_m), (sV, v_m), (sPE, t_m)]),
                             (Tw, [(sSY, y_m), (sG, g_m), (sV, v_m), (sS, s_m)])]:
                for sm, val in marks:
                    w(sm, val)

        c_idb = sbt("c_idb", [128, 128], BF16)
        c_idf = sbt("c_idf", [128, 128], F32)
        c_tri = sbt("c_tri", [128, 128], F32)
        c_mask128 = sbt("c_mask128", [128, 1], F32)
        c_mask32 = sbt("c_mask32", [SG, 1], F32)
        c_Alog = sbt("c_Alog", [NH, 1], F32)
        c_nA = sbt("c_nA", [NH, 1], F32)
        c_dtb = sbt("c_dtb", [NH, 1], F32)
        c_dparam = sbt("c_dparam", [128, NH], F32)
        c_sc = sbt("c_sc", [128, 8], F32)
        bn6 = sbt("bn6", [128, 24], F32)
        bn2 = sbt("bn2", [128, 2], F32)
        rstd = sbt("rstd", [128, 1], F32)
        nmr = sbt("nmr", [128, 1], F32)
        sqacc = sbt("sqacc", [128, 4], F32)
        sqscr = sbt("sqscr", [128, 512], F32)
        Sf = sbt("Sf", [SG, NH * PH], F32)
        Sinit = sbt("Sinit", [SG, NH * PH], F32)
        Sinitb = sbt("Sinitb", [SG, NH * PH], BF16)
        Ybuf = sbt("Ybuf", [128, DSSM], F32)
        xmt = sbt("xmt", [128, D], F32)

        pA = pst("pA", [128, 512], F32)
        pB = pst("pB", [128, 512], F32)
        pTb = pst("pTb", [128, 1024], BF16)
        pTf = pst("pTf", [128, 512], F32)
        pCB = pst("pCB", [128, 512], F32)
        pE = pst("pE", [128, 512], F32)
        pY = pst("pY", [128, 512], F32)
        pYi = pst("pYi", [128, 512], F32)

        for dst, src in [(c_idb, d_idb), (c_idf, d_idf), (c_tri, d_tri),
                         (c_Alog, d_Alog), (c_dtb, d_dtb)]:
            Yw(sAUX, curs(sAUX))
            Y_(lambda e, dst=dst, src=src: e.dma_start(
                dst[:, :], src[:, :]).then_inc(sAUX, DMA))
            incs(sAUX, DMA)
        for dst, src, np_ in [(c_dparam, d_Drow, 128), (c_mask128, d_mask, 128),
                              (c_mask32, d_mask, SG), (c_sc, d_sc, 128)]:
            Gw(sAUXG, curs(sAUXG))
            G_(lambda e, dst=dst, src=src, np_=np_: e.dma_start(
                dst[:, :], src.ap().partition_broadcast(np_)).then_inc(sAUXG, DMA))
            incs(sAUXG, DMA)
        AUX0 = curs(sAUX)
        AUXG0 = curs(sAUXG)
        Vw(sAUX, AUX0)
        Vw(sAUXG, AUXG0)
        V_(lambda e: e.memset(Sf[:, :], 0.0))
        Sw(sAUX, AUX0)
        Sw(sAUXG, AUXG0)
        S_(lambda e: e.activation(c_nA[:, :], c_Alog[:, :], AF.Exp))
        s_na = Si(lambda e: e.drain())
        Vw(sS, s_na)
        V_(lambda e: e.tensor_scalar(c_nA[:, :], c_nA[:, :], -1.0, None,
                                     ALU.mult))

        with contextlib.ExitStack() as es0:
            scsb = sbt("scsb", [NKI, D], F16, es0)
            scsb2 = sbt("scsb2", [16, 2 * II], F16, es0)
            Gw(sAUXG, curs(sAUXG))
            G_(lambda e: e.dma_start(scsb2[:, :], d_gus[:, :])
               .then_inc(sAUXG, DMA))
            incs(sAUXG, DMA)
            AUX1 = curs(sAUXG)
            Vw(sAUXG, AUX1)
            V_(lambda e: e.tensor_scalar(scsb2[:, :], scsb2[:, :], 0.25, None,
                                         ALU.mult))
            v_gu = Vi(lambda e: e.drain())
            Yw(sV, v_gu)
            Yw(sMSC, curs(sMSC))
            Y_(lambda e: e.dma_start(gusq_scr[:, :], scsb2[:, :])
               .then_inc(sMSC, DMA))
            incs(sMSC, DMA)
            Gw(sAUXG, curs(sAUXG))
            G_(lambda e: e.dma_start(scsb[:, :], d_dns[:, :]).then_inc(sAUXG, DMA))
            incs(sAUXG, DMA)
            AUX2 = curs(sAUXG)
            Vw(sAUXG, AUX2)
            V_(lambda e: e.tensor_scalar(scsb[:, :], scsb[:, :], 0.25, None,
                                         ALU.mult))
            v_dn = Vi(lambda e: e.drain())
            Yw(sV, v_dn)
            Yw(sMSC, curs(sMSC))
            Y_(lambda e: e.dma_start(dnsq_scr[:, :], scsb[:, :])
               .then_inc(sMSC, DMA))
            incs(sMSC, DMA)
            barrier()

        def emit_ln(src, dst, epscol, rows=128):
            for q in range(4):
                V_(lambda e, q=q, src=src, rows=rows: e.bn_stats(
                    bn6[0:rows, 6 * q:6 * (q + 1)],
                    src[0:rows, 512 * q:512 * (q + 1)]))
            V_(lambda e: e.drain())
            vm = Vi(lambda e, rows=rows: e.bn_aggr(bn2[0:rows, :],
                                                   bn6[0:rows, :]))
            Sw(sV, vm)
            S_(lambda e, epscol=epscol, rows=rows: e.activation(
                rstd[0:rows, :], bn2[0:rows, 1:2], AF.Ln,
                bias=c_sc[0:rows, epscol:epscol + 1]))
            S_(lambda e: e.drain())
            sm = Si(lambda e, rows=rows: e.activation(
                rstd[0:rows, :], rstd[0:rows, :], AF.Exp,
                scale=c_sc[0:rows, 4:5]))
            Vw(sS, sm)
            vm2 = Vi(lambda e, rows=rows: e.scalar_tensor_tensor(
                nmr[0:rows, :], bn2[0:rows, 0:1], -1.0, rstd[0:rows, :],
                ALU.mult, ALU.mult))
            V_(lambda e: e.drain())
            Sw(sV, vm2)
            return Si(lambda e, src=src, dst=dst, rows=rows: e.activation(
                dst[0:rows, :], src[0:rows, :], AF.Identity,
                bias=nmr[0:rows, :], scale=rstd[0:rows, :]))

        with contextlib.ExitStack() as es2:
            c_oneh = sbt("c_oneh", [NH, NH * 128], F32, es2)
            c_binr = sbt("c_binr", [128, NM], F32, es2)
            c_mup = sbt("c_mup", [128, NM], F32, es2)
            c_bmup = sbt("c_bmup", [128, NM], F32, es2)
            c_convw = sbt("c_convw", [128, 80], F32, es2)
            c_convb = sbt("c_convb", [128, 20], F32, es2)
            xsub = sbt("xsub", [128, D], F32, es2)
            xn = sbt("xn", [128, D], BF16, es2)
            xnT = sbt("xnT", [128, NK * TT], BF16, es2)
            wbuf = [sbt(f"wbuf{i}", [128, NK * 128], BF16, es2) for i in range(2)]
            zev = [sbt(f"zev{i}", [128, TT], F32, es2) for i in range(2)]
            xin = sbt("xin", [128, 20 * (TT + 3)], F32, es2)
            xbcb = sbt("xbcb", [128, 20 * TT], BF16, es2)
            convacc = [sbt(f"convacc{i}", [128, TT], F32, es2) for i in range(2)]
            dtf = sbt("dtf", [NH, TT], F32, es2)
            dtp = sbt("dtp", [NH, TT], F32, es2)
            sp1 = sbt("sp1", [NH, TT], F32, es2)
            sp2 = sbt("sp2", [NH, TT], F32, es2)
            acum = sbt("acum", [NH, TT], F32, es2)
            zeros32 = sbt("zeros32", [NH, CH], F32, es2)
            ufm = sbt("ufm", [NH, CH], F32, es2)
            lamfm = sbt("lamfm", [NH, CH], F32, es2)
            LAMfm = sbt("LAMfm", [NH, CH], F32, es2)
            gam = sbt("gam", [NH, 1], F32, es2)
            logR = sbt("logR", [NH, 1], F32, es2)
            grow = sbt("grow", [1, NH], F32, es2)
            gbc = sbt("gbc", [SG, NH], F32, es2)
            acumT = sbt("acumT", [128, NH], F32, es2)
            nacumT = sbt("nacumT", [128, NH], F32, es2)
            dtpT = sbt("dtpT", [128, NH], F32, es2)
            uT = sbt("uT", [128, NH], F32, es2)
            lamT = sbt("lamT", [128, NH], F32, es2)
            LAMT = sbt("LAMT", [128, NH], F32, es2)
            dpT = sbt("dpT", [128, NH], F32, es2)
            xtt = sbt("xtt", [128, DSSM], BF16, es2)
            BT = sbt("BT", [128, 512], BF16, es2)
            B0 = sbt("B0", [SG, NG * 128], BF16, es2)
            C0 = sbt("C0", [SG, NG * 128], BF16, es2)
            Bhat = sbt("Bhat", [128, SG], BF16, es2)
            cbm = sbt("cbm", [128, NG * 128], F32, es2)

            Sb = sbt("Sb", [SG, NH * PH], BF16, es2)
            es2h = contextlib.ExitStack()
            xhs = sbt("xhs", [3, D], F32, es2h)
            xhn = sbt("xhn", [3, D], BF16, es2h)
            xhT = sbt("xhT", [128, NK * 3], BF16, es2h)

            for dst, src in [(c_oneh, d_onehot), (c_binr, d_binr),
                             (c_mup, d_mup), (c_convw, d_convw),
                             (c_convb, d_convb)]:
                Yw(sAUX, curs(sAUX))
                Y_(lambda e, dst=dst, src=src: e.dma_start(
                    dst[:, :], src[:, :]).then_inc(sAUX, DMA))
                incs(sAUX, DMA)
            AUXC = curs(sAUX)
            Vw(sAUX, AUXC)
            V_(lambda e: e.tensor_tensor(c_bmup[:, :], c_binr[:, :],
                                         c_mup[:, :], ALU.mult))
            V_(lambda e: e.memset(zeros32[:, :], 0.0))
            V_(lambda e: e.memset(logR[:, :], 0.0))
            v_cst2 = Vi(lambda e: e.drain())
            Sw(sV, v_cst2)
            Tw(sV, v_cst2)
            Gw(sV, v_cst2)

            wbuf_gate = [0, 0]

            def stream_win(m, bi):
                if wbuf_gate[bi]:
                    Gw(sPE, wbuf_gate[bi])
                hi = 128 * (m + 1) if m < 36 else PROJ
                ocnt = 128 if m < 36 else 32
                src = d_Win.ap().rearrange("(c p) o -> p c o", p=128)[
                    :, :, 128 * m:hi]
                Gw(sWIN, curs(sWIN))
                G_(lambda e, src=src, bi=bi, ocnt=ocnt: e.dma_start(
                    wbuf[bi][:, :].rearrange("p (c o) -> p c o", c=NK)
                    [:, :, 0:ocnt], src).then_inc(sWIN, DMA))
                return incs(sWIN, DMA)

            Gw(sXIN, curs(sXIN))
            G_(lambda e: e.dma_start(xhs[:, :], d_xh[:, :]).then_inc(sXIN, DMA))
            incs(sXIN, DMA)
            XH0 = curs(sXIN)
            Vw(sXIN, XH0)
            Sw(sXIN, XH0)
            s_hn = emit_ln(xhs, xhn, 0, rows=3)
            Tw(sS, s_hn)
            for k in range(NK):
                pe = Ti(lambda e, k=k: e.transpose(
                    pTb[0:128, 0:3], xhn[:, 128 * k:128 * (k + 1)],
                    c_idb[0:3, 0:3]))
                Vw(sPE, pe)
                vv = Vi(lambda e, k=k: e.tensor_copy(
                    xhT[:, 3 * k:3 * (k + 1)], pTb[0:128, 0:3]))
                Tw(sV, vv)
            for mi, m in enumerate(range(16, 36)):
                bi = mi % 2
                wv = stream_win(m, bi)
                Tw(sWIN, wv)
                for k in range(NK):
                    T_(lambda e, k=k, bi=bi: e.matmul(
                        pA[0:128, 0:3], wbuf[bi][:, 128 * k:128 * (k + 1)],
                        xhT[:, 3 * k:3 * (k + 1)],
                        start=(k == 0), stop=(k == NK - 1)))
                pe = Ti(lambda e: e.drain())
                wbuf_gate[bi] = pe
                Sw(sPE, pe)
                S_(lambda e, m=m, mi=mi: e.activation(
                    xin[:, 515 * mi:515 * mi + 3], pA[0:128, 0:3], AF.Identity,
                    bias=c_bmup[:, m:m + 1], scale=c_mup[:, m:m + 1]))
                sm = Si(lambda e: e.drain())
                Tw(sS, sm)
                Vw(sS, sm)
                V_(lambda e, mi=mi: e.tensor_scalar(
                    xin[:, 515 * mi:515 * mi + 3],
                    xin[:, 515 * mi:515 * mi + 3],
                    c_mask128[:, :], None, ALU.mult))
            v_halo = Vi(lambda e: e.drain())
            Sw(sV, v_halo)
            es2h.close()
            esb = [sbt(f"esb{i}", [128, 128], F32, es2) for i in range(2)]
            mpb = [sbt(f"mpb{i}", [128, 128], BF16, es2) for i in range(2)]
            Ybuf2 = sbt("Ybuf2", [128, DSSM], F32, es2)
            Bhat2 = [sbt(f"Bhat2{i}", [128, SG], BF16, es2) for i in range(2)]

            xsub_gate = [0]
            zev_dma = [0, 0]

            def emit_inproj_tile(t):
                pe0 = cur("sPE")
                Vw(sPE, pe0)
                Sw(sPE, pe0)
                for gi in range(4):
                    row0 = TT * t + 128 * gi
                    if xsub_gate[0]:
                        Gw(sS, xsub_gate[0])
                    Gw(sXIN, curs(sXIN))
                    G_(lambda e, row0=row0: e.dma_start(
                        xsub[:, :], d_x[row0:row0 + 128, :]).then_inc(sXIN, DMA))
                    xv = incs(sXIN, DMA)
                    Vw(sXIN, xv)
                    Sw(sXIN, xv)
                    s_n = emit_ln(xsub, xn, 0)
                    xsub_gate[0] = s_n
                    Tw(sS, s_n)
                    for k in range(NK):
                        pe = Ti(lambda e, k=k: e.transpose(
                            pTb[0:128, 0:128], xn[:, 128 * k:128 * (k + 1)],
                            c_idb[:, :]))
                        Vw(sPE, pe)
                        vv = Vi(lambda e, k=k, gi=gi: e.tensor_copy(
                            xnT[:, TT * k + 128 * gi:TT * k + 128 * (gi + 1)],
                            pTb[0:128, 0:128]))
                        Tw(sV, vv)
                pa_gate = [0, 0]
                for m in range(NM):
                    bi = m % 2
                    wv = stream_win(m, bi)
                    bank = pA if bi == 0 else pB
                    Tw(sWIN, wv)
                    if pa_gate[bi]:
                        Tw(sS, pa_gate[bi])
                    rows = 128 if m < 36 else 32
                    for k in range(NK):
                        T_(lambda e, k=k, bi=bi, bank=bank, rows=rows: e.matmul(
                            bank[0:rows, 0:TT],
                            wbuf[bi][:, 128 * k:128 * k + rows],
                            xnT[:, TT * k:TT * (k + 1)],
                            start=(k == 0), stop=(k == NK - 1)))
                    pe = Ti(lambda e: e.drain())
                    wbuf_gate[bi] = pe
                    Sw(sPE, pe)
                    if m < 16:
                        zi = m % 2
                        if zev_dma[zi]:
                            Sw(sZRW, zev_dma[zi])
                        S_(lambda e, m=m, bank=bank, zi=zi: e.activation(
                            zev[zi][:, :], bank[0:128, 0:TT], AF.Identity,
                            bias=c_bmup[:, m:m + 1], scale=c_mup[:, m:m + 1]))
                        sm = Si(lambda e: e.drain())
                        pa_gate[bi] = sm
                        Yw(sS, sm)
                        Yw(sZRW, curs(sZRW))
                        Y_(lambda e, m=m, t=t, zi=zi: e.dma_start(
                            z_scr[m, :, TT * t:TT * (t + 1)], zev[zi][:, :])
                            .then_inc(sZRW, DMA))
                        zev_dma[zi] = incs(sZRW, DMA)
                    elif m < 36:
                        mi = m - 16
                        S_(lambda e, mi=mi, m=m, bank=bank: e.activation(
                            xin[:, 515 * mi + 3:515 * mi + 515],
                            bank[0:128, 0:TT], AF.Identity,
                            bias=c_bmup[:, m:m + 1], scale=c_mup[:, m:m + 1]))
                        pa_gate[bi] = Si(lambda e: e.drain())
                    else:
                        S_(lambda e, bank=bank: e.activation(
                            dtf[:, :], bank[0:SG, 0:TT], AF.Identity,
                            bias=c_bmup[0:SG, 36:37], scale=c_mup[0:SG, 36:37]))
                        pa_gate[bi] = Si(lambda e: e.drain())
                return Si(lambda e: e.drain())

            def emit_conv_tile(t, s_ip):
                Vw(sS, s_ip)
                Sw(sS, s_ip)
                silu_done = [0, 0]
                for mi in range(20):
                    base = 515 * mi
                    ci = mi % 2
                    acc = convacc[ci]
                    if silu_done[ci]:
                        Vw(sS, silu_done[ci])
                    V_(lambda e, base=base, mi=mi, acc=acc: e.tensor_scalar(
                        acc[:, :], xin[:, base:base + TT],
                        c_convw[:, 4 * mi:4 * mi + 1], None, ALU.mult))
                    for kk in range(1, 4):
                        V_(lambda e, base=base, mi=mi, kk=kk, acc=acc:
                           e.scalar_tensor_tensor(
                               acc[:, :], xin[:, base + kk:base + kk + TT],
                               c_convw[:, 4 * mi + kk:4 * mi + kk + 1],
                               acc[:, :], ALU.mult, ALU.add))
                    vv = Vi(lambda e: e.drain())
                    Sw(sV, vv)
                    silu_done[ci] = Si(lambda e, mi=mi, acc=acc: e.activation(
                        xbcb[:, TT * mi:TT * (mi + 1)], acc[:, :], AF.Silu,
                        bias=c_convb[:, mi:mi + 1]))
                s_conv = silu_done[1]
                v_tail = 0
                if t < NTILE - 1:
                    Vw(sS, s_conv)
                    for mi in range(19):
                        V_(lambda e, mi=mi: e.tensor_copy(
                            xin[:, 515 * mi:515 * mi + 3],
                            xin[:, 515 * mi + 512:515 * mi + 515]))
                    v_tail = Vi(lambda e: e.tensor_copy(
                        xin[:, 515 * 19:515 * 19 + 3],
                        xin[:, 515 * 19 + 512:515 * 19 + 515]))
                return s_conv, v_tail

            def emit_dt_tile(t, s_ip):
                Vw(sS, s_ip)
                V_(lambda e: e.tensor_scalar(sp1[:, :], dtf[:, :],
                                             c_dtb[0:NH, :], None, ALU.add))
                vm = Vi(lambda e: e.drain())
                Sw(sV, vm)
                S_(lambda e: e.activation(sp2[:, :], sp1[:, :], AF.Abs))
                S_(lambda e: e.drain())
                S_(lambda e: e.activation(sp2[:, :], sp2[:, :], AF.Exp,
                                          scale=c_sc[0:NH, 5:6]))
                S_(lambda e: e.drain())
                sm = Si(lambda e: e.activation(sp2[:, :], sp2[:, :], AF.Ln,
                                               bias=c_sc[0:NH, 3:4]))
                Vw(sS, sm)
                V_(lambda e: e.tensor_scalar(sp1[:, :], sp1[:, :], 0.0, None,
                                             ALU.max))
                V_(lambda e: e.tensor_tensor(dtp[:, :], sp1[:, :], sp2[:, :],
                                             ALU.add))
                V_(lambda e: e.tensor_scalar(sp1[:, :], dtp[:, :],
                                             c_nA[0:NH, :], None, ALU.mult))
                for ct in range(3):
                    V_(lambda e, ct=ct: e.tensor_tensor_scan(
                        acum[:, CH * ct:CH * (ct + 1)],
                        sp1[:, CH * ct:CH * (ct + 1)],
                        zeros32[:, :], 0.0, ALU.add, ALU.add))
                return Vi(lambda e: e.tensor_tensor_scan(
                    acum[:, CH * 3:CH * 4], sp1[:, CH * 3:CH * 4],
                    zeros32[:, :], 0.0, ALU.add, ALU.add))

            ssd_prev = {"v_logr": 0, "v_store": 0, "pe_end": 0}

            def emit_ssd_chunk(t, ct, ccg, v_dt):
                colc = CH * ct
                Sw(sV, v_dt)
                if ssd_prev["v_logr"]:
                    Sw(sV, ssd_prev["v_logr"])
                S_(lambda e, colc=colc: e.activation(
                    gam[:, :], acum[:, colc + CH - 1:colc + CH], AF.Exp))
                S_(lambda e, colc=colc: e.activation(
                    ufm[:, :], acum[:, colc:colc + CH], AF.Exp,
                    scale=c_sc[0:NH, 5:6],
                    bias=acum[:, colc + CH - 1:colc + CH]))
                S_(lambda e, colc=colc: e.activation(
                    lamfm[:, :], acum[:, colc:colc + CH], AF.Exp))
                s_vecs = Si(lambda e, colc=colc: e.activation(
                    LAMfm[:, :], acum[:, colc:colc + CH], AF.Exp,
                    bias=logR[0:NH, :]))
                Vw(sS, s_vecs)
                ssd_prev["v_logr"] = Vi(lambda e, colc=colc: e.tensor_tensor(
                    logR[:, :], logR[:, :],
                    acum[:, colc + CH - 1:colc + CH], ALU.add))
                V_(lambda e: e.drain())
                Tw(sS, s_vecs)
                for srcap, dstT in [
                        (acum[:, colc:colc + CH], acumT),
                        (dtp[:, colc:colc + CH], dtpT),
                        (ufm[:, :], uT), (lamfm[:, :], lamT),
                        (LAMfm[:, :], LAMT)]:
                    pe = Ti(lambda e, srcap=srcap: e.transpose(
                        pTf[0:128, 0:NH], srcap, c_idf[0:NH, 0:NH]))
                    Vw(sPE, pe)
                    vv = Vi(lambda e, dstT=dstT: e.tensor_copy(
                        dstT[:, :], pTf[0:128, 0:NH]))
                    Tw(sV, vv)
                pe = Ti(lambda e: e.transpose(
                    pTf[0:1, 0:NH], gam[:, :], c_idf[0:NH, 0:NH]))
                Vw(sPE, pe)
                vv = Vi(lambda e: e.tensor_copy(grow[:, :], pTf[0:1, 0:NH]))
                Tw(sV, vv)
                T_(lambda e: e.matmul(pTf[0:SG, 256:256 + NH],
                                      c_tri[0:1, 0:SG], grow[:, :],
                                      start=True, stop=True))
                pe2 = Ti(lambda e: e.drain())
                Vw(sPE, pe2)
                vv = Vi(lambda e: e.tensor_copy(gbc[:, :],
                                                pTf[0:SG, 256:256 + NH]))
                Tw(sV, vv)
                V_(lambda e: e.tensor_scalar(nacumT[:, :], acumT[:, :], -1.0,
                                             None, ALU.mult))
                V_(lambda e: e.reciprocal(dpT[:, :], dtpT[:, :]))
                V_(lambda e: e.drain())
                v_tm = Vi(lambda e: e.tensor_tensor(dpT[:, :], dpT[:, :],
                                                    c_dparam[:, :], ALU.mult))
                Tw(sV, v_tm)
                for k in range(NK):
                    pe = Ti(lambda e, k=k, ct=ct: e.transpose(
                        pTb[0:128, 0:128],
                        xbcb[:, TT * k + CH * ct:TT * k + CH * (ct + 1)],
                        c_idb[:, :]))
                    Vw(sPE, pe)
                    for hh in range(2):
                        h = 2 * k + hh
                        vv = Vi(lambda e, h=h, hh=hh: e.tensor_scalar(
                            xtt[:, PH * h:PH * (h + 1)],
                            pTb[0:128, PH * hh:PH * (hh + 1)],
                            dtpT[:, h:h + 1], None, ALU.mult))
                    Tw(sV, vv)
                for bi_, k in enumerate((16, 17, 18, 19)):
                    pe = Ti(lambda e, k=k, ct=ct: e.transpose(
                        pTb[0:128, 0:128],
                        xbcb[:, TT * k + CH * ct:TT * k + CH * (ct + 1)],
                        c_idb[:, :]))
                    Vw(sPE, pe)
                    vv = Vi(lambda e, bi_=bi_: e.tensor_copy(
                        BT[:, 128 * bi_:128 * (bi_ + 1)], pTb[0:128, 0:128]))
                    Tw(sV, vv)
                for g in range(NG):
                    pe = Ti(lambda e, g=g: e.transpose(
                        pTb[0:SG, 0:128], BT[:, SG * g:SG * (g + 1)],
                        c_idb[:, :]))
                    Vw(sPE, pe)
                    vv = Vi(lambda e, g=g: e.tensor_copy(
                        B0[:, 128 * g:128 * (g + 1)], pTb[0:SG, 0:128]))
                    Tw(sV, vv)
                for g in range(NG):
                    pe = Ti(lambda e, g=g: e.transpose(
                        pTb[0:SG, 0:128], BT[:, 256 + SG * g:256 + SG * (g + 1)],
                        c_idb[:, :]))
                    Vw(sPE, pe)
                    vv = Vi(lambda e, g=g: e.tensor_copy(
                        C0[:, 128 * g:128 * (g + 1)], pTb[0:SG, 0:128]))
                    Tw(sV, vv)
                v_sb = Vi(lambda e: e.tensor_copy(Sb[:, :], Sf[:, :]))
                Tw(sV, v_sb)
                cb_gate = [0, 0]
                cb_banks = [pCB, pY]
                for g in range(NG):
                    half = g % 2
                    bank = cb_banks[half]
                    if cb_gate[half]:
                        Tw(sV, cb_gate[half])
                    pe = Ti(lambda e, g=g, bank=bank: e.matmul(
                        bank[0:128, 0:128],
                        B0[:, 128 * g:128 * (g + 1)],
                        C0[:, 128 * g:128 * (g + 1)], start=True, stop=True))
                    Vw(sPE, pe)
                    vv = Vi(lambda e, g=g, bank=bank: e.tensor_tensor(
                        cbm[:, 128 * g:128 * (g + 1)],
                        bank[0:128, 0:128],
                        c_tri[:, :], ALU.mult))
                    cb_gate[half] = vv
                Ebank = [pE, pCB]
                DSbank = [pA, pB]
                pe_e = {}
                pe_y = {}
                pe_yi = {}
                pe_ds = {}
                vclip = {}
                sexp = {}
                vm_m = {}
                vb_m = {}
                y1_m = {}
                y2_m = {}
                st_m = {}
                Tw(sV, cur("sV"))

                def emit_pe_head(h):
                    g = h // 4
                    if h - 1 >= 0:
                        Tw(sV, y1_m[h - 1])
                        Tw(sV, y2_m[h - 1])
                    if h - 2 >= 0:
                        Tw(sV, st_m[h - 2])
                    Tw(sV, vm_m[h])
                    Tw(sV, vb_m[h])
                    pe_y[h] = Ti(lambda e, h=h: e.matmul(
                        pY[0:128, 0:PH], mpb[h % 2][:, :],
                        xtt[:, PH * h:PH * (h + 1)], start=True, stop=True))
                    pe_yi[h] = Ti(lambda e, h=h, g=g: e.matmul(
                        pYi[0:128, 0:PH], C0[:, 128 * g:128 * (g + 1)],
                        Sb[:, PH * h:PH * (h + 1)], start=True, stop=True))
                    pe_ds[h] = Ti(lambda e, h=h: e.matmul(
                        DSbank[h % 2][0:SG, 0:PH], Bhat2[h % 2][:, :],
                        xtt[:, PH * h:PH * (h + 1)], start=True, stop=True))

                def emit_v_evac_head(h):
                    Vw(sPE, pe_y[h])
                    y1_m[h] = Vi(lambda e, h=h: e.scalar_tensor_tensor(
                        Ybuf[:, PH * h:PH * (h + 1)],
                        xtt[:, PH * h:PH * (h + 1)], dpT[:, h:h + 1],
                        pY[0:128, 0:PH], ALU.mult, ALU.add))
                    Vw(sPE, pe_yi[h])
                    y2_m[h] = Vi(lambda e, h=h: e.tensor_scalar(
                        Ybuf2[:, PH * h:PH * (h + 1)], pYi[0:128, 0:PH],
                        lamT[:, h:h + 1], None, ALU.mult))
                    Vw(sPE, pe_ds[h])
                    st_m[h] = Vi(lambda e, h=h: e.scalar_tensor_tensor(
                        Sf[:, PH * h:PH * (h + 1)],
                        Sf[:, PH * h:PH * (h + 1)],
                        gbc[:, h:h + 1], DSbank[h % 2][0:SG, 0:PH],
                        ALU.mult, ALU.add))

                for i in range(NH + 3):
                    if i < NH:
                        h = i
                        if h - 2 >= 0:
                            Tw(sV, vclip[h - 2])
                        pe_e[h] = Ti(lambda e, h=h, colc=colc: e.matmul(
                            Ebank[h % 2][0:128, 0:128],
                            c_oneh[:, 128 * h:128 * (h + 1)],
                            acum[:, colc:colc + CH], start=True, stop=True))
                    if 0 <= i - 1 < NH:
                        h = i - 1
                        Vw(sPE, pe_e[h])
                        vclip[h] = Vi(lambda e, h=h: e.tensor_scalar(
                            esb[h % 2][:, :], Ebank[h % 2][0:128, 0:128],
                            nacumT[:, h:h + 1], 0.0, ALU.add, ALU.min))
                        Sw(sV, vclip[h])
                        sexp[h] = Si(lambda e, h=h: e.activation(
                            esb[h % 2][:, :], esb[h % 2][:, :], AF.Exp))
                    if 0 <= i - 3 < NH:
                        emit_v_evac_head(i - 3)
                    if 0 <= i - 2 < NH:
                        h = i - 2
                        g = h // 4
                        Vw(sS, sexp[h])
                        if h - 2 >= 0:
                            Vw(sPE, pe_y[h - 2])
                        vm_m[h] = Vi(lambda e, h=h, g=g: e.tensor_tensor(
                            mpb[h % 2][:, :], esb[h % 2][:, :],
                            cbm[:, 128 * g:128 * (g + 1)], ALU.mult))
                        if h - 2 >= 0:
                            Vw(sPE, pe_ds[h - 2])
                        vb_m[h] = Vi(lambda e, h=h, g=g: e.tensor_scalar(
                            Bhat2[h % 2][:, :], BT[:, SG * g:SG * (g + 1)],
                            uT[:, h:h + 1], None, ALU.mult))
                        emit_pe_head(h)
                v_done = Vi(lambda e: e.tensor_tensor(
                    Ybuf[:, :], Ybuf[:, :], Ybuf2[:, :], ALU.add))
                Yw(sV, v_done)
                Yw(sYRW, curs(sYRW))
                Y_(lambda e, ccg=ccg: e.dma_start(
                    Y_scr[ccg, :, :], Ybuf[:, :]).then_inc(sYRW, DMA))
                yv = incs(sYRW, DMA)
                Yw(sMSC, curs(sMSC))
                Y_(lambda e, ccg=ccg: e.dma_start(
                    C0_scr[ccg, :, :], C0[:, :]).then_inc(sMSC, DMA))
                incs(sMSC, DMA)
                Yw(sMSC, curs(sMSC))
                Y_(lambda e, ccg=ccg: e.dma_start(
                    LAM_scr[ccg, :, :], LAMT[:, :]).then_inc(sMSC, DMA))
                incs(sMSC, DMA)
                Vw(sYRW, yv)
                Vw(sMSC, curs(sMSC))
                Tw(sMSC, curs(sMSC))
                return v_done

            ccg = 0
            v_dt_prev = 0
            for t in range(NTILE):
                if v_dt_prev:
                    Sw(sV, v_dt_prev)
                s_ip = emit_inproj_tile(t)
                s_conv, v_tail = emit_conv_tile(t, s_ip)
                v_dt = emit_dt_tile(t, s_ip)
                v_dt_prev = v_dt
                Tw(sS, s_conv)
                for ct in range(4):
                    emit_ssd_chunk(t, ct, ccg, v_dt)
                    ccg += 1
                if v_tail:
                    Sw(sV, v_tail)

            v_state = Vi(lambda e: e.drain())
            Yw(sV, v_state)
            Yw(sMSC, curs(sMSC))
            Y_(lambda e: e.dma_start(cc_in[:, :], Sf[:, :]).then_inc(sMSC, DMA))
            incs(sMSC, DMA)
            MSC_ST = curs(sMSC)
            Gw(sMSC, MSC_ST)
            G_(lambda e: e.collective_compute(
                "AllGather", ALU.bypass,
                replica_groups=[[0, 1], [2, 3], [4, 5], [6, 7]],
                ins=[cc_in.ap().opt()], outs=[cc_out.ap().opt()])
                .then_inc(sCC, 1))
            incs(sCC, 1)
            Gw(sCC, 1)
            Gw(sAUXG, curs(sAUXG))
            G_(lambda e: e.dma_start(Sinit[:, :], cc_out[0:SG, :])
               .then_inc(sAUXG, DMA))
            incs(sAUXG, DMA)
            Vw(sCC, 1)
            Vw(sAUXG, curs(sAUXG))
            V_(lambda e: e.tensor_scalar(Sinit[:, :], Sinit[:, :],
                                         c_mask32[:, :], None, ALU.mult))
            V_(lambda e: e.tensor_copy(Sinitb[:, :], Sinit[:, :]))
            barrier()

        with contextlib.ExitStack() as es4:
            c_normw = sbt("c_normw", [128, DSSM], F32, es4)
            c_bout = sbt("c_bout", [128, D], F32, es4)
            zfm = sbt("zfm", [128, NK * 128], F32, es4)
            zT = sbt("zT", [128, D], F32, es4)
            ssq8 = sbt("ssq8", [128, 8], F32, es4)
            rstd8 = sbt("rstd8", [128, 8], F32, es4)
            ynb = sbt("ynb", [128, DSSM], BF16, es4)
            ynT = sbt("ynT", [128, DSSM], BF16, es4)
            woutb = sbt("woutb", [128, NK * D], BF16, es4)
            x2n = sbt("x2n", [128, D], BF16, es4)
            lamld = sbt("lamld", [128, NH], F32, es4)
            c0ld = sbt("c0ld", [SG, NG * 128], BF16, es4)

            for dst, src, np_ in [(c_normw, d_nwrow, 128),
                                  (c_bout, d_borow, 128)]:
                Gw(sAUXG, curs(sAUXG))
                G_(lambda e, dst=dst, src=src, np_=np_: e.dma_start(
                    dst[:, :], src.ap().partition_broadcast(np_))
                    .then_inc(sAUXG, DMA))
                incs(sAUXG, DMA)
            AUX4 = curs(sAUXG)
            for k in range(NK):
                src = d_Wout.ap().rearrange("(c p) o -> p c o", p=128)[:, k, :]
                Gw(sWIN, curs(sWIN))
                G_(lambda e, src=src, k=k: e.dma_start(
                    woutb[:, D * k:D * (k + 1)], src).then_inc(sWIN, DMA))
                incs(sWIN, DMA)
            WOUT_LD = curs(sWIN)
            Vw(sAUXG, AUX4)
            Tw(sWIN, WOUT_LD)

            gates = {"ybuf_v": 0, "zfm_pe": 0, "ld_v": 0, "ld_pe": 0,
                     "xmt_s": 0, "xmt_msc": 0}
            for cg in range(NCHUNK):
                if gates["ybuf_v"]:
                    Yw(sV, gates["ybuf_v"])
                Yw(sYRW, curs(sYRW))
                Y_(lambda e, cg=cg: e.dma_start(
                    Ybuf[:, :], Y_scr[cg, :, :]).then_inc(sYRW, DMA))
                y_ld = incs(sYRW, DMA)
                if gates["zfm_pe"]:
                    Yw(sPE, gates["zfm_pe"])
                Yw(sZRW, curs(sZRW))
                Y_(lambda e, cg=cg: e.dma_start(
                    zfm[:, :].rearrange("p (m t) -> p m t", m=16),
                    z_scr.ap().transpose([1, 0, 2])[:, :,
                                                    CH * cg:CH * (cg + 1)])
                    .then_inc(sZRW, DMA))
                z_ld = incs(sZRW, DMA)
                if gates["ld_v"]:
                    Yw(sV, gates["ld_v"])
                if gates["ld_pe"]:
                    Yw(sPE, gates["ld_pe"])
                Yw(sMSC, curs(sMSC))
                Y_(lambda e, cg=cg: e.dma_start(
                    lamld[:, :], LAM_scr[cg, :, :]).then_inc(sMSC, DMA))
                incs(sMSC, DMA)
                Yw(sMSC, curs(sMSC))
                Y_(lambda e, cg=cg: e.dma_start(
                    c0ld[:, :], C0_scr[cg, :, :]).then_inc(sMSC, DMA))
                ld_msc = incs(sMSC, DMA)
                if gates["xmt_s"]:
                    Gw(sS, gates["xmt_s"])
                if gates["xmt_msc"]:
                    Gw(sMSC, gates["xmt_msc"])
                Gw(sXIN, curs(sXIN))
                G_(lambda e, cg=cg: e.dma_start(
                    xmt[:, :], d_x[CH * cg:CH * (cg + 1), :])
                    .then_inc(sXIN, DMA))
                x_ld = incs(sXIN, DMA)
                Tw(sMSC, ld_msc)
                Vw(sYRW, y_ld)
                Vw(sMSC, ld_msc)
                for h in range(NH):
                    g = h // 4
                    cb = pCB if h % 2 == 0 else pE
                    pe_c = Ti(lambda e, h=h, g=g, cb=cb: e.matmul(
                        cb[0:128, 0:PH], c0ld[:, 128 * g:128 * (g + 1)],
                        Sinitb[:, PH * h:PH * (h + 1)], start=True, stop=True))
                    Vw(sPE, pe_c)
                    vv = Vi(lambda e, h=h, cb=cb: e.scalar_tensor_tensor(
                        Ybuf[:, PH * h:PH * (h + 1)], cb[0:128, 0:PH],
                        lamld[:, h:h + 1], Ybuf[:, PH * h:PH * (h + 1)],
                        ALU.mult, ALU.add))
                    Tw(sV, vv)
                gates["ld_v"] = cur("sV")
                gates["ld_pe"] = cur("sPE")
                Vw(sZRW, z_ld)
                Tw(sZRW, z_ld)
                for k in range(NK):
                    pe = Ti(lambda e, k=k: e.transpose(
                        pTf[0:128, 0:128], zfm[:, 128 * k:128 * (k + 1)],
                        c_idf[:, :]))
                    Vw(sPE, pe)
                    vv = Vi(lambda e, k=k: e.tensor_copy(
                        zT[:, 128 * k:128 * (k + 1)], pTf[0:128, 0:128]))
                    Tw(sV, vv)
                gates["zfm_pe"] = cur("sPE")
                v_zt = cur("sV")
                Sw(sV, v_zt)
                S_(lambda e: e.activation(zT[:, :], zT[:, :], AF.Silu))
                s_silu = Si(lambda e: e.drain())
                Vw(sS, s_silu)
                v_gate = Vi(lambda e: e.tensor_tensor(
                    Ybuf[:, :], Ybuf[:, :], zT[:, :], ALU.mult))
                Sw(sV, v_gate)
                for g in range(8):
                    S_(lambda e, g=g: e.activation(
                        sqscr[:, 0:256], Ybuf[:, 256 * g:256 * (g + 1)],
                        AF.Square, accum_out=ssq8[:, g:g + 1]))
                S_(lambda e: e.drain())
                S_(lambda e: e.activation(ssq8[:, :], ssq8[:, :], AF.Ln,
                                          scale=c_sc[:, 6:7],
                                          bias=c_sc[:, 0:1]))
                S_(lambda e: e.drain())
                s_gn = Si(lambda e: e.activation(rstd8[:, :], ssq8[:, :],
                                                 AF.Exp, scale=c_sc[:, 4:5]))
                Vw(sS, s_gn)
                for g in range(7):
                    V_(lambda e, g=g: e.scalar_tensor_tensor(
                        ynb[:, 256 * g:256 * (g + 1)],
                        Ybuf[:, 256 * g:256 * (g + 1)], rstd8[:, g:g + 1],
                        c_normw[:, 256 * g:256 * (g + 1)], ALU.mult, ALU.mult))
                v_yn = Vi(lambda e: e.scalar_tensor_tensor(
                    ynb[:, 1792:2048], Ybuf[:, 1792:2048], rstd8[:, 7:8],
                    c_normw[:, 1792:2048], ALU.mult, ALU.mult))
                gates["ybuf_v"] = v_yn
                Tw(sV, v_yn)
                for k in range(NK):
                    pe = Ti(lambda e, k=k: e.transpose(
                        pTb[0:128, 0:128], ynb[:, 128 * k:128 * (k + 1)],
                        c_idb[:, :]))
                    Vw(sPE, pe)
                    vv = Vi(lambda e, k=k: e.tensor_copy(
                        ynT[:, 128 * k:128 * (k + 1)], pTb[0:128, 0:128]))
                    Tw(sV, vv)
                banks4 = [pA, pB, pY, pYi]
                for n in range(4):
                    for k in range(NK):
                        T_(lambda e, n=n, k=k: e.matmul(
                            banks4[n][0:128, 0:512],
                            ynT[:, 128 * k:128 * (k + 1)],
                            woutb[:, D * k + 512 * n:D * k + 512 * (n + 1)],
                            start=(k == 0), stop=(k == NK - 1)))
                pe_o = Ti(lambda e: e.drain())
                Vw(sPE, pe_o)
                Vw(sXIN, x_ld)
                for n in range(4):
                    V_(lambda e, n=n: e.tensor_tensor(
                        xmt[:, 512 * n:512 * (n + 1)],
                        xmt[:, 512 * n:512 * (n + 1)],
                        banks4[n][0:128, 0:512], ALU.add))
                v_xm = Vi(lambda e: e.tensor_tensor(
                    xmt[:, :], xmt[:, :], c_bout[:, :], ALU.add))
                Tw(sV, v_xm)
                Yw(sV, v_xm)
                Yw(sMSC, curs(sMSC))
                Y_(lambda e, cg=cg: e.dma_start(
                    xmid_scr[CH * cg:CH * (cg + 1), :], xmt[:, :])
                    .then_inc(sMSC, DMA))
                incs(sMSC, DMA)
                if stage == "xmid":
                    Yw(sOUT, curs(sOUT))
                    Y_(lambda e, cg=cg: e.dma_start(
                        d_out[CH * cg:CH * (cg + 1), :], xmt[:, :])
                        .then_inc(sOUT, DMA))
                    incs(sOUT, DMA)
                s_n2 = emit_ln(xmt, x2n, 0)
                gates["xmt_s"] = s_n2
                Yw(sS, s_n2)
                Yw(sMSC, curs(sMSC))
                Y_(lambda e, cg=cg: e.dma_start(
                    x2n_scr[CH * cg:CH * (cg + 1), :], x2n[:, :])
                    .then_inc(sMSC, DMA))
                gates["xmt_msc"] = incs(sMSC, DMA)
            barrier()

        if stage == "xmid":
            Yw(sOUT, curs(sOUT))
            _replay(nc, plan)
            return nc

        with contextlib.ExitStack() as esf:
            u16b = [sbt(f"u16b{i}", [128, 512], U16, esf) for i in range(2)]
            i32b = sbt("i32b", [128, 512], I32, esf)
            tnib = sbt("tnib", [128, 512], I32, esf)
            dx1 = sbt("dx1", [128, 512], I32, esf)
            dx2 = sbt("dx2", [128, 512], I32, esf)
            dsg = sbt("dsg", [128, 512], I32, esf)
            dbits = sbt("dbits", [128, 512], U16, esf)
            sclb = [sbt(f"sclb{i}", [128, 512], F16, esf) for i in range(2)]

            dec_gate = [0, 0]

            def emit_decode_c(pi, dst, cslot):
                V_(lambda e, pi=pi: e.tensor_copy(i32b[:, :], u16b[pi][:, :]))
                for j in range(4):
                    V_(lambda e, j=j: e.tensor_scalar(
                        tnib[:, :], i32b[:, :], 4 * j, 15,
                        ALU.logical_shift_right, ALU.bitwise_and))
                    V_(lambda e: e.tensor_scalar(
                        dx1[:, :], tnib[:, :], 7, 6,
                        ALU.bitwise_and, ALU.logical_shift_left))
                    V_(lambda e: e.tensor_scalar(
                        dx2[:, :], tnib[:, :], 14, 7 << 14,
                        ALU.logical_shift_left, ALU.bitwise_and))
                    V_(lambda e: e.tensor_scalar(
                        dsg[:, :], tnib[:, :], 8, 12,
                        ALU.bitwise_and, ALU.logical_shift_left))
                    V_(lambda e: e.scalar_tensor_tensor(
                        dx1[:, :], dx1[:, :], 16384, dx2[:, :],
                        ALU.add, ALU.min))
                    V_(lambda e: e.tensor_tensor(
                        dbits[:, :], dx1[:, :], dsg[:, :], ALU.add))
                    vv = Vi(lambda e, pi=pi, dst=dst, cslot=cslot, j=j:
                            e.tensor_tensor(
                                dst[:, 512 * (4 * cslot + j):
                                    512 * (4 * cslot + j + 1)],
                                dbits[:, :].bitcast(BF16), sclb[pi][:, :],
                                ALU.mult))
                dec_gate[pi] = cur("sV")
                return cur("sV")

            with contextlib.ExitStack() as es5:
                x2nT = sbt("x2nT", [128, NK * T], BF16, es5)
                x2rl = sbt("x2rl", [128, D], BF16, es5)
                decg = [sbt(f"decg{i}", [128, 16 * 512], BF16, es5)
                        for i in range(2)]
                hb = [sbt(f"hb{i}", [128, TT], BF16, es5) for i in range(2)]
                gsil = [sbt(f"gsil{i}", [128, TT], BF16, es5) for i in range(2)]

                x2_gate = 0
                for cg in range(NCHUNK):
                    if x2_gate:
                        Gw(sPE, x2_gate)
                    Gw(sXIN, curs(sXIN))
                    G_(lambda e, cg=cg: e.dma_start(
                        x2rl[:, :], x2n_scr[CH * cg:CH * (cg + 1), :])
                        .then_inc(sXIN, DMA))
                    xv = incs(sXIN, DMA)
                    Tw(sXIN, xv)
                    for cj in range(NK):
                        c4, j4 = cj // 4, cj % 4
                        pe = Ti(lambda e, c4=c4, j4=j4: e.transpose(
                            pTb[0:128, 0:128],
                            x2rl[:, 512 * c4 + j4:512 * (c4 + 1):4],
                            c_idb[:, :]))
                        Vw(sPE, pe)
                        vv = Vi(lambda e, cj=cj, cg=cg: e.tensor_copy(
                            x2nT[:, T * cj + CH * cg:T * cj + CH * (cg + 1)],
                            pTb[0:128, 0:128]))
                        Tw(sV, vv)
                    x2_gate = cur("sPE")

                hb_gate = [0, 0]
                gu_banks = [(pA, pB), (pCB, pE)]
                gu_sgate = [0, 0]
                gu_vgate = [0, 0]
                gu_it = 0
                for s5 in range(11):
                    pe_mm_prev = cur("sPE")
                    for side in range(2):
                        orow = 512 * s5 + II * side
                        for c in range(4):
                            pi = c % 2
                            if dec_gate[pi]:
                                Yw(sV, dec_gate[pi])
                                Gw(sV, dec_gate[pi])
                            Yw(sGUI, curs(sGUI))
                            Y_(lambda e, pi=pi, c=c, orow=orow:
                               e.dma_start_transpose(
                                   u16b[pi][:, :],
                                   d_guu[orow:orow + 512,
                                         128 * c:128 * (c + 1)])
                               .then_inc(sGUI, DMA))
                            uv = incs(sGUI, DMA)
                            for q in range(4):
                                Gw(sGUS, curs(sGUS))
                                G_(lambda e, pi=pi, c=c, q=q, orow=orow:
                                   e.dma_start(
                                       sclb[pi][32 * q:32 * (q + 1), :],
                                       gusq_scr[4 * c + q:4 * c + q + 1,
                                                orow:orow + 512]
                                       .partition_broadcast(32))
                                   .then_inc(sGUS, DMA))
                                incs(sGUS, DMA)
                            Vw(sGUI, uv)
                            Vw(sGUS, curs(sGUS))
                            if c == 0:
                                Vw(sPE, pe_mm_prev)
                            emit_decode_c(pi, decg[side], c)
                    v_dec = Vi(lambda e: e.drain())
                    Tw(sV, v_dec)
                    for j5 in range(4):
                        ki = 4 * s5 + j5
                        for nt in range(4):
                            par = gu_it % 2
                            bg, bu = gu_banks[par]
                            if gu_sgate[par]:
                                Tw(sS, gu_sgate[par])
                            if gu_vgate[par]:
                                Tw(sV, gu_vgate[par])
                            for cj in range(NK):
                                T_(lambda e, cj=cj, j5=j5, nt=nt, bg=bg:
                                   e.matmul(
                                       bg[0:128, 0:TT],
                                       decg[0][:, 512 * cj + j5:
                                               512 * (cj + 1):4],
                                       x2nT[:, T * cj + TT * nt:
                                            T * cj + TT * (nt + 1)],
                                       start=(cj == 0), stop=(cj == NK - 1)))
                            for cj in range(NK - 1):
                                T_(lambda e, cj=cj, j5=j5, nt=nt, bu=bu:
                                   e.matmul(
                                       bu[0:128, 0:TT],
                                       decg[1][:, 512 * cj + j5:
                                               512 * (cj + 1):4],
                                       x2nT[:, T * cj + TT * nt:
                                            T * cj + TT * (nt + 1)],
                                       start=(cj == 0), stop=False))
                            pe_g = Ti(lambda e, j5=j5, nt=nt, bu=bu: e.matmul(
                                bu[0:128, 0:TT],
                                decg[1][:, 512 * (NK - 1) + j5:512 * NK:4],
                                x2nT[:, T * (NK - 1) + TT * nt:
                                     T * (NK - 1) + TT * (nt + 1)],
                                start=False, stop=True))
                            Sw(sPE, pe_g)
                            s_g5 = Si(lambda e, par=par, bg=bg: e.activation(
                                gsil[par][:, :], bg[0:128, 0:TT], AF.Silu))
                            gu_sgate[par] = s_g5
                            hi = gu_it % 2
                            if hb_gate[hi]:
                                Vw(sHIO, hb_gate[hi])
                            Vw(sS, s_g5)
                            vv = Vi(lambda e, hi=hi, par=par, bu=bu:
                                    e.tensor_tensor(
                                        hb[hi][:, :], gsil[par][:, :],
                                        bu[0:128, 0:TT], ALU.mult))
                            gu_vgate[par] = vv
                            Yw(sV, vv)
                            Yw(sHIO, curs(sHIO))
                            Y_(lambda e, ki=ki, nt=nt, hi=hi: e.dma_start(
                                h_scr[ki, :, TT * nt:TT * (nt + 1)],
                                hb[hi][:, :]).then_inc(sHIO, DMA))
                            hb_gate[hi] = incs(sHIO, DMA)
                            gu_it += 1
                barrier()

            with contextlib.ExitStack() as es6:
                decd = sbt("decd", [128, NKI * 512], BF16, es6)
                hall = sbt("hall", [128, 22 * 512], BF16, es6)
                y2f = sbt("y2f", [128, 512], BF16, es6)
                y2T = sbt("y2T", [128, 16 * D], BF16, es6)
                rrms = sbt("rrms", [128, 1], F32, es6)
                outt = sbt("outt", [128, D], F32, es6)

                dbanks = [pCB, pE, pY, pYi]
                for dp in range(4):
                    dbase = 512 * dp
                    pe_prev = cur("sPE")
                    for c6 in range(11):
                        pi = c6 % 2
                        if dec_gate[pi]:
                            Yw(sV, dec_gate[pi])
                            Gw(sV, dec_gate[pi])
                        Yw(sDNI, curs(sDNI))
                        Y_(lambda e, pi=pi, c6=c6, dbase=dbase:
                           e.dma_start_transpose(
                               u16b[pi][:, :],
                               d_dnu[dbase:dbase + 512,
                                     128 * c6:128 * (c6 + 1)])
                           .then_inc(sDNI, DMA))
                        incs(sDNI, DMA)
                        for q in range(4):
                            Gw(sDNS, curs(sDNS))
                            G_(lambda e, pi=pi, c6=c6, q=q, dbase=dbase:
                               e.dma_start(
                                   sclb[pi][32 * q:32 * (q + 1), :],
                                   dnsq_scr[4 * c6 + q:4 * c6 + q + 1,
                                            dbase:dbase + 512]
                                   .partition_broadcast(32))
                               .then_inc(sDNS, DMA))
                            incs(sDNS, DMA)
                        Vw(sDNI, curs(sDNI))
                        Vw(sDNS, curs(sDNS))
                        if c6 == 0:
                            Vw(sPE, pe_prev)
                        emit_decode_c(pi, decd, c6)
                    v_dd = Vi(lambda e: e.drain())
                    Tw(sV, v_dd)
                    pe_h = 0
                    for nt in range(4):
                        for ht in range(2):
                            Yw(sHIO, curs(sHIO))
                            if pe_h:
                                Yw(sPE, pe_h)
                            Yw(sHIO, curs(sHIO))
                            Y_(lambda e, nt=nt, ht=ht: e.dma_start(
                                hall[:, :].rearrange("p (c t) -> p c t", c=22),
                                h_scr.ap().transpose([1, 0, 2])
                                [:, 22 * ht:22 * (ht + 1),
                                 TT * nt:TT * (nt + 1)]).then_inc(sHIO, DMA))
                            h_ld = incs(sHIO, DMA)
                            Tw(sHIO, h_ld)
                            for dc in range(4):
                                for kk in range(22):
                                    kki = 22 * ht + kk
                                    T_(lambda e, kk=kk, kki=kki, dc=dc, ht=ht:
                                       e.matmul(
                                           dbanks[dc][0:128, 0:TT],
                                           decd[:, 512 * kki + 128 * dc:
                                                512 * kki + 128 * (dc + 1)],
                                           hall[:, TT * kk:TT * (kk + 1)],
                                           start=(ht == 0 and kk == 0),
                                           stop=(ht == 1 and kk == 21)))
                            pe_h = Ti(lambda e: e.drain())
                        for dc in range(4):
                            Sw(sPE, cur("sPE"))
                            S_(lambda e, dc=dc: e.activation(
                                y2f[:, :], dbanks[dc][0:128, 0:TT], AF.Identity))
                            s_y2 = Si(lambda e: e.drain())
                            Tw(sS, s_y2)
                            for tc in range(4):
                                pe = Ti(lambda e, tc=tc: e.transpose(
                                    pTb[0:128, 0:128],
                                    y2f[:, 128 * tc:128 * (tc + 1)],
                                    c_idb[:, :]))
                                Vw(sPE, pe)
                                tkc = 4 * nt + tc
                                dcol = dbase + 128 * dc
                                vv = Vi(lambda e, tkc=tkc, dcol=dcol:
                                        e.tensor_copy(
                                            y2T[:, D * tkc + dcol:
                                                D * tkc + dcol + 128],
                                            pTb[0:128, 0:128]))
                                Tw(sV, vv)
                v_p6 = Vi(lambda e: e.drain())

                xm_gate = 0
                for tk in range(16):
                    if xm_gate:
                        Gw(sV, xm_gate)
                    Gw(sXIN, curs(sXIN))
                    G_(lambda e, tk=tk: e.dma_start(
                        xmt[:, :], xmid_scr[CH * tk:CH * (tk + 1), :])
                        .then_inc(sXIN, DMA))
                    xm_ld = incs(sXIN, DMA)
                    Sw(sV, v_p6)
                    for q in range(4):
                        S_(lambda e, tk=tk, q=q: e.activation(
                            sqscr[:, 0:512],
                            y2T[:, D * tk + 512 * q:D * tk + 512 * (q + 1)],
                            AF.Square, accum_out=sqacc[:, q:q + 1]))
                    s_r = Si(lambda e: e.drain())
                    Vw(sS, s_r)
                    V_(lambda e: e.tensor_tensor(sqacc[:, 0:1], sqacc[:, 0:1],
                                                 sqacc[:, 1:2], ALU.add))
                    V_(lambda e: e.tensor_tensor(sqacc[:, 2:3], sqacc[:, 2:3],
                                                 sqacc[:, 3:4], ALU.add))
                    V_(lambda e: e.drain())
                    v_r = Vi(lambda e: e.tensor_tensor(
                        sqacc[:, 0:1], sqacc[:, 0:1], sqacc[:, 2:3], ALU.add))
                    Sw(sV, v_r)
                    S_(lambda e: e.activation(rrms[:, :], sqacc[:, 0:1],
                                              AF.Ln, scale=c_sc[:, 7:8],
                                              bias=c_sc[:, 2:3]))
                    S_(lambda e: e.drain())
                    s_rr = Si(lambda e: e.activation(rrms[:, :], rrms[:, :],
                                                     AF.Exp,
                                                     scale=c_sc[:, 4:5]))
                    Vw(sS, s_rr)
                    Vw(sXIN, xm_ld)
                    for q in range(4):
                        V_(lambda e, tk=tk, q=q: e.scalar_tensor_tensor(
                            outt[:, 512 * q:512 * (q + 1)],
                            y2T[:, D * tk + 512 * q:D * tk + 512 * (q + 1)],
                            rrms[:, :], xmt[:, 512 * q:512 * (q + 1)],
                            ALU.mult, ALU.add))
                    v_o = Vi(lambda e: e.drain())
                    xm_gate = v_o
                    Yw(sV, v_o)
                    Yw(sOUT, curs(sOUT))
                    Y_(lambda e, tk=tk: e.dma_start(
                        d_out[CH * tk:CH * (tk + 1), :], outt[:, :])
                        .then_inc(sOUT, DMA))
                    incs(sOUT, DMA)
                    Vw(sOUT, curs(sOUT))
                Yw(sOUT, curs(sOUT))
        _replay(nc, plan)
    return nc


def _replay(nc, plan):
    with nc.Block() as block:
        @block.sync
        def _(e):
            for f in plan["sync"]:
                f(e)

        @block.gpsimd
        def _(e):
            for f in plan["gpsimd"]:
                f(e)

        @block.vector
        def _(e):
            for f in plan["vector"]:
                f(e)

        @block.scalar
        def _(e):
            for f in plan["scalar"]:
                f(e)

        @block.tensor
        def _(e):
            for f in plan["tensor"]:
                f(e)


def _host_consts():
    ident_b = np.eye(128, dtype=ml_dtypes.bfloat16)
    ident_f = np.eye(128, dtype=np.float32)
    onehot = np.zeros((NH, NH * 128), dtype=np.float32)
    for h in range(NH):
        onehot[h, 128 * h:128 * (h + 1)] = 1.0
    tri = np.triu(np.ones((128, 128), dtype=np.float32))
    return ident_b, ident_f, onehot, tri


def make_in_maps(L0_attn_hidden_states, L0_attn_W_in, L0_attn_b_in, L0_attn_conv_w,
           L0_attn_conv_b, L0_attn_A_log, L0_attn_D_param, L0_attn_dt_bias,
           L0_attn_norm_w, L0_attn_W_out, L0_attn_b_out, L0_attn_mup_vector,
           L0_ffn_gate_up_packed, L0_ffn_gate_up_scales, L0_ffn_down_packed,
           L0_ffn_down_scales):
    x = np.ascontiguousarray(np.asarray(L0_attn_hidden_states, np.float32))
    B, L, _ = x.shape
    W_in = np.ascontiguousarray(np.asarray(L0_attn_W_in, np.float32))
    W_out = np.ascontiguousarray(np.asarray(L0_attn_W_out, np.float32))
    b_in = np.asarray(L0_attn_b_in, np.float32)
    mup = np.asarray(L0_attn_mup_vector, np.float32)
    conv_w = np.asarray(L0_attn_conv_w, np.float32)
    conv_b = np.asarray(L0_attn_conv_b, np.float32)
    A_log = np.asarray(L0_attn_A_log, np.float32)
    D_param = np.asarray(L0_attn_D_param, np.float32)
    dt_bias = np.asarray(L0_attn_dt_bias, np.float32)
    norm_w = np.asarray(L0_attn_norm_w, np.float32)
    b_out = np.asarray(L0_attn_b_out, np.float32)
    gup = np.ascontiguousarray(np.asarray(L0_ffn_gate_up_packed, np.int32))
    gus = np.ascontiguousarray(np.asarray(L0_ffn_gate_up_scales, np.float16))
    dnp = np.ascontiguousarray(np.asarray(L0_ffn_down_packed, np.int32))
    dns = np.ascontiguousarray(np.asarray(L0_ffn_down_scales, np.float16))

    def col_chunks(vec, nchunk):
        out = np.zeros((128, nchunk), np.float32)
        flat = vec.reshape(-1)
        for m in range(nchunk):
            seg = flat[128 * m:128 * (m + 1)]
            out[:len(seg), m] = seg
        return out

    b_in_t = col_chunks(b_in, NM)
    mup_t = col_chunks(mup, NM)
    conv_w_t = np.zeros((128, 80), np.float32)
    conv_b_t = np.zeros((128, 20), np.float32)
    for mi in range(20):
        conv_w_t[:, 4 * mi:4 * mi + 4] = conv_w[128 * mi:128 * (mi + 1), :]
        conv_b_t[:, mi] = conv_b[128 * mi:128 * (mi + 1)]
    ident_b, ident_f, onehot, tri = _host_consts()

    common = dict(
        W_in=W_in, W_out=W_out, b_in_t=b_in_t, mup_t=mup_t,
        conv_w_t=conv_w_t, conv_b_t=conv_b_t,
        A_log_c=np.ascontiguousarray(A_log.reshape(NH, 1)),
        dt_bias_c=np.ascontiguousarray(dt_bias.reshape(NH, 1)),
        D_row=np.ascontiguousarray(D_param.reshape(1, NH)),
        normw_row=np.ascontiguousarray(norm_w.reshape(1, DSSM)),
        bout_row=np.ascontiguousarray(b_out.reshape(1, D)),
        gu_u16=gup.view(np.uint16), gu_scales=gus,
        dn_u16=dnp.view(np.uint16), dn_scales=dns,
        ident_b=ident_b, ident_f=ident_f, onehot=onehot, tri=tri,
        sconst=np.array([[LN_EPS, 0.0, RMS_EPS, 1.0, -0.5, -1.0,
                          1.0 / 256.0, 1.0 / D]], np.float32),
    )
    in_maps = []
    for c in range(8):
        seq, half = c // 2, c % 2
        rows = x[seq, 2048 * half:2048 * (half + 1), :]
        halo = (x[seq, 2045:2048, :] if half == 1
                else np.zeros((3, D), np.float32))
        m = dict(common)
        m["x"] = np.ascontiguousarray(rows)
        m["xh"] = np.ascontiguousarray(halo)
        m["maskp"] = np.array([[float(half)]], np.float32)
        in_maps.append(m)
    return in_maps


def kernel(**inputs):
    x = np.asarray(inputs["L0_attn_hidden_states"], np.float32)
    B, L, _ = x.shape
    in_maps = make_in_maps(**inputs)
    nc = build_program()
    res = run_bass_kernel_spmd(nc, in_maps, core_ids=list(range(8)))
    out = np.empty((B, L, D), np.float32)
    for c in range(8):
        seq, half = c // 2, c % 2
        out[seq, 2048 * half:2048 * (half + 1), :] = \
            np.asarray(res.results[c]["out"], np.float32)
    return out


# version 11
# speedup vs baseline: 1.2279x; 1.0866x over previous
import contextlib
import numpy as np
import ml_dtypes

import concourse.bass as bass
import concourse.mybir as mybir
from concourse.bass_utils import run_bass_kernel_spmd

F32 = mybir.dt.float32
BF16 = mybir.dt.bfloat16
F16 = mybir.dt.float16
I32 = mybir.dt.int32
U16 = mybir.dt.uint16
AF = mybir.ActivationFunctionType
ALU = mybir.AluOpType

D = 2048
PROJ = 4640
DSSM = 2048
NH = 32
PH = 64
NG = 8
SG = 32
T = 2048
NTILE = 4
TT = 512
NCHUNK = 16
CH = 128
NM = 37
NK = 16
LN_EPS = 1e-5
GN_EPS = 1e-5
RMS_EPS = 1e-6
II = 5632
NKI = 44
DMA = 16

STAGE = "full"


def build_program(stage=STAGE):
    nc = bass.Bass("TRN2", target_bir_lowering=False)

    d_x = nc.dram_tensor("x", [T, D], F32, kind="ExternalInput")
    d_xh = nc.dram_tensor("xh", [3, D], F32, kind="ExternalInput")
    d_mask = nc.dram_tensor("maskp", [1, 1], F32, kind="ExternalInput")
    d_Win = nc.dram_tensor("W_in", [D, PROJ], F32, kind="ExternalInput")
    d_Wout = nc.dram_tensor("W_out", [DSSM, D], F32, kind="ExternalInput")
    d_binr = nc.dram_tensor("b_in_t", [128, NM], F32, kind="ExternalInput")
    d_mup = nc.dram_tensor("mup_t", [128, NM], F32, kind="ExternalInput")
    d_convw = nc.dram_tensor("conv_w_t", [128, 80], F32, kind="ExternalInput")
    d_convb = nc.dram_tensor("conv_b_t", [128, 20], F32, kind="ExternalInput")
    d_Alog = nc.dram_tensor("A_log_c", [NH, 1], F32, kind="ExternalInput")
    d_dtb = nc.dram_tensor("dt_bias_c", [NH, 1], F32, kind="ExternalInput")
    d_Drow = nc.dram_tensor("D_row", [1, NH], F32, kind="ExternalInput")
    d_nwrow = nc.dram_tensor("normw_row", [1, DSSM], F32, kind="ExternalInput")
    d_borow = nc.dram_tensor("bout_row", [1, D], F32, kind="ExternalInput")
    d_guu = nc.dram_tensor("gu_u16", [2 * II, 512], U16, kind="ExternalInput")
    d_gus = nc.dram_tensor("gu_scales", [16, 2 * II], F16, kind="ExternalInput")
    d_dnu = nc.dram_tensor("dn_u16", [D, 1408], U16, kind="ExternalInput")
    d_dns = nc.dram_tensor("dn_scales", [NKI, D], F16, kind="ExternalInput")
    d_idb = nc.dram_tensor("ident_b", [128, 128], BF16, kind="ExternalInput")
    d_idf = nc.dram_tensor("ident_f", [128, 128], F32, kind="ExternalInput")
    d_onehot = nc.dram_tensor("onehot", [NH, NH * 128], F32, kind="ExternalInput")
    d_tri = nc.dram_tensor("tri", [128, 128], F32, kind="ExternalInput")
    d_sc = nc.dram_tensor("sconst", [1, 8], F32, kind="ExternalInput")

    d_out = nc.dram_tensor("out", [T, D], F32, kind="ExternalOutput")

    z_scr = nc.dram_tensor("z_scr", [2048, T], BF16)
    Y_scr = nc.dram_tensor("Y_scr", [NCHUNK, 128, DSSM], F32)
    xmid_scr = nc.dram_tensor("xmid_scr", [T, D], F32)
    x2n_scr = nc.dram_tensor("x2n_scr", [T, D], BF16)
    C0_scr = nc.dram_tensor("C0_scr", [NCHUNK, SG, NG * 128], BF16)
    LAM_scr = nc.dram_tensor("LAM_scr", [NCHUNK, 128, NH], F32)
    h_scr = nc.dram_tensor("h_scr", [NKI, 128, T], BF16)
    gusq_scr = nc.dram_tensor("gusq_scr", [16, 2 * II], F16)
    dnsq_scr = nc.dram_tensor("dnsq_scr", [NKI, D], F16)
    cc_in = nc.dram_tensor("cc_in", [SG, NH * PH], F32)
    cc_out = nc.dram_tensor("cc_out", [2 * SG, NH * PH], F32)

    plan = {e: [] for e in ["sync", "gpsimd", "vector", "scalar", "tensor"]}
    cnt = {}

    def inc(s, n=1):
        cnt[s] = cnt.get(s, 0) + n
        return cnt[s]

    def cur(s):
        return cnt.get(s, 0)

    esg = contextlib.ExitStack()
    with esg:
        def sbt(name, shape, dt_, scope=None):
            return (scope or esg).enter_context(nc.sbuf_tensor(name, shape, dt_))

        def pst(name, shape, dt_):
            return esg.enter_context(nc.psum_tensor(name, shape, dt_))

        SEMN = ["sAUX", "sXIN", "sWIN", "sZRW", "sYRW", "sMSC", "sCC", "sGUI",
                "sDNI", "sHIO", "sOUT", "sPE", "sV", "sS", "sG", "sSY",
                "sAUXG", "sGUS", "sDNS"]
        SEM = {n: esg.enter_context(nc.semaphore(n)) for n in SEMN}
        (sAUX, sXIN, sWIN, sZRW, sYRW, sMSC, sCC, sGUI, sDNI, sHIO, sOUT,
         sPE, sV, sS, sG, sSY, sAUXG, sGUS, sDNS) = (SEM[n] for n in SEMN)
        DMASEMS = [sAUX, sXIN, sWIN, sZRW, sYRW, sMSC, sCC, sGUI, sDNI,
                   sHIO, sOUT, sAUXG, sGUS, sDNS]

        def Y_(fn):
            plan["sync"].append(fn)

        def G_(fn):
            plan["gpsimd"].append(fn)

        def V_(fn):
            plan["vector"].append(fn)

        def S_(fn):
            plan["scalar"].append(fn)

        def T_(fn):
            plan["tensor"].append(fn)

        def Vd():
            V_(lambda e: e.drain())

        def Sd():
            S_(lambda e: e.drain())

        def Gd():
            G_(lambda e: e.drain())

        def Vw(sm, val):
            if val > 0:
                V_(lambda e, sm=sm, val=val: e.wait_ge(sm, val))

        def Sw(sm, val):
            if val > 0:
                S_(lambda e, sm=sm, val=val: e.wait_ge(sm, val))

        def Tw(sm, val):
            if val > 0:
                T_(lambda e, sm=sm, val=val: e.wait_ge(sm, val))

        def Gw(sm, val):
            if val > 0:
                G_(lambda e, sm=sm, val=val: e.wait_ge(sm, val))

        def Yw(sm, val):
            if val > 0:
                Y_(lambda e, sm=sm, val=val: e.wait_ge(sm, val))

        def Vi(fn):
            n = inc("sV")
            V_(lambda e, fn=fn: fn(e).then_inc(sV, 1))
            return n

        def Si(fn):
            n = inc("sS")
            S_(lambda e, fn=fn: fn(e).then_inc(sS, 1))
            return n

        def Ti(fn):
            n = inc("sPE")
            T_(lambda e, fn=fn: fn(e).then_inc(sPE, 1))
            return n

        def Gi(fn):
            n = inc("sG")
            G_(lambda e, fn=fn: fn(e).then_inc(sG, 1))
            return n

        def Yi(fn):
            n = inc("sSY")
            Y_(lambda e, fn=fn: fn(e).then_inc(sSY, 1))
            return n

        def YD(sem, fn):
            Yw(sem, curs(sem))
            Y_(lambda e, fn=fn, sem=sem: fn(e).then_inc(sem, DMA))
            return incs(sem, DMA)

        def GD(sem, fn):
            Gw(sem, curs(sem))
            G_(lambda e, fn=fn, sem=sem: fn(e).then_inc(sem, DMA))
            return incs(sem, DMA)

        _semname = {id(SEM[n]): n for n in SEMN}

        def incs(sm, n=1):
            return inc(_semname[id(sm)], n)

        def curs(sm):
            return cur(_semname[id(sm)])

        def barrier():
            for sm in DMASEMS:
                Yw(sm, curs(sm))
            y_m = Yi(lambda e: e.nop())
            g_m = Gi(lambda e: e.drain())
            v_m = Vi(lambda e: e.drain())
            s_m = Si(lambda e: e.drain())
            t_m = Ti(lambda e: e.drain())
            for w, marks in [(Yw, [(sG, g_m), (sV, v_m), (sS, s_m), (sPE, t_m)]),
                             (Gw, [(sSY, y_m), (sV, v_m), (sS, s_m), (sPE, t_m)]),
                             (Vw, [(sSY, y_m), (sG, g_m), (sS, s_m), (sPE, t_m)]),
                             (Sw, [(sSY, y_m), (sG, g_m), (sV, v_m), (sPE, t_m)]),
                             (Tw, [(sSY, y_m), (sG, g_m), (sV, v_m), (sS, s_m)])]:
                for sm, val in marks:
                    w(sm, val)

        c_idb = sbt("c_idb", [128, 128], BF16)
        c_idf = sbt("c_idf", [128, 128], F32)
        c_tri = sbt("c_tri", [128, 128], F32)
        c_mask128 = sbt("c_mask128", [128, 1], F32)
        c_mask32 = sbt("c_mask32", [SG, 1], F32)
        c_Alog = sbt("c_Alog", [NH, 1], F32)
        c_nA = sbt("c_nA", [NH, 1], F32)
        c_dtb = sbt("c_dtb", [NH, 1], F32)
        c_dparam = sbt("c_dparam", [128, NH], F32)
        c_sc = sbt("c_sc", [128, 8], F32)
        bn6 = sbt("bn6", [128, 24], F32)
        bn2 = sbt("bn2", [128, 2], F32)
        rstd = sbt("rstd", [128, 1], F32)
        nmr = sbt("nmr", [128, 1], F32)
        sqacc = sbt("sqacc", [128, 4], F32)
        sqscr = sbt("sqscr", [128, 512], F32)
        Sf = sbt("Sf", [SG, NH * PH], F32)
        Sinit = sbt("Sinit", [SG, NH * PH], F32)
        Sinitb = sbt("Sinitb", [SG, NH * PH], BF16)
        Ybuf = sbt("Ybuf", [128, DSSM], F32)
        xmt = sbt("xmt", [128, D], F32)

        pA = pst("pA", [128, 512], F32)
        pB = pst("pB", [128, 512], F32)
        pTb = pst("pTb", [128, 1024], BF16)
        pTf = pst("pTf", [128, 512], F32)
        pCB = pst("pCB", [128, 512], F32)
        pE = pst("pE", [128, 512], F32)
        pY = pst("pY", [128, 512], F32)
        pYi = pst("pYi", [128, 512], F32)

        for dst, src in [(c_idb, d_idb), (c_idf, d_idf), (c_tri, d_tri),
                         (c_Alog, d_Alog), (c_dtb, d_dtb)]:
            Yw(sAUX, curs(sAUX))
            Y_(lambda e, dst=dst, src=src: e.dma_start(
                dst[:, :], src[:, :]).then_inc(sAUX, DMA))
            incs(sAUX, DMA)
        for dst, src, np_ in [(c_dparam, d_Drow, 128), (c_mask128, d_mask, 128),
                              (c_mask32, d_mask, SG), (c_sc, d_sc, 128)]:
            Gw(sAUXG, curs(sAUXG))
            G_(lambda e, dst=dst, src=src, np_=np_: e.dma_start(
                dst[:, :], src.ap().partition_broadcast(np_)).then_inc(sAUXG, DMA))
            incs(sAUXG, DMA)
        AUX0 = curs(sAUX)
        AUXG0 = curs(sAUXG)
        Vw(sAUX, AUX0)
        Vw(sAUXG, AUXG0)
        V_(lambda e: e.memset(Sf[:, :], 0.0))
        Sw(sAUX, AUX0)
        Sw(sAUXG, AUXG0)
        S_(lambda e: e.activation(c_nA[:, :], c_Alog[:, :], AF.Exp))
        s_na = Si(lambda e: e.drain())
        Vw(sS, s_na)
        V_(lambda e: e.tensor_scalar(c_nA[:, :], c_nA[:, :], -1.0, None,
                                     ALU.mult))

        with contextlib.ExitStack() as es0:
            scsb = sbt("scsb", [NKI, D], F16, es0)
            scsb2 = sbt("scsb2", [16, 2 * II], F16, es0)
            Gw(sAUXG, curs(sAUXG))
            G_(lambda e: e.dma_start(scsb2[:, :], d_gus[:, :])
               .then_inc(sAUXG, DMA))
            incs(sAUXG, DMA)
            AUX1 = curs(sAUXG)
            Vw(sAUXG, AUX1)
            V_(lambda e: e.tensor_scalar(scsb2[:, :], scsb2[:, :], 0.25, None,
                                         ALU.mult))
            v_gu = Vi(lambda e: e.drain())
            Yw(sV, v_gu)
            Yw(sMSC, curs(sMSC))
            Y_(lambda e: e.dma_start(gusq_scr[:, :], scsb2[:, :])
               .then_inc(sMSC, DMA))
            incs(sMSC, DMA)
            Gw(sAUXG, curs(sAUXG))
            G_(lambda e: e.dma_start(scsb[:, :], d_dns[:, :]).then_inc(sAUXG, DMA))
            incs(sAUXG, DMA)
            AUX2 = curs(sAUXG)
            Vw(sAUXG, AUX2)
            V_(lambda e: e.tensor_scalar(scsb[:, :], scsb[:, :], 0.25, None,
                                         ALU.mult))
            v_dn = Vi(lambda e: e.drain())
            Yw(sV, v_dn)
            Yw(sMSC, curs(sMSC))
            Y_(lambda e: e.dma_start(dnsq_scr[:, :], scsb[:, :])
               .then_inc(sMSC, DMA))
            incs(sMSC, DMA)
            barrier()

        def emit_ln(src, dst, epscol, rows=128):
            for q in range(4):
                V_(lambda e, q=q, src=src, rows=rows: e.bn_stats(
                    bn6[0:rows, 6 * q:6 * (q + 1)],
                    src[0:rows, 512 * q:512 * (q + 1)]))
            V_(lambda e: e.drain())
            vm = Vi(lambda e, rows=rows: e.bn_aggr(bn2[0:rows, :],
                                                   bn6[0:rows, :]))
            Sw(sV, vm)
            S_(lambda e, epscol=epscol, rows=rows: e.activation(
                rstd[0:rows, :], bn2[0:rows, 1:2], AF.Ln,
                bias=c_sc[0:rows, epscol:epscol + 1]))
            S_(lambda e: e.drain())
            sm = Si(lambda e, rows=rows: e.activation(
                rstd[0:rows, :], rstd[0:rows, :], AF.Exp,
                scale=c_sc[0:rows, 4:5]))
            Vw(sS, sm)
            vm2 = Vi(lambda e, rows=rows: e.scalar_tensor_tensor(
                nmr[0:rows, :], bn2[0:rows, 0:1], -1.0, rstd[0:rows, :],
                ALU.mult, ALU.mult))
            V_(lambda e: e.drain())
            Sw(sV, vm2)
            return Si(lambda e, src=src, dst=dst, rows=rows: e.activation(
                dst[0:rows, :], src[0:rows, :], AF.Identity,
                bias=nmr[0:rows, :], scale=rstd[0:rows, :]))

        with contextlib.ExitStack() as es2:
            c_oneh = sbt("c_oneh", [NH, NH * 128], F32, es2)
            c_binr = sbt("c_binr", [128, NM], F32, es2)
            c_mup = sbt("c_mup", [128, NM], F32, es2)
            c_bmup = sbt("c_bmup", [128, NM], F32, es2)
            c_convw = sbt("c_convw", [128, 80], F32, es2)
            c_convb = sbt("c_convb", [128, 20], F32, es2)
            xsub = sbt("xsub", [128, D], F32, es2)
            xn = sbt("xn", [128, D], BF16, es2)
            xnT = sbt("xnT", [128, NK * TT], BF16, es2)
            wbuf = [sbt(f"wbuf{i}", [128, NK * 128], BF16, es2) for i in range(2)]
            zev = [sbt(f"zev{i}", [128, TT], BF16, es2) for i in range(2)]
            xin = sbt("xin", [128, 20 * (TT + 3)], F32, es2)
            xbcb = sbt("xbcb", [128, 20 * TT], BF16, es2)
            convacc = [sbt(f"convacc{i}", [128, TT], F32, es2) for i in range(2)]
            dtf = sbt("dtf", [NH, TT], F32, es2)
            dtp = sbt("dtp", [NH, TT], F32, es2)
            sp1 = sbt("sp1", [NH, TT], F32, es2)
            sp2 = sbt("sp2", [NH, TT], F32, es2)
            acum = sbt("acum", [NH, TT], F32, es2)
            zeros32 = sbt("zeros32", [NH, CH], F32, es2)
            ufm = sbt("ufm", [NH, CH], F32, es2)
            lamfm = sbt("lamfm", [NH, CH], F32, es2)
            LAMfm = sbt("LAMfm", [NH, CH], F32, es2)
            gam = sbt("gam", [NH, 1], F32, es2)
            logR = sbt("logR", [NH, 1], F32, es2)
            grow = sbt("grow", [1, NH], F32, es2)
            gbc = sbt("gbc", [SG, NH], F32, es2)
            acumT = sbt("acumT", [128, NH], F32, es2)
            nacumT = sbt("nacumT", [128, NH], F32, es2)
            dtpT = sbt("dtpT", [128, NH], F32, es2)
            uT = sbt("uT", [128, NH], F32, es2)
            lamT = sbt("lamT", [128, NH], F32, es2)
            LAMT = sbt("LAMT", [128, NH], F32, es2)
            dpT = sbt("dpT", [128, NH], F32, es2)
            xtt = sbt("xtt", [128, DSSM], BF16, es2)
            BT = sbt("BT", [128, 512], BF16, es2)
            B0 = sbt("B0", [SG, NG * 128], BF16, es2)
            C0 = sbt("C0", [SG, NG * 128], BF16, es2)
            Bhat = sbt("Bhat", [128, SG], BF16, es2)
            cbm = sbt("cbm", [128, NG * 128], F32, es2)

            Sb = sbt("Sb", [SG, NH * PH], BF16, es2)
            es2h = contextlib.ExitStack()
            xhs = sbt("xhs", [3, D], F32, es2h)
            xhn = sbt("xhn", [3, D], BF16, es2h)
            xhT = sbt("xhT", [128, NK * 3], BF16, es2h)

            for dst, src in [(c_oneh, d_onehot), (c_binr, d_binr),
                             (c_mup, d_mup), (c_convw, d_convw),
                             (c_convb, d_convb)]:
                Yw(sAUX, curs(sAUX))
                Y_(lambda e, dst=dst, src=src: e.dma_start(
                    dst[:, :], src[:, :]).then_inc(sAUX, DMA))
                incs(sAUX, DMA)
            AUXC = curs(sAUX)
            Vw(sAUX, AUXC)
            V_(lambda e: e.tensor_tensor(c_bmup[:, :], c_binr[:, :],
                                         c_mup[:, :], ALU.mult))
            V_(lambda e: e.memset(zeros32[:, :], 0.0))
            V_(lambda e: e.memset(logR[:, :], 0.0))
            v_cst2 = Vi(lambda e: e.drain())
            Sw(sV, v_cst2)
            Tw(sV, v_cst2)
            Gw(sV, v_cst2)

            wbuf_gate = [0, 0]

            def stream_win(m, bi):
                if wbuf_gate[bi]:
                    Gw(sPE, wbuf_gate[bi])
                hi = 128 * (m + 1) if m < 36 else PROJ
                ocnt = 128 if m < 36 else 32
                src = d_Win.ap().rearrange("(c p) o -> p c o", p=128)[
                    :, :, 128 * m:hi]
                Gw(sWIN, curs(sWIN))
                G_(lambda e, src=src, bi=bi, ocnt=ocnt: e.dma_start(
                    wbuf[bi][:, :].rearrange("p (c o) -> p c o", c=NK)
                    [:, :, 0:ocnt], src).then_inc(sWIN, DMA))
                return incs(sWIN, DMA)

            Gw(sXIN, curs(sXIN))
            G_(lambda e: e.dma_start(xhs[:, :], d_xh[:, :]).then_inc(sXIN, DMA))
            incs(sXIN, DMA)
            XH0 = curs(sXIN)
            Vw(sXIN, XH0)
            Sw(sXIN, XH0)
            s_hn = emit_ln(xhs, xhn, 0, rows=3)
            Tw(sS, s_hn)
            for k in range(NK):
                pe = Ti(lambda e, k=k: e.transpose(
                    pTb[0:128, 0:3], xhn[:, 128 * k:128 * (k + 1)],
                    c_idb[0:3, 0:3]))
                Vw(sPE, pe)
                vv = Vi(lambda e, k=k: e.tensor_copy(
                    xhT[:, 3 * k:3 * (k + 1)], pTb[0:128, 0:3]))
                Tw(sV, vv)
            for mi, m in enumerate(range(16, 36)):
                bi = mi % 2
                wv = stream_win(m, bi)
                Tw(sWIN, wv)
                for k in range(NK):
                    T_(lambda e, k=k, bi=bi: e.matmul(
                        pA[0:128, 0:3], wbuf[bi][:, 128 * k:128 * (k + 1)],
                        xhT[:, 3 * k:3 * (k + 1)],
                        start=(k == 0), stop=(k == NK - 1)))
                pe = Ti(lambda e: e.drain())
                wbuf_gate[bi] = pe
                Sw(sPE, pe)
                S_(lambda e, m=m, mi=mi: e.activation(
                    xin[:, 515 * mi:515 * mi + 3], pA[0:128, 0:3], AF.Identity,
                    bias=c_bmup[:, m:m + 1], scale=c_mup[:, m:m + 1]))
                sm = Si(lambda e: e.drain())
                Tw(sS, sm)
                Vw(sS, sm)
                V_(lambda e, mi=mi: e.tensor_scalar(
                    xin[:, 515 * mi:515 * mi + 3],
                    xin[:, 515 * mi:515 * mi + 3],
                    c_mask128[:, :], None, ALU.mult))
            v_halo = Vi(lambda e: e.drain())
            Sw(sV, v_halo)
            es2h.close()
            esb = [sbt(f"esb{i}", [128, 128], F32, es2) for i in range(2)]
            mpb = [sbt(f"mpb{i}", [128, 128], BF16, es2) for i in range(2)]
            Ybuf2 = sbt("Ybuf2", [128, DSSM], F32, es2)
            Bhat2 = [sbt(f"Bhat2{i}", [128, SG], BF16, es2) for i in range(2)]

            xsub_gate = [0]
            zev_dma = [0, 0]

            def emit_inproj_tile(t):
                pe0 = cur("sPE")
                Vw(sPE, pe0)
                Sw(sPE, pe0)
                for gi in range(4):
                    row0 = TT * t + 128 * gi
                    if xsub_gate[0]:
                        Gw(sS, xsub_gate[0])
                    Gw(sXIN, curs(sXIN))
                    G_(lambda e, row0=row0: e.dma_start(
                        xsub[:, :], d_x[row0:row0 + 128, :]).then_inc(sXIN, DMA))
                    xv = incs(sXIN, DMA)
                    Vw(sXIN, xv)
                    Sw(sXIN, xv)
                    s_n = emit_ln(xsub, xn, 0)
                    xsub_gate[0] = s_n
                    Tw(sS, s_n)
                    for k in range(NK):
                        pe = Ti(lambda e, k=k: e.transpose(
                            pTb[0:128, 0:128], xn[:, 128 * k:128 * (k + 1)],
                            c_idb[:, :]))
                        Vw(sPE, pe)
                        vv = Vi(lambda e, k=k, gi=gi: e.tensor_copy(
                            xnT[:, TT * k + 128 * gi:TT * k + 128 * (gi + 1)],
                            pTb[0:128, 0:128]))
                        Tw(sV, vv)
                pa_gate = [0, 0]
                for m in range(NM):
                    bi = m % 2
                    wv = stream_win(m, bi)
                    bank = pA if bi == 0 else pB
                    Tw(sWIN, wv)
                    if pa_gate[bi]:
                        Tw(sS, pa_gate[bi])
                    rows = 128 if m < 36 else 32
                    for k in range(NK):
                        T_(lambda e, k=k, bi=bi, bank=bank, rows=rows: e.matmul(
                            bank[0:rows, 0:TT],
                            wbuf[bi][:, 128 * k:128 * k + rows],
                            xnT[:, TT * k:TT * (k + 1)],
                            start=(k == 0), stop=(k == NK - 1)))
                    pe = Ti(lambda e: e.drain())
                    wbuf_gate[bi] = pe
                    Sw(sPE, pe)
                    if m < 16:
                        zi = m % 2
                        if zev_dma[zi]:
                            Sw(sZRW, zev_dma[zi])
                        S_(lambda e, m=m, bank=bank, zi=zi: e.activation(
                            zev[zi][:, :], bank[0:128, 0:TT], AF.Identity,
                            bias=c_bmup[:, m:m + 1], scale=c_mup[:, m:m + 1]))
                        sm = Si(lambda e: e.drain())
                        pa_gate[bi] = sm
                        Yw(sS, sm)
                        Yw(sZRW, curs(sZRW))
                        Y_(lambda e, m=m, t=t, zi=zi: e.dma_start(
                            z_scr[128 * m:128 * (m + 1), TT * t:TT * (t + 1)],
                            zev[zi][:, :]).then_inc(sZRW, DMA))
                        zev_dma[zi] = incs(sZRW, DMA)
                    elif m < 36:
                        mi = m - 16
                        S_(lambda e, mi=mi, m=m, bank=bank: e.activation(
                            xin[:, 515 * mi + 3:515 * mi + 515],
                            bank[0:128, 0:TT], AF.Identity,
                            bias=c_bmup[:, m:m + 1], scale=c_mup[:, m:m + 1]))
                        pa_gate[bi] = Si(lambda e: e.drain())
                    else:
                        S_(lambda e, bank=bank: e.activation(
                            dtf[:, :], bank[0:SG, 0:TT], AF.Identity,
                            bias=c_bmup[0:SG, 36:37], scale=c_mup[0:SG, 36:37]))
                        pa_gate[bi] = Si(lambda e: e.drain())
                return Si(lambda e: e.drain())

            def emit_conv_tile(t, s_ip):
                Vw(sS, s_ip)
                Sw(sS, s_ip)
                silu_done = [0, 0]
                for mi in range(20):
                    base = 515 * mi
                    ci = mi % 2
                    acc = convacc[ci]
                    if silu_done[ci]:
                        Vw(sS, silu_done[ci])
                    V_(lambda e, base=base, mi=mi, acc=acc: e.tensor_scalar(
                        acc[:, :], xin[:, base:base + TT],
                        c_convw[:, 4 * mi:4 * mi + 1], None, ALU.mult))
                    for kk in range(1, 4):
                        V_(lambda e, base=base, mi=mi, kk=kk, acc=acc:
                           e.scalar_tensor_tensor(
                               acc[:, :], xin[:, base + kk:base + kk + TT],
                               c_convw[:, 4 * mi + kk:4 * mi + kk + 1],
                               acc[:, :], ALU.mult, ALU.add))
                    vv = Vi(lambda e: e.drain())
                    Sw(sV, vv)
                    silu_done[ci] = Si(lambda e, mi=mi, acc=acc: e.activation(
                        xbcb[:, TT * mi:TT * (mi + 1)], acc[:, :], AF.Silu,
                        bias=c_convb[:, mi:mi + 1]))
                s_conv = silu_done[1]
                v_tail = 0
                if t < NTILE - 1:
                    Vw(sS, s_conv)
                    for mi in range(19):
                        V_(lambda e, mi=mi: e.tensor_copy(
                            xin[:, 515 * mi:515 * mi + 3],
                            xin[:, 515 * mi + 512:515 * mi + 515]))
                    v_tail = Vi(lambda e: e.tensor_copy(
                        xin[:, 515 * 19:515 * 19 + 3],
                        xin[:, 515 * 19 + 512:515 * 19 + 515]))
                return s_conv, v_tail

            def emit_dt_tile(t, s_ip):
                Vw(sS, s_ip)
                V_(lambda e: e.tensor_scalar(sp1[:, :], dtf[:, :],
                                             c_dtb[0:NH, :], None, ALU.add))
                vm = Vi(lambda e: e.drain())
                Sw(sV, vm)
                S_(lambda e: e.activation(sp2[:, :], sp1[:, :], AF.Abs))
                S_(lambda e: e.drain())
                S_(lambda e: e.activation(sp2[:, :], sp2[:, :], AF.Exp,
                                          scale=c_sc[0:NH, 5:6]))
                S_(lambda e: e.drain())
                sm = Si(lambda e: e.activation(sp2[:, :], sp2[:, :], AF.Ln,
                                               bias=c_sc[0:NH, 3:4]))
                Vw(sS, sm)
                V_(lambda e: e.tensor_scalar(sp1[:, :], sp1[:, :], 0.0, None,
                                             ALU.max))
                V_(lambda e: e.tensor_tensor(dtp[:, :], sp1[:, :], sp2[:, :],
                                             ALU.add))
                V_(lambda e: e.tensor_scalar(sp1[:, :], dtp[:, :],
                                             c_nA[0:NH, :], None, ALU.mult))
                for ct in range(3):
                    V_(lambda e, ct=ct: e.tensor_tensor_scan(
                        acum[:, CH * ct:CH * (ct + 1)],
                        sp1[:, CH * ct:CH * (ct + 1)],
                        zeros32[:, :], 0.0, ALU.add, ALU.add))
                return Vi(lambda e: e.tensor_tensor_scan(
                    acum[:, CH * 3:CH * 4], sp1[:, CH * 3:CH * 4],
                    zeros32[:, :], 0.0, ALU.add, ALU.add))

            ssd_prev = {"v_logr": 0, "v_store": 0, "pe_end": 0}

            def emit_ssd_chunk(t, ct, ccg, v_dt):
                colc = CH * ct
                Sw(sV, v_dt)
                if ssd_prev["v_logr"]:
                    Sw(sV, ssd_prev["v_logr"])
                S_(lambda e, colc=colc: e.activation(
                    gam[:, :], acum[:, colc + CH - 1:colc + CH], AF.Exp))
                S_(lambda e, colc=colc: e.activation(
                    ufm[:, :], acum[:, colc:colc + CH], AF.Exp,
                    scale=c_sc[0:NH, 5:6],
                    bias=acum[:, colc + CH - 1:colc + CH]))
                S_(lambda e, colc=colc: e.activation(
                    lamfm[:, :], acum[:, colc:colc + CH], AF.Exp))
                s_vecs = Si(lambda e, colc=colc: e.activation(
                    LAMfm[:, :], acum[:, colc:colc + CH], AF.Exp,
                    bias=logR[0:NH, :]))
                Vw(sS, s_vecs)
                ssd_prev["v_logr"] = Vi(lambda e, colc=colc: e.tensor_tensor(
                    logR[:, :], logR[:, :],
                    acum[:, colc + CH - 1:colc + CH], ALU.add))
                V_(lambda e: e.drain())
                Tw(sS, s_vecs)
                for srcap, dstT in [
                        (acum[:, colc:colc + CH], acumT),
                        (dtp[:, colc:colc + CH], dtpT),
                        (ufm[:, :], uT), (lamfm[:, :], lamT),
                        (LAMfm[:, :], LAMT)]:
                    pe = Ti(lambda e, srcap=srcap: e.transpose(
                        pTf[0:128, 0:NH], srcap, c_idf[0:NH, 0:NH]))
                    Vw(sPE, pe)
                    vv = Vi(lambda e, dstT=dstT: e.tensor_copy(
                        dstT[:, :], pTf[0:128, 0:NH]))
                    Tw(sV, vv)
                pe = Ti(lambda e: e.transpose(
                    pTf[0:1, 0:NH], gam[:, :], c_idf[0:NH, 0:NH]))
                Vw(sPE, pe)
                vv = Vi(lambda e: e.tensor_copy(grow[:, :], pTf[0:1, 0:NH]))
                Tw(sV, vv)
                T_(lambda e: e.matmul(pTf[0:SG, 256:256 + NH],
                                      c_tri[0:1, 0:SG], grow[:, :],
                                      start=True, stop=True))
                pe2 = Ti(lambda e: e.drain())
                Vw(sPE, pe2)
                vv = Vi(lambda e: e.tensor_copy(gbc[:, :],
                                                pTf[0:SG, 256:256 + NH]))
                Tw(sV, vv)
                V_(lambda e: e.tensor_scalar(nacumT[:, :], acumT[:, :], -1.0,
                                             None, ALU.mult))
                V_(lambda e: e.reciprocal(dpT[:, :], dtpT[:, :]))
                V_(lambda e: e.drain())
                v_tm = Vi(lambda e: e.tensor_tensor(dpT[:, :], dpT[:, :],
                                                    c_dparam[:, :], ALU.mult))
                Tw(sV, v_tm)
                for k in range(NK):
                    pe = Ti(lambda e, k=k, ct=ct: e.transpose(
                        pTb[0:128, 0:128],
                        xbcb[:, TT * k + CH * ct:TT * k + CH * (ct + 1)],
                        c_idb[:, :]))
                    Vw(sPE, pe)
                    for hh in range(2):
                        h = 2 * k + hh
                        vv = Vi(lambda e, h=h, hh=hh: e.tensor_scalar(
                            xtt[:, PH * h:PH * (h + 1)],
                            pTb[0:128, PH * hh:PH * (hh + 1)],
                            dtpT[:, h:h + 1], None, ALU.mult))
                    Tw(sV, vv)
                for bi_, k in enumerate((16, 17, 18, 19)):
                    pe = Ti(lambda e, k=k, ct=ct: e.transpose(
                        pTb[0:128, 0:128],
                        xbcb[:, TT * k + CH * ct:TT * k + CH * (ct + 1)],
                        c_idb[:, :]))
                    Vw(sPE, pe)
                    vv = Vi(lambda e, bi_=bi_: e.tensor_copy(
                        BT[:, 128 * bi_:128 * (bi_ + 1)], pTb[0:128, 0:128]))
                    Tw(sV, vv)
                for g in range(NG):
                    pe = Ti(lambda e, g=g: e.transpose(
                        pTb[0:SG, 0:128], BT[:, SG * g:SG * (g + 1)],
                        c_idb[:, :]))
                    Vw(sPE, pe)
                    vv = Vi(lambda e, g=g: e.tensor_copy(
                        B0[:, 128 * g:128 * (g + 1)], pTb[0:SG, 0:128]))
                    Tw(sV, vv)
                for g in range(NG):
                    pe = Ti(lambda e, g=g: e.transpose(
                        pTb[0:SG, 0:128], BT[:, 256 + SG * g:256 + SG * (g + 1)],
                        c_idb[:, :]))
                    Vw(sPE, pe)
                    vv = Vi(lambda e, g=g: e.tensor_copy(
                        C0[:, 128 * g:128 * (g + 1)], pTb[0:SG, 0:128]))
                    Tw(sV, vv)
                v_sb = Vi(lambda e: e.tensor_copy(Sb[:, :], Sf[:, :]))
                Tw(sV, v_sb)
                cb_gate = [0, 0]
                cb_banks = [pCB, pY]
                for g in range(NG):
                    half = g % 2
                    bank = cb_banks[half]
                    if cb_gate[half]:
                        Tw(sV, cb_gate[half])
                    pe = Ti(lambda e, g=g, bank=bank: e.matmul(
                        bank[0:128, 0:128],
                        B0[:, 128 * g:128 * (g + 1)],
                        C0[:, 128 * g:128 * (g + 1)], start=True, stop=True))
                    Vw(sPE, pe)
                    vv = Vi(lambda e, g=g, bank=bank: e.tensor_tensor(
                        cbm[:, 128 * g:128 * (g + 1)],
                        bank[0:128, 0:128],
                        c_tri[:, :], ALU.mult))
                    cb_gate[half] = vv
                Ebank = [pE, pCB]
                DSbank = [pA, pB]
                pe_e = {}
                pe_y = {}
                pe_yi = {}
                pe_ds = {}
                vclip = {}
                sexp = {}
                vm_m = {}
                vb_m = {}
                y1_m = {}
                y2_m = {}
                st_m = {}
                Tw(sV, cur("sV"))

                def emit_pe_head(h):
                    g = h // 4
                    if h - 1 >= 0:
                        Tw(sV, y1_m[h - 1])
                        Tw(sV, y2_m[h - 1])
                    if h - 2 >= 0:
                        Tw(sV, st_m[h - 2])
                    Tw(sV, vm_m[h])
                    Tw(sV, vb_m[h])
                    pe_y[h] = Ti(lambda e, h=h: e.matmul(
                        pY[0:128, 0:PH], mpb[h % 2][:, :],
                        xtt[:, PH * h:PH * (h + 1)], start=True, stop=True))
                    pe_yi[h] = Ti(lambda e, h=h, g=g: e.matmul(
                        pYi[0:128, 0:PH], C0[:, 128 * g:128 * (g + 1)],
                        Sb[:, PH * h:PH * (h + 1)], start=True, stop=True))
                    pe_ds[h] = Ti(lambda e, h=h: e.matmul(
                        DSbank[h % 2][0:SG, 0:PH], Bhat2[h % 2][:, :],
                        xtt[:, PH * h:PH * (h + 1)], start=True, stop=True))

                def emit_v_evac_head(h):
                    Vw(sPE, pe_y[h])
                    y1_m[h] = Vi(lambda e, h=h: e.scalar_tensor_tensor(
                        Ybuf[:, PH * h:PH * (h + 1)],
                        xtt[:, PH * h:PH * (h + 1)], dpT[:, h:h + 1],
                        pY[0:128, 0:PH], ALU.mult, ALU.add))
                    Vw(sPE, pe_yi[h])
                    y2_m[h] = Vi(lambda e, h=h: e.tensor_scalar(
                        Ybuf2[:, PH * h:PH * (h + 1)], pYi[0:128, 0:PH],
                        lamT[:, h:h + 1], None, ALU.mult))
                    Vw(sPE, pe_ds[h])
                    st_m[h] = Vi(lambda e, h=h: e.scalar_tensor_tensor(
                        Sf[:, PH * h:PH * (h + 1)],
                        Sf[:, PH * h:PH * (h + 1)],
                        gbc[:, h:h + 1], DSbank[h % 2][0:SG, 0:PH],
                        ALU.mult, ALU.add))

                for i in range(NH + 3):
                    if i < NH:
                        h = i
                        if h - 2 >= 0:
                            Tw(sV, vclip[h - 2])
                        pe_e[h] = Ti(lambda e, h=h, colc=colc: e.matmul(
                            Ebank[h % 2][0:128, 0:128],
                            c_oneh[:, 128 * h:128 * (h + 1)],
                            acum[:, colc:colc + CH], start=True, stop=True))
                    if 0 <= i - 1 < NH:
                        h = i - 1
                        Vw(sPE, pe_e[h])
                        vclip[h] = Vi(lambda e, h=h: e.tensor_scalar(
                            esb[h % 2][:, :], Ebank[h % 2][0:128, 0:128],
                            nacumT[:, h:h + 1], 0.0, ALU.add, ALU.min))
                        Sw(sV, vclip[h])
                        sexp[h] = Si(lambda e, h=h: e.activation(
                            esb[h % 2][:, :], esb[h % 2][:, :], AF.Exp))
                    if 0 <= i - 3 < NH:
                        emit_v_evac_head(i - 3)
                    if 0 <= i - 2 < NH:
                        h = i - 2
                        g = h // 4
                        Vw(sS, sexp[h])
                        if h - 2 >= 0:
                            Vw(sPE, pe_y[h - 2])
                        vm_m[h] = Vi(lambda e, h=h, g=g: e.tensor_tensor(
                            mpb[h % 2][:, :], esb[h % 2][:, :],
                            cbm[:, 128 * g:128 * (g + 1)], ALU.mult))
                        if h - 2 >= 0:
                            Vw(sPE, pe_ds[h - 2])
                        vb_m[h] = Vi(lambda e, h=h, g=g: e.tensor_scalar(
                            Bhat2[h % 2][:, :], BT[:, SG * g:SG * (g + 1)],
                            uT[:, h:h + 1], None, ALU.mult))
                        emit_pe_head(h)
                v_done = Vi(lambda e: e.tensor_tensor(
                    Ybuf[:, :], Ybuf[:, :], Ybuf2[:, :], ALU.add))
                Yw(sV, v_done)
                Yw(sYRW, curs(sYRW))
                Y_(lambda e, ccg=ccg: e.dma_start(
                    Y_scr[ccg, :, :], Ybuf[:, :]).then_inc(sYRW, DMA))
                yv = incs(sYRW, DMA)
                Yw(sMSC, curs(sMSC))
                Y_(lambda e, ccg=ccg: e.dma_start(
                    C0_scr[ccg, :, :], C0[:, :]).then_inc(sMSC, DMA))
                incs(sMSC, DMA)
                Yw(sMSC, curs(sMSC))
                Y_(lambda e, ccg=ccg: e.dma_start(
                    LAM_scr[ccg, :, :], LAMT[:, :]).then_inc(sMSC, DMA))
                incs(sMSC, DMA)
                Vw(sYRW, yv)
                Vw(sMSC, curs(sMSC))
                Tw(sMSC, curs(sMSC))
                return v_done

            ccg = 0
            v_dt_prev = 0
            for t in range(NTILE):
                if v_dt_prev:
                    Sw(sV, v_dt_prev)
                s_ip = emit_inproj_tile(t)
                s_conv, v_tail = emit_conv_tile(t, s_ip)
                v_dt = emit_dt_tile(t, s_ip)
                v_dt_prev = v_dt
                Tw(sS, s_conv)
                for ct in range(4):
                    emit_ssd_chunk(t, ct, ccg, v_dt)
                    ccg += 1
                if v_tail:
                    Sw(sV, v_tail)

            v_state = Vi(lambda e: e.drain())
            Yw(sV, v_state)
            Yw(sMSC, curs(sMSC))
            Y_(lambda e: e.dma_start(cc_in[:, :], Sf[:, :]).then_inc(sMSC, DMA))
            incs(sMSC, DMA)
            MSC_ST = curs(sMSC)
            Gw(sMSC, MSC_ST)
            G_(lambda e: e.collective_compute(
                "AllGather", ALU.bypass,
                replica_groups=[[0, 1], [2, 3], [4, 5], [6, 7]],
                ins=[cc_in.ap().opt()], outs=[cc_out.ap().opt()])
                .then_inc(sCC, 1))
            incs(sCC, 1)
            Gw(sCC, 1)
            Gw(sAUXG, curs(sAUXG))
            G_(lambda e: e.dma_start(Sinit[:, :], cc_out[0:SG, :])
               .then_inc(sAUXG, DMA))
            incs(sAUXG, DMA)
            Vw(sCC, 1)
            Vw(sAUXG, curs(sAUXG))
            V_(lambda e: e.tensor_scalar(Sinit[:, :], Sinit[:, :],
                                         c_mask32[:, :], None, ALU.mult))
            V_(lambda e: e.tensor_copy(Sinitb[:, :], Sinit[:, :]))
            barrier()

        with contextlib.ExitStack() as es4:
            c_normw = sbt("c_normw", [128, DSSM], F32, es4)
            c_bout = sbt("c_bout", [128, D], F32, es4)
            zTb = sbt("zTb", [128, D], BF16, es4)
            zT = sbt("zT", [128, D], F32, es4)
            ssq8 = sbt("ssq8", [128, 8], F32, es4)
            rstd8 = sbt("rstd8", [128, 8], F32, es4)
            ynb = sbt("ynb", [128, DSSM], BF16, es4)
            ynT = sbt("ynT", [128, DSSM], BF16, es4)
            woutb = sbt("woutb", [128, NK * D], BF16, es4)
            x2n = sbt("x2n", [128, D], BF16, es4)
            lamld = sbt("lamld", [128, NH], F32, es4)
            c0ld = sbt("c0ld", [SG, NG * 128], BF16, es4)

            for dst, src, np_ in [(c_normw, d_nwrow, 128),
                                  (c_bout, d_borow, 128)]:
                Gw(sAUXG, curs(sAUXG))
                G_(lambda e, dst=dst, src=src, np_=np_: e.dma_start(
                    dst[:, :], src.ap().partition_broadcast(np_))
                    .then_inc(sAUXG, DMA))
                incs(sAUXG, DMA)
            AUX4 = curs(sAUXG)
            for k in range(NK):
                src = d_Wout.ap().rearrange("(c p) o -> p c o", p=128)[:, k, :]
                Gw(sWIN, curs(sWIN))
                G_(lambda e, src=src, k=k: e.dma_start(
                    woutb[:, D * k:D * (k + 1)], src).then_inc(sWIN, DMA))
                incs(sWIN, DMA)
            WOUT_LD = curs(sWIN)
            Vw(sAUXG, AUX4)
            Tw(sWIN, WOUT_LD)

            gates = {"ybuf_v": 0, "zfm_pe": 0, "ld_v": 0, "ld_pe": 0,
                     "xmt_s": 0, "xmt_msc": 0}
            for cg in range(NCHUNK):
                if gates["ybuf_v"]:
                    Yw(sV, gates["ybuf_v"])
                Yw(sYRW, curs(sYRW))
                Y_(lambda e, cg=cg: e.dma_start(
                    Ybuf[:, :], Y_scr[cg, :, :]).then_inc(sYRW, DMA))
                y_ld = incs(sYRW, DMA)
                if gates.get("ztb_s"):
                    Yw(sS, gates["ztb_s"])
                Yw(sZRW, curs(sZRW))
                Y_(lambda e, cg=cg: e.dma_start_transpose(
                    zTb[:, :], z_scr[0:2048, CH * cg:CH * (cg + 1)])
                    .then_inc(sZRW, DMA))
                z_ld = incs(sZRW, DMA)
                if gates["ld_v"]:
                    Yw(sV, gates["ld_v"])
                if gates["ld_pe"]:
                    Yw(sPE, gates["ld_pe"])
                Yw(sMSC, curs(sMSC))
                Y_(lambda e, cg=cg: e.dma_start(
                    lamld[:, :], LAM_scr[cg, :, :]).then_inc(sMSC, DMA))
                incs(sMSC, DMA)
                Yw(sMSC, curs(sMSC))
                Y_(lambda e, cg=cg: e.dma_start(
                    c0ld[:, :], C0_scr[cg, :, :]).then_inc(sMSC, DMA))
                ld_msc = incs(sMSC, DMA)
                if gates["xmt_s"]:
                    Gw(sS, gates["xmt_s"])
                if gates["xmt_msc"]:
                    Gw(sMSC, gates["xmt_msc"])
                Gw(sXIN, curs(sXIN))
                G_(lambda e, cg=cg: e.dma_start(
                    xmt[:, :], d_x[CH * cg:CH * (cg + 1), :])
                    .then_inc(sXIN, DMA))
                x_ld = incs(sXIN, DMA)
                Tw(sMSC, ld_msc)
                Vw(sYRW, y_ld)
                Vw(sMSC, ld_msc)
                corr_vv = {}
                for h in range(NH):
                    g = h // 4
                    cb = pCB if h % 2 == 0 else pE
                    if h >= 2:
                        Tw(sV, corr_vv[h - 2])
                    pe_c = Ti(lambda e, h=h, g=g, cb=cb: e.matmul(
                        cb[0:128, 0:PH], c0ld[:, 128 * g:128 * (g + 1)],
                        Sinitb[:, PH * h:PH * (h + 1)], start=True, stop=True))
                    Vw(sPE, pe_c)
                    corr_vv[h] = Vi(lambda e, h=h, cb=cb: e.scalar_tensor_tensor(
                        Ybuf[:, PH * h:PH * (h + 1)], cb[0:128, 0:PH],
                        lamld[:, h:h + 1], Ybuf[:, PH * h:PH * (h + 1)],
                        ALU.mult, ALU.add))
                Tw(sV, corr_vv[NH - 1])
                Tw(sV, corr_vv[NH - 2])
                gates["ld_v"] = cur("sV")
                gates["ld_pe"] = cur("sPE")
                Sw(sZRW, z_ld)
                s_silu = Si(lambda e: e.activation(zT[:, :], zTb[:, :],
                                                   AF.Silu))
                gates["ztb_s"] = s_silu
                Vw(sS, s_silu)
                v_gate = Vi(lambda e: e.tensor_tensor(
                    Ybuf[:, :], Ybuf[:, :], zT[:, :], ALU.mult))
                Sw(sV, v_gate)
                for g in range(8):
                    S_(lambda e, g=g: e.activation(
                        sqscr[:, 0:256], Ybuf[:, 256 * g:256 * (g + 1)],
                        AF.Square, accum_out=ssq8[:, g:g + 1]))
                S_(lambda e: e.drain())
                S_(lambda e: e.activation(ssq8[:, :], ssq8[:, :], AF.Ln,
                                          scale=c_sc[:, 6:7],
                                          bias=c_sc[:, 0:1]))
                S_(lambda e: e.drain())
                s_gn = Si(lambda e: e.activation(rstd8[:, :], ssq8[:, :],
                                                 AF.Exp, scale=c_sc[:, 4:5]))
                Vw(sS, s_gn)
                for g in range(7):
                    V_(lambda e, g=g: e.scalar_tensor_tensor(
                        ynb[:, 256 * g:256 * (g + 1)],
                        Ybuf[:, 256 * g:256 * (g + 1)], rstd8[:, g:g + 1],
                        c_normw[:, 256 * g:256 * (g + 1)], ALU.mult, ALU.mult))
                v_yn = Vi(lambda e: e.scalar_tensor_tensor(
                    ynb[:, 1792:2048], Ybuf[:, 1792:2048], rstd8[:, 7:8],
                    c_normw[:, 1792:2048], ALU.mult, ALU.mult))
                gates["ybuf_v"] = v_yn
                Tw(sV, v_yn)
                for k in range(NK):
                    pe = Ti(lambda e, k=k: e.transpose(
                        pTb[0:128, 0:128], ynb[:, 128 * k:128 * (k + 1)],
                        c_idb[:, :]))
                    Vw(sPE, pe)
                    vv = Vi(lambda e, k=k: e.tensor_copy(
                        ynT[:, 128 * k:128 * (k + 1)], pTb[0:128, 0:128]))
                    Tw(sV, vv)
                banks4 = [pA, pB, pY, pYi]
                for n in range(4):
                    for k in range(NK):
                        T_(lambda e, n=n, k=k: e.matmul(
                            banks4[n][0:128, 0:512],
                            ynT[:, 128 * k:128 * (k + 1)],
                            woutb[:, D * k + 512 * n:D * k + 512 * (n + 1)],
                            start=(k == 0), stop=(k == NK - 1)))
                pe_o = Ti(lambda e: e.drain())
                Vw(sPE, pe_o)
                Vw(sXIN, x_ld)
                for n in range(4):
                    V_(lambda e, n=n: e.tensor_tensor(
                        xmt[:, 512 * n:512 * (n + 1)],
                        xmt[:, 512 * n:512 * (n + 1)],
                        banks4[n][0:128, 0:512], ALU.add))
                v_xm = Vi(lambda e: e.tensor_tensor(
                    xmt[:, :], xmt[:, :], c_bout[:, :], ALU.add))
                Tw(sV, v_xm)
                Yw(sV, v_xm)
                Yw(sMSC, curs(sMSC))
                Y_(lambda e, cg=cg: e.dma_start(
                    xmid_scr[CH * cg:CH * (cg + 1), :], xmt[:, :])
                    .then_inc(sMSC, DMA))
                incs(sMSC, DMA)
                if stage == "xmid":
                    Yw(sOUT, curs(sOUT))
                    Y_(lambda e, cg=cg: e.dma_start(
                        d_out[CH * cg:CH * (cg + 1), :], xmt[:, :])
                        .then_inc(sOUT, DMA))
                    incs(sOUT, DMA)
                s_n2 = emit_ln(xmt, x2n, 0)
                gates["xmt_s"] = s_n2
                Yw(sS, s_n2)
                Yw(sMSC, curs(sMSC))
                Y_(lambda e, cg=cg: e.dma_start(
                    x2n_scr[CH * cg:CH * (cg + 1), :], x2n[:, :])
                    .then_inc(sMSC, DMA))
                gates["xmt_msc"] = incs(sMSC, DMA)
            barrier()

        if stage == "xmid":
            Yw(sOUT, curs(sOUT))
            _replay(nc, plan)
            return nc

        with contextlib.ExitStack() as esf:
            u16b = [sbt(f"u16b{i}", [128, 512], U16, esf) for i in range(2)]
            i32b = sbt("i32b", [128, 512], I32, esf)
            tnib = sbt("tnib", [128, 512], I32, esf)
            dx1 = sbt("dx1", [128, 512], I32, esf)
            dx2 = sbt("dx2", [128, 512], I32, esf)
            dsg = sbt("dsg", [128, 512], I32, esf)
            dbits = sbt("dbits", [128, 512], U16, esf)
            sclb = [sbt(f"sclb{i}", [128, 512], F16, esf) for i in range(2)]

            dec_gate = [0, 0]

            def emit_decode_c(pi, dst, cslot):
                V_(lambda e, pi=pi: e.tensor_copy(i32b[:, :], u16b[pi][:, :]))
                for j in range(4):
                    V_(lambda e, j=j: e.tensor_scalar(
                        tnib[:, :], i32b[:, :], 4 * j, 15,
                        ALU.logical_shift_right, ALU.bitwise_and))
                    V_(lambda e: e.tensor_scalar(
                        dx1[:, :], tnib[:, :], 7, 6,
                        ALU.bitwise_and, ALU.logical_shift_left))
                    V_(lambda e: e.tensor_scalar(
                        dx2[:, :], tnib[:, :], 14, 7 << 14,
                        ALU.logical_shift_left, ALU.bitwise_and))
                    V_(lambda e: e.tensor_scalar(
                        dsg[:, :], tnib[:, :], 8, 12,
                        ALU.bitwise_and, ALU.logical_shift_left))
                    V_(lambda e: e.scalar_tensor_tensor(
                        dx1[:, :], dx1[:, :], 16384, dx2[:, :],
                        ALU.add, ALU.min))
                    V_(lambda e: e.tensor_tensor(
                        dbits[:, :], dx1[:, :], dsg[:, :], ALU.add))
                    vv = Vi(lambda e, pi=pi, dst=dst, cslot=cslot, j=j:
                            e.tensor_tensor(
                                dst[:, 512 * (4 * cslot + j):
                                    512 * (4 * cslot + j + 1)],
                                dbits[:, :].bitcast(BF16), sclb[pi][:, :],
                                ALU.mult))
                dec_gate[pi] = cur("sV")
                return cur("sV")

            with contextlib.ExitStack() as es5:
                x2nT = sbt("x2nT", [128, NK * T], BF16, es5)
                x2rl = sbt("x2rl", [128, D], BF16, es5)
                decg = [sbt(f"decg{i}", [128, 16 * 512], BF16, es5)
                        for i in range(2)]
                hb = [sbt(f"hb{i}", [128, TT], BF16, es5) for i in range(2)]
                gsil = [sbt(f"gsil{i}", [128, TT], BF16, es5) for i in range(2)]

                x2_gate = 0
                for cg in range(NCHUNK):
                    if x2_gate:
                        Gw(sPE, x2_gate)
                    Gw(sXIN, curs(sXIN))
                    G_(lambda e, cg=cg: e.dma_start(
                        x2rl[:, :], x2n_scr[CH * cg:CH * (cg + 1), :])
                        .then_inc(sXIN, DMA))
                    xv = incs(sXIN, DMA)
                    Tw(sXIN, xv)
                    for cj in range(NK):
                        c4, j4 = cj // 4, cj % 4
                        pe = Ti(lambda e, c4=c4, j4=j4: e.transpose(
                            pTb[0:128, 0:128],
                            x2rl[:, 512 * c4 + j4:512 * (c4 + 1):4],
                            c_idb[:, :]))
                        Vw(sPE, pe)
                        vv = Vi(lambda e, cj=cj, cg=cg: e.tensor_copy(
                            x2nT[:, T * cj + CH * cg:T * cj + CH * (cg + 1)],
                            pTb[0:128, 0:128]))
                        Tw(sV, vv)
                    x2_gate = cur("sPE")

                hb_gate = [0, 0]
                gu_banks = [(pA, pB), (pCB, pE)]
                gu_sgate = [0, 0]
                gu_vgate = [0, 0]
                gu_it = 0
                for s5 in range(11):
                    pe_mm_prev = cur("sPE")
                    for side in range(2):
                        orow = 512 * s5 + II * side
                        for c in range(4):
                            pi = c % 2
                            if dec_gate[pi]:
                                Yw(sV, dec_gate[pi])
                                Gw(sV, dec_gate[pi])
                            Yw(sGUI, curs(sGUI))
                            Y_(lambda e, pi=pi, c=c, orow=orow:
                               e.dma_start_transpose(
                                   u16b[pi][:, :],
                                   d_guu[orow:orow + 512,
                                         128 * c:128 * (c + 1)])
                               .then_inc(sGUI, DMA))
                            uv = incs(sGUI, DMA)
                            for q in range(4):
                                Gw(sGUS, curs(sGUS))
                                G_(lambda e, pi=pi, c=c, q=q, orow=orow:
                                   e.dma_start(
                                       sclb[pi][32 * q:32 * (q + 1), :],
                                       gusq_scr[4 * c + q:4 * c + q + 1,
                                                orow:orow + 512]
                                       .partition_broadcast(32))
                                   .then_inc(sGUS, DMA))
                                incs(sGUS, DMA)
                            Vw(sGUI, uv)
                            Vw(sGUS, curs(sGUS))
                            if c == 0:
                                Vw(sPE, pe_mm_prev)
                            emit_decode_c(pi, decg[side], c)
                    v_dec = Vi(lambda e: e.drain())
                    Tw(sV, v_dec)
                    for j5 in range(4):
                        ki = 4 * s5 + j5
                        for nt in range(4):
                            par = gu_it % 2
                            bg, bu = gu_banks[par]
                            if gu_sgate[par]:
                                Tw(sS, gu_sgate[par])
                            if gu_vgate[par]:
                                Tw(sV, gu_vgate[par])
                            for cj in range(NK):
                                T_(lambda e, cj=cj, j5=j5, nt=nt, bg=bg:
                                   e.matmul(
                                       bg[0:128, 0:TT],
                                       decg[0][:, 512 * cj + j5:
                                               512 * (cj + 1):4],
                                       x2nT[:, T * cj + TT * nt:
                                            T * cj + TT * (nt + 1)],
                                       start=(cj == 0), stop=(cj == NK - 1)))
                            for cj in range(NK - 1):
                                T_(lambda e, cj=cj, j5=j5, nt=nt, bu=bu:
                                   e.matmul(
                                       bu[0:128, 0:TT],
                                       decg[1][:, 512 * cj + j5:
                                               512 * (cj + 1):4],
                                       x2nT[:, T * cj + TT * nt:
                                            T * cj + TT * (nt + 1)],
                                       start=(cj == 0), stop=False))
                            pe_g = Ti(lambda e, j5=j5, nt=nt, bu=bu: e.matmul(
                                bu[0:128, 0:TT],
                                decg[1][:, 512 * (NK - 1) + j5:512 * NK:4],
                                x2nT[:, T * (NK - 1) + TT * nt:
                                     T * (NK - 1) + TT * (nt + 1)],
                                start=False, stop=True))
                            Sw(sPE, pe_g)
                            s_g5 = Si(lambda e, par=par, bg=bg: e.activation(
                                gsil[par][:, :], bg[0:128, 0:TT], AF.Silu))
                            gu_sgate[par] = s_g5
                            hi = gu_it % 2
                            if hb_gate[hi]:
                                Vw(sHIO, hb_gate[hi])
                            Vw(sS, s_g5)
                            vv = Vi(lambda e, hi=hi, par=par, bu=bu:
                                    e.tensor_tensor(
                                        hb[hi][:, :], gsil[par][:, :],
                                        bu[0:128, 0:TT], ALU.mult))
                            gu_vgate[par] = vv
                            Yw(sV, vv)
                            Yw(sHIO, curs(sHIO))
                            Y_(lambda e, ki=ki, nt=nt, hi=hi: e.dma_start(
                                h_scr[ki, :, TT * nt:TT * (nt + 1)],
                                hb[hi][:, :]).then_inc(sHIO, DMA))
                            hb_gate[hi] = incs(sHIO, DMA)
                            gu_it += 1
                barrier()

            with contextlib.ExitStack() as es6:
                decd = sbt("decd", [128, NKI * 512], BF16, es6)
                hall = sbt("hall", [128, 22 * 512], BF16, es6)
                y2f = sbt("y2f", [128, 512], BF16, es6)
                y2T = sbt("y2T", [128, 16 * D], BF16, es6)
                rrms = sbt("rrms", [128, 1], F32, es6)
                outt = sbt("outt", [128, D], F32, es6)

                dbanks = [pCB, pE, pY, pYi]
                for dp in range(4):
                    dbase = 512 * dp
                    pe_prev = cur("sPE")
                    for c6 in range(11):
                        pi = c6 % 2
                        if dec_gate[pi]:
                            Yw(sV, dec_gate[pi])
                            Gw(sV, dec_gate[pi])
                        Yw(sDNI, curs(sDNI))
                        Y_(lambda e, pi=pi, c6=c6, dbase=dbase:
                           e.dma_start_transpose(
                               u16b[pi][:, :],
                               d_dnu[dbase:dbase + 512,
                                     128 * c6:128 * (c6 + 1)])
                           .then_inc(sDNI, DMA))
                        incs(sDNI, DMA)
                        for q in range(4):
                            Gw(sDNS, curs(sDNS))
                            G_(lambda e, pi=pi, c6=c6, q=q, dbase=dbase:
                               e.dma_start(
                                   sclb[pi][32 * q:32 * (q + 1), :],
                                   dnsq_scr[4 * c6 + q:4 * c6 + q + 1,
                                            dbase:dbase + 512]
                                   .partition_broadcast(32))
                               .then_inc(sDNS, DMA))
                            incs(sDNS, DMA)
                        Vw(sDNI, curs(sDNI))
                        Vw(sDNS, curs(sDNS))
                        if c6 == 0:
                            Vw(sPE, pe_prev)
                        emit_decode_c(pi, decd, c6)
                    v_dd = Vi(lambda e: e.drain())
                    Tw(sV, v_dd)
                    pe_h = 0
                    for nt in range(4):
                        for ht in range(2):
                            if pe_h:
                                Yw(sPE, pe_h)
                            h_lds = []
                            for hh in range(2):
                                Yw(sHIO, curs(sHIO))
                                Y_(lambda e, nt=nt, ht=ht, hh=hh: e.dma_start(
                                    hall[:, 512 * 11 * hh:512 * 11 * (hh + 1)]
                                    .rearrange("p (c t) -> p c t", c=11),
                                    h_scr.ap().transpose([1, 0, 2])
                                    [:, 22 * ht + 11 * hh:
                                     22 * ht + 11 * (hh + 1),
                                     TT * nt:TT * (nt + 1)])
                                    .then_inc(sHIO, DMA))
                                h_lds.append(incs(sHIO, DMA))
                            for dc in range(4):
                                for kk in range(22):
                                    kki = 22 * ht + kk
                                    if dc == 0 and kk in (0, 11):
                                        Tw(sHIO, h_lds[kk // 11])
                                    T_(lambda e, kk=kk, kki=kki, dc=dc, ht=ht:
                                       e.matmul(
                                           dbanks[dc][0:128, 0:TT],
                                           decd[:, 512 * kki + 128 * dc:
                                                512 * kki + 128 * (dc + 1)],
                                           hall[:, TT * kk:TT * (kk + 1)],
                                           start=(ht == 0 and kk == 0),
                                           stop=(ht == 1 and kk == 21)))
                            pe_h = Ti(lambda e: e.drain())
                        for dc in range(4):
                            Sw(sPE, cur("sPE"))
                            S_(lambda e, dc=dc: e.activation(
                                y2f[:, :], dbanks[dc][0:128, 0:TT], AF.Identity))
                            s_y2 = Si(lambda e: e.drain())
                            Tw(sS, s_y2)
                            for tc in range(4):
                                pe = Ti(lambda e, tc=tc: e.transpose(
                                    pTb[0:128, 0:128],
                                    y2f[:, 128 * tc:128 * (tc + 1)],
                                    c_idb[:, :]))
                                Vw(sPE, pe)
                                tkc = 4 * nt + tc
                                dcol = dbase + 128 * dc
                                vv = Vi(lambda e, tkc=tkc, dcol=dcol:
                                        e.tensor_copy(
                                            y2T[:, D * tkc + dcol:
                                                D * tkc + dcol + 128],
                                            pTb[0:128, 0:128]))
                                Tw(sV, vv)
                v_p6 = Vi(lambda e: e.drain())

                xm_gate = 0
                for tk in range(16):
                    if xm_gate:
                        Gw(sV, xm_gate)
                    Gw(sXIN, curs(sXIN))
                    G_(lambda e, tk=tk: e.dma_start(
                        xmt[:, :], xmid_scr[CH * tk:CH * (tk + 1), :])
                        .then_inc(sXIN, DMA))
                    xm_ld = incs(sXIN, DMA)
                    Sw(sV, v_p6)
                    for q in range(4):
                        S_(lambda e, tk=tk, q=q: e.activation(
                            sqscr[:, 0:512],
                            y2T[:, D * tk + 512 * q:D * tk + 512 * (q + 1)],
                            AF.Square, accum_out=sqacc[:, q:q + 1]))
                    s_r = Si(lambda e: e.drain())
                    Vw(sS, s_r)
                    V_(lambda e: e.tensor_tensor(sqacc[:, 0:1], sqacc[:, 0:1],
                                                 sqacc[:, 1:2], ALU.add))
                    V_(lambda e: e.tensor_tensor(sqacc[:, 2:3], sqacc[:, 2:3],
                                                 sqacc[:, 3:4], ALU.add))
                    V_(lambda e: e.drain())
                    v_r = Vi(lambda e: e.tensor_tensor(
                        sqacc[:, 0:1], sqacc[:, 0:1], sqacc[:, 2:3], ALU.add))
                    Sw(sV, v_r)
                    S_(lambda e: e.activation(rrms[:, :], sqacc[:, 0:1],
                                              AF.Ln, scale=c_sc[:, 7:8],
                                              bias=c_sc[:, 2:3]))
                    S_(lambda e: e.drain())
                    s_rr = Si(lambda e: e.activation(rrms[:, :], rrms[:, :],
                                                     AF.Exp,
                                                     scale=c_sc[:, 4:5]))
                    Vw(sS, s_rr)
                    Vw(sXIN, xm_ld)
                    for q in range(4):
                        V_(lambda e, tk=tk, q=q: e.scalar_tensor_tensor(
                            outt[:, 512 * q:512 * (q + 1)],
                            y2T[:, D * tk + 512 * q:D * tk + 512 * (q + 1)],
                            rrms[:, :], xmt[:, 512 * q:512 * (q + 1)],
                            ALU.mult, ALU.add))
                    v_o = Vi(lambda e: e.drain())
                    xm_gate = v_o
                    Yw(sV, v_o)
                    Yw(sOUT, curs(sOUT))
                    Y_(lambda e, tk=tk: e.dma_start(
                        d_out[CH * tk:CH * (tk + 1), :], outt[:, :])
                        .then_inc(sOUT, DMA))
                    incs(sOUT, DMA)
                    Vw(sOUT, curs(sOUT))
                Yw(sOUT, curs(sOUT))
        _replay(nc, plan)
    return nc


def _replay(nc, plan):
    with nc.Block() as block:
        @block.sync
        def _(e):
            for f in plan["sync"]:
                f(e)

        @block.gpsimd
        def _(e):
            for f in plan["gpsimd"]:
                f(e)

        @block.vector
        def _(e):
            for f in plan["vector"]:
                f(e)

        @block.scalar
        def _(e):
            for f in plan["scalar"]:
                f(e)

        @block.tensor
        def _(e):
            for f in plan["tensor"]:
                f(e)


def _host_consts():
    ident_b = np.eye(128, dtype=ml_dtypes.bfloat16)
    ident_f = np.eye(128, dtype=np.float32)
    onehot = np.zeros((NH, NH * 128), dtype=np.float32)
    for h in range(NH):
        onehot[h, 128 * h:128 * (h + 1)] = 1.0
    tri = np.triu(np.ones((128, 128), dtype=np.float32))
    return ident_b, ident_f, onehot, tri


def make_in_maps(L0_attn_hidden_states, L0_attn_W_in, L0_attn_b_in, L0_attn_conv_w,
           L0_attn_conv_b, L0_attn_A_log, L0_attn_D_param, L0_attn_dt_bias,
           L0_attn_norm_w, L0_attn_W_out, L0_attn_b_out, L0_attn_mup_vector,
           L0_ffn_gate_up_packed, L0_ffn_gate_up_scales, L0_ffn_down_packed,
           L0_ffn_down_scales):
    x = np.ascontiguousarray(np.asarray(L0_attn_hidden_states, np.float32))
    B, L, _ = x.shape
    W_in = np.ascontiguousarray(np.asarray(L0_attn_W_in, np.float32))
    W_out = np.ascontiguousarray(np.asarray(L0_attn_W_out, np.float32))
    b_in = np.asarray(L0_attn_b_in, np.float32)
    mup = np.asarray(L0_attn_mup_vector, np.float32)
    conv_w = np.asarray(L0_attn_conv_w, np.float32)
    conv_b = np.asarray(L0_attn_conv_b, np.float32)
    A_log = np.asarray(L0_attn_A_log, np.float32)
    D_param = np.asarray(L0_attn_D_param, np.float32)
    dt_bias = np.asarray(L0_attn_dt_bias, np.float32)
    norm_w = np.asarray(L0_attn_norm_w, np.float32)
    b_out = np.asarray(L0_attn_b_out, np.float32)
    gup = np.ascontiguousarray(np.asarray(L0_ffn_gate_up_packed, np.int32))
    gus = np.ascontiguousarray(np.asarray(L0_ffn_gate_up_scales, np.float16))
    dnp = np.ascontiguousarray(np.asarray(L0_ffn_down_packed, np.int32))
    dns = np.ascontiguousarray(np.asarray(L0_ffn_down_scales, np.float16))

    def col_chunks(vec, nchunk):
        out = np.zeros((128, nchunk), np.float32)
        flat = vec.reshape(-1)
        for m in range(nchunk):
            seg = flat[128 * m:128 * (m + 1)]
            out[:len(seg), m] = seg
        return out

    b_in_t = col_chunks(b_in, NM)
    mup_t = col_chunks(mup, NM)
    conv_w_t = np.zeros((128, 80), np.float32)
    conv_b_t = np.zeros((128, 20), np.float32)
    for mi in range(20):
        conv_w_t[:, 4 * mi:4 * mi + 4] = conv_w[128 * mi:128 * (mi + 1), :]
        conv_b_t[:, mi] = conv_b[128 * mi:128 * (mi + 1)]
    ident_b, ident_f, onehot, tri = _host_consts()

    common = dict(
        W_in=W_in, W_out=W_out, b_in_t=b_in_t, mup_t=mup_t,
        conv_w_t=conv_w_t, conv_b_t=conv_b_t,
        A_log_c=np.ascontiguousarray(A_log.reshape(NH, 1)),
        dt_bias_c=np.ascontiguousarray(dt_bias.reshape(NH, 1)),
        D_row=np.ascontiguousarray(D_param.reshape(1, NH)),
        normw_row=np.ascontiguousarray(norm_w.reshape(1, DSSM)),
        bout_row=np.ascontiguousarray(b_out.reshape(1, D)),
        gu_u16=gup.view(np.uint16), gu_scales=gus,
        dn_u16=dnp.view(np.uint16), dn_scales=dns,
        ident_b=ident_b, ident_f=ident_f, onehot=onehot, tri=tri,
        sconst=np.array([[LN_EPS, 0.0, RMS_EPS, 1.0, -0.5, -1.0,
                          1.0 / 256.0, 1.0 / D]], np.float32),
    )
    in_maps = []
    for c in range(8):
        seq, half = c // 2, c % 2
        rows = x[seq, 2048 * half:2048 * (half + 1), :]
        halo = (x[seq, 2045:2048, :] if half == 1
                else np.zeros((3, D), np.float32))
        m = dict(common)
        m["x"] = np.ascontiguousarray(rows)
        m["xh"] = np.ascontiguousarray(halo)
        m["maskp"] = np.array([[float(half)]], np.float32)
        in_maps.append(m)
    return in_maps


def kernel(**inputs):
    x = np.asarray(inputs["L0_attn_hidden_states"], np.float32)
    B, L, _ = x.shape
    in_maps = make_in_maps(**inputs)
    nc = build_program()
    res = run_bass_kernel_spmd(nc, in_maps, core_ids=list(range(8)))
    out = np.empty((B, L, D), np.float32)
    for c in range(8):
        seq, half = c // 2, c % 2
        out[seq, 2048 * half:2048 * (half + 1), :] = \
            np.asarray(res.results[c]["out"], np.float32)
    return out
